# version 18
# baseline (speedup 1.0000x reference)
import math
import numpy as np
import ml_dtypes
from collections import defaultdict

import concourse.bass as bass
import concourse.mybir as mybir
from concourse.bass_utils import run_bass_kernel_spmd

F32 = mybir.dt.float32
BF16 = mybir.dt.bfloat16
AF = mybir.ActivationFunctionType
ALU = mybir.AluOpType
AX = mybir.AxisListType.X
bfdt = ml_dtypes.bfloat16

B, TFULL, C = 8, 4096, 2048
H, N, L = 32, 64, 64
NT = C // 128
FF = 5632
NF = FF // 128
TSA = 256
TSB = 512
GN_EPS = 1e-5 * 64.0
LN_EPS = 1e-5
Q_EPS = 1e-8

COMPUTE = ("tensor", "vector", "scalar", "gpsimd")
DMAQ = {"qsync": "sync", "qscalar": "scalar", "qpool": "gpsimd",
        "qvector": "vector"}
PHYS = ("tensor", "vector", "scalar", "gpsimd", "sync")


class _Op:
    __slots__ = ("stream", "phys", "fn", "reads", "writes", "idx", "waits",
                 "milestone", "drain_before", "tag", "_deps", "stream_pos",
                 "is_dma")

    def __init__(self, stream, phys, fn, reads, writes, idx, tag, is_dma):
        self.stream = stream
        self.phys = phys
        self.fn = fn
        self.reads = reads
        self.writes = writes
        self.idx = idx
        self.waits = {}
        self.milestone = False
        self.drain_before = False
        self.tag = tag
        self.is_dma = is_dma


class Emitter:
    def __init__(self):
        self.ops = []
        self.last_writer = {}
        self.readers = defaultdict(list)
        self.dma_semkeys = []
        self.batch_keys = set()
        self.semkey_queue = {}
        self.fence_keys = ()

    def _record(self, o):
        o.reads = list(o.reads) + list(self.fence_keys)
        self.ops.append(o)
        deps = []
        for k in o.reads:
            w = self.last_writer.get(k)
            if w is not None:
                deps.append(w)
        for k in o.writes:
            w = self.last_writer.get(k)
            if w is not None:
                deps.append(w)
            deps.extend(self.readers.get(k, ()))
        o._deps = [d for d in deps if d is not o]
        for k in o.reads:
            self.readers[k].append(o)
        for k in o.writes:
            self.last_writer[k] = o
            self.readers[k] = []
        return o

    def op(self, engine, fn, reads=(), writes=(), tag=""):
        assert engine in COMPUTE, engine
        return self._record(_Op(engine, engine, fn, list(reads), list(writes),
                                len(self.ops), tag, False))

    def t(self, fn, reads=(), writes=(), tag=""):
        return self.op("tensor", fn, reads, writes, tag)

    def v(self, fn, reads=(), writes=(), tag=""):
        return self.op("vector", fn, reads, writes, tag)

    def s(self, fn, reads=(), writes=(), tag=""):
        return self.op("scalar", fn, reads, writes, tag)

    def g(self, fn, reads=(), writes=(), tag=""):
        return self.op("gpsimd", fn, reads, writes, tag)

    def dma(self, queue, out, in_, reads=(), writes=(), semkey=None, batch=False,
            tag="", **kw):
        assert queue in DMAQ, queue
        if semkey is None:
            semkey = "dma:" + (writes[0] if writes else (reads[0] if reads else tag))
        else:
            semkey = "dma:" + semkey
        if semkey not in self.dma_semkeys:
            self.dma_semkeys.append(semkey)
            self.semkey_queue[semkey] = queue
        assert self.semkey_queue[semkey] == queue, (semkey, queue)
        if batch:
            self.batch_keys.add(semkey)

        def fn(eng):
            return eng.dma_start(out=out, in_=in_, **kw)
        return self._record(_Op(semkey, DMAQ[queue], fn, list(reads), list(writes),
                                len(self.ops), tag, True))

    def fence(self, tiny_ops, key):
        self._fence_marks = getattr(self, "_fence_marks", [])
        marks = []
        for eng, (fn, extra) in tiny_ops.items():
            o = self.op(eng, fn, reads=(), writes=(key + ":" + eng,) + tuple(extra),
                        tag="fence:" + eng)
            o.drain_before = True
            marks.append(o)
        self._fence_marks.append((key, [m.idx for m in marks], len(self.ops)))
        self.fence_keys = tuple(key + ":" + e for e in tiny_ops)

    def resolve(self):
        per_stream = defaultdict(list)
        for o in self.ops:
            per_stream[o.stream].append(o)
            o.stream_pos = len(per_stream[o.stream]) - 1

        for key, mark_idxs, _pt in getattr(self, "_fence_marks", []):
            for mi in mark_idxs:
                mark = self.ops[mi]
                for sk in self.dma_semkeys:
                    last = None
                    for o in per_stream[sk]:
                        if o.idx < mi:
                            last = o
                        else:
                            break
                    if last is not None:
                        mark._deps.append(last)

        know = {p: defaultdict(lambda: -1) for p in PHYS}
        drained_upto = {p: -1 for p in PHYS}
        last_compute = {p: -1 for p in PHYS}
        needed = defaultdict(set)

        for o in self.ops:
            kn = know[o.phys]
            want = defaultdict(lambda: -1)
            drain = o.drain_before
            for d in o._deps:
                if (not d.is_dma) and d.phys == o.phys:
                    if o.phys == "tensor":
                        continue
                    if d.stream_pos > drained_upto[o.phys]:
                        drain = True
                else:
                    if o.is_dma and d.is_dma and d.stream == o.stream:
                        continue
                    if d.stream_pos > kn[d.stream]:
                        want[d.stream] = max(want[d.stream], d.stream_pos)
            if drain:
                o.drain_before = True
                drained_upto[o.phys] = last_compute[o.phys]
            for ps, pos in want.items():
                needed[ps].add(pos)
                o.waits[ps] = pos
                kn[ps] = max(kn[ps], pos)
            if not o.is_dma:
                last_compute[o.phys] = o.stream_pos

        self.milestone_ord = {}
        self.milestone_count = {}
        for stream, lst in per_stream.items():
            if stream.startswith("dma:"):
                for o in lst:
                    o.milestone = True
                    self.milestone_ord[(stream, o.stream_pos)] = o.stream_pos + 1
                self.milestone_count[stream] = len(lst)
            else:
                poss = sorted(needed.get(stream, ()))
                for n, pos in enumerate(poss):
                    lst[pos].milestone = True
                    self.milestone_ord[(stream, pos)] = n + 1
                self.milestone_count[stream] = len(poss)
        for o in self.ops:
            o.waits = {ps: (self.milestone_count[ps] if ps in self.batch_keys
                            else self.milestone_ord[(ps, pos)])
                       for ps, pos in o.waits.items()}
        self.per_stream = per_stream
        self.stats = dict(
            ops=len(self.ops),
            drains=sum(1 for o in self.ops if o.drain_before),
            waits=sum(len(o.waits) for o in self.ops),
            milestones=sum(1 for o in self.ops if o.milestone and not o.is_dma),
            dma_sems=len(self.dma_semkeys))
        return self

    def sem_streams(self):
        return [s for s in COMPUTE if self.milestone_count.get(s, 0) > 0] + \
            list(self.dma_semkeys)

    def make_runner(self, sems):
        by_phys = defaultdict(list)
        for o in self.ops:
            by_phys[o.phys].append(o)

        def run(eng, phys):
            for o in by_phys.get(phys, ()):
                if o.drain_before:
                    eng.drain()
                for ps, ordinal in sorted(o.waits.items()):
                    mult = 16 if ps.startswith("dma:") else 1
                    eng.wait_ge(sems[ps], ordinal * mult)
                inst = o.fn(eng)
                if o.milestone:
                    mult = 16 if o.is_dma else 1
                    if inst is None:
                        raise RuntimeError(f"milestone op '{o.tag}' has no instruction")
                    inst.then_inc(sems[o.stream], mult)
        return run

    def final_dma_waits(self, eng, sems, keys=None):
        ks = (["dma:" + k for k in keys] if keys is not None else list(self.dma_semkeys))
        for k in ks:
            eng.wait_ge(sems[k], self.milestone_count[k] * 16)


def build_program(T, parts="PAB"):
    NSA = T // TSA
    NSB = T // TSB
    NCC = TSA // L

    nc = bass.Bass("TRN2", target_bir_lowering=False)

    def din(name, shape, dt=F32):
        return nc.dram_tensor(name, list(shape), dt, kind="ExternalInput").ap()

    x_in = din("x", [T, C])
    W_attn = {m: din("W" + m, [C, C]) for m in "rkvgo"}
    maa_w1r = din("maa_w1r", [128, NT * 160])
    w2f = [din(f"w2f{f}", [32, C]) for f in range(5)]
    maa_vec = {f: din("maa_" + f, [C]) for f in "xwkvrg"}
    dw1r = din("dw1r", [128, NT * 64])
    dw2 = din("dw2", [64, C])
    tdec = din("tdec", [C])
    u_bc_in = din("u_bc", [128, NT * 64])
    s0_in = din("s0", [128, NT * 64])
    lnw_in = din("lnw_chan", [128, NT])
    maax_chan = din("maax_chan", [128, NT])
    lnb_in = din("lnb_chan", [128, NT])
    wf1 = din("wf1", [FF, C])
    wf2 = din("wf2", [FF, C])
    wf3 = din("wf3", [C, FF])
    tri_ex = din("tri_ex", [128, 64])
    tri_in = din("tri_in", [128, 64])
    tri_suf = din("tri_suf", [128, 64])
    i128 = din("i128", [128, 128], BF16)
    i128f = din("i128f", [128, 128], F32)
    triu2 = din("triu2", [128, 128], BF16)
    tri4 = din("tri4", [128, 256], BF16)
    onesblk = din("onesblk", [128, NT * 32], BF16)
    blockones = din("blockones", [32, C], BF16)

    y_out = nc.dram_tensor("y", [T, C], F32, kind="ExternalOutput").ap()

    scrA = nc.dram_tensor("scrA", [5, NT, 128, C], BF16, kind="Internal").ap()
    scrQ12 = nc.dram_tensor("scrQ12", [2, NF, 128, C], BF16, kind="Internal").ap()
    scrQ3 = nc.dram_tensor("scrQ3", [NT, 128, FF], BF16, kind="Internal").ap()
    x2s = nc.dram_tensor("x2s", [T, C], BF16, kind="Internal").ap()
    x_bf = nc.dram_tensor("x_bf", [T, C], BF16, kind="Internal").ap()
    TQ = T // 4
    scrG = nc.dram_tensor("scrG", [4, NF, 128, TQ], BF16, kind="Internal").ap()

    em = Emitter()
    import contextlib
    stackG = contextlib.ExitStack()
    stackA = contextlib.ExitStack()
    stackB = contextlib.ExitStack()

    def sb(stack, name, shape, dt):
        return stack.enter_context(nc.sbuf_tensor(name, list(shape), dt))

    with stackG:
        c_triex = sb(stackG, "c_triex", [128, 64], F32)
        c_triin = sb(stackG, "c_triin", [128, 64], F32)
        c_trisuf = sb(stackG, "c_trisuf", [128, 64], F32)
        c_i128 = sb(stackG, "c_i128", [128, 128], BF16)
        c_i128f = sb(stackG, "c_i128f", [128, 128], F32)
        c_triu2 = sb(stackG, "c_triu2", [128, 128], BF16)
        c_tri4 = sb(stackG, "c_tri4", [128, 256], BF16)
        c_onesblk = sb(stackG, "c_onesblk", [128, NT * 32], BF16)
        c_blockones = sb(stackG, "c_blockones", [32, C], BF16)
        c_lnw = sb(stackG, "c_lnw", [128, NT], F32)
        c_lnb = sb(stackG, "c_lnb", [128, NT], F32)
        c_lnbt = sb(stackG, "c_lnbt", [128, NT], F32)
        c_lnw_half = sb(stackG, "c_lnw_half", [128, NT], F32)
        c_maa_x = sb(stackG, "c_maa_x", [128, NT], F32)
        S_sb = sb(stackG, "S_sb", [128, NT * 64], F32)
        Sbf = sb(stackG, "Sbf", [128, NT * 64], BF16)
        h_sb = sb(stackG, "h_sb", [128, NT * TSA], BF16)
        hc_sb = sb(stackG, "hc_sb", [128, NT], BF16)
        gacc = sb(stackG, "gacc", [128, 4], F32)
        ginv = sb(stackG, "ginv", [128, 4], F32)
        tiny = sb(stackG, "tiny", [128, 8], F32)
        tinyb = sb(stackG, "tinyb", [128, 8], BF16)
        c_ones1 = sb(stackG, "c_ones1", [128, 1], F32)
        c_onesrow = sb(stackG, "c_onesrow", [1, 128], F32)
        c_bias = sb(stackG, "c_bias", [128, 4], F32)
        stackP = contextlib.ExitStack()
        PW = 512
        prep_in = [sb(stackP, f"prep_in{i}", [128, PW], F32) for i in range(3)]
        prep_bf = [sb(stackP, f"prep_bf{i}", [128, PW], BF16) for i in range(2)]
        prep_out = [sb(stackP, f"prep_out{i}", [128, PW], BF16) for i in range(2)]
        prep_t = [sb(stackP, f"prep_t{i}", [128, PW], F32) for i in range(1)]

        stackPS = contextlib.ExitStack()
        PM = [stackPS.enter_context(nc.psum_tensor(f"PM{i}", [128, 512], F32)) for i in range(2)]
        PC = [stackPS.enter_context(nc.psum_tensor(f"PC{i}", [128, 512], F32)) for i in range(2)]
        PT = [stackPS.enter_context(nc.psum_tensor(f"PT{i}", [128, 1024], BF16)) for i in range(2)]
        PA = stackPS.enter_context(nc.psum_tensor("PA", [128, 512], F32))
        PB = stackPS.enter_context(nc.psum_tensor("PB", [128, 512], F32))

        def ld(dst, src, tag, queue="qsync"):
            em.dma(queue, dst, src, reads=(), writes=(tag,), semkey="init",
                   batch=True, tag="ld_" + tag)

        ld(c_triex[:], tri_ex[:], "c_triex")
        ld(c_triin[:], tri_in[:], "c_triin")
        ld(c_trisuf[:], tri_suf[:], "c_trisuf")
        ld(c_i128[:], i128[:], "c_i128")
        ld(c_i128f[:], i128f[:], "c_i128f")
        ld(c_triu2[:], triu2[:], "c_triu2")
        ld(c_tri4[:], tri4[:], "c_tri4")
        ld(c_onesblk[:], onesblk[:], "c_onesblk")
        ld(c_blockones[:], blockones[:], "c_blockones")
        ld(c_lnw[:], lnw_in[:], "c_lnw")
        ld(c_lnb[:], lnb_in[:], "c_lnb")
        em.dma("qsync", S_sb[:], s0_in[:], reads=(),
               writes=tuple(f"S:{j}" for j in range(NT)), semkey="init",
               batch=True, tag="ld_S")
        em.dma("qsync", c_maa_x[:], maax_chan[:],
               writes=("c_maa_x",), semkey="init", batch=True, tag="ld_maax")

        em.v(lambda e: e.tensor_copy(Sbf[:], S_sb[:]),
             reads=tuple(f"S:{j}" for j in range(NT)),
             writes=tuple(f"Sbf:{j}" for j in range(NT)), tag="Sbf0")
        em.v(lambda e: e.reciprocal(c_lnbt[:], c_lnw[:]), reads=("c_lnw",), writes=("c_lnbt",), tag="rlnw")
        em.v(lambda e: e.tensor_tensor(c_lnbt[:], c_lnbt[:], c_lnb[:], op=ALU.mult),
             reads=("c_lnb", "c_lnbt"), writes=("c_lnbt",), tag="lnbt")
        em.v(lambda e: e.tensor_scalar_mul(c_lnw_half[:], c_lnw[:], 0.5),
             reads=("c_lnw",), writes=("c_lnw_half",), tag="lnwh")
        em.v(lambda e: e.memset(c_ones1[:], 1.0), writes=("c_ones1",), tag="ones1")
        for col, val in [(0, LN_EPS), (1, GN_EPS), (2, 0.5), (3, 1.0)]:
            em.v(lambda e, col=col, val=val: e.memset(c_bias[:, col:col + 1], val),
                 writes=("c_bias",), tag=f"bias{col}")
        em.v(lambda e: e.memset(c_onesrow[:], 1.0), writes=("c_onesrow",), tag="onesr")
        em.v(lambda e: e.memset(hc_sb[:], 0.0), writes=("hc",), tag="hcz")

        MATS = "rkvgo"

        def attn_prepass():
            pc = 0
            for mi, m in enumerate(MATS):
                for ot in range(NT):
                    for piece in range(C // PW):
                        sl = pc % 3
                        pc += 1
                        ikey = f"prepin:{sl}"
                        osl = pc % 2
                        bkey = f"prepbf:{osl}"
                        okey = f"prepout:{osl}"
                        em.dma("qscalar", prep_in[sl][:],
                               W_attn[m][ot * 128:(ot + 1) * 128,
                                         piece * PW:(piece + 1) * PW],
                               reads=(), writes=(ikey,), semkey=f"pl{sl}",
                               tag=f"wp_{m}{ot}_{piece}")
                        em.v(lambda e, sl=sl, osl=osl: e.tensor_copy(
                                prep_bf[osl][:], prep_in[sl][:]),
                             reads=(ikey,), writes=(bkey,), tag=f"wc_{m}{ot}_{piece}")
                        pt = PT[osl]
                        ptk = f"PT{osl}"
                        for j in range(PW // 128):
                            kt = piece * (PW // 128) + j
                            em.t(lambda e, pt=pt, j=j, osl=osl, kt=kt, piece=piece: e.transpose(
                                    pt[:, j * 128:(j + 1) * 128],
                                    prep_bf[osl][:, j * 128:(j + 1) * 128], c_i128[:]),
                                 reads=(bkey, "c_i128"), writes=(ptk,),
                                 tag=f"wt_{m}{ot}_{piece}_{j}")
                        em.s(lambda e, pt=pt, osl=osl: e.copy(prep_out[osl][:], pt[:, 0:PW]),
                             reads=(ptk,), writes=(okey,), tag=f"we_{m}{ot}_{piece}")
                        if m == "o":
                            for j in range(PW // 128):
                                kt = piece * (PW // 128) + j
                                em.v(lambda e, osl=osl, j=j, kt=kt: e.tensor_scalar_mul(
                                        prep_out[osl][:, j * 128:(j + 1) * 128],
                                        prep_out[osl][:, j * 128:(j + 1) * 128],
                                        c_lnw_half[:, kt:kt + 1]),
                                     reads=(okey, "c_lnw_half"), writes=(okey,),
                                     tag=f"wf_{ot}_{piece}_{j}")
                        em.dma("qsync", scrA[mi, ot, :, piece * PW:(piece + 1) * PW],
                               prep_out[osl][:],
                               reads=(okey,), writes=(f"scrA:{m}:{ot}",),
                               semkey=f"ps{osl}", tag=f"ws_{m}{ot}_{piece}")

        attn_prepass()

        ffn_rows = [("wf1", wf1, FF // 128, 0), ("wf2", wf2, FF // 128, 1),
                    ("wf3", wf3, C // 128, 2)]

        def col_chunks(ncols):
            out, c0 = [], 0
            while c0 < ncols:
                w = min(PW, ncols - c0)
                out.append((c0, w))
                c0 += w
            return out

        def gamma_tile(qi, ap, row, c0, w, sl):
            key = f"prepin:{sl}"
            em.dma("qscalar", prep_in[sl][:, 0:w],
                   ap[row * 128:(row + 1) * 128, c0:c0 + w],
                   reads=(), writes=(key,), semkey=f"pl{sl}", tag=f"gl{qi}_{row}_{c0}")
            em.v(lambda e, sl=sl, w=w: e.tensor_reduce(
                    out=tiny[:, 0:1], in_=prep_in[sl][:, 0:w], axis=AX, op=ALU.add,
                    apply_absolute_value=True),
                 reads=(key,), writes=("tiny",), tag=f"gr{qi}_{row}_{c0}")
            em.v(lambda e, qi=qi: e.tensor_tensor(
                    gacc[:, qi:qi + 1], gacc[:, qi:qi + 1], tiny[:, 0:1], op=ALU.add),
                 reads=("tiny", "gacc"), writes=("gacc",), tag=f"ga{qi}_{row}_{c0}")

        em.v(lambda e: e.memset(gacc[:], 0.0), writes=("gacc",), tag="gacc0")
        gamma_jobs = []
        for qi, (nm, ap, nrows, gi) in enumerate(ffn_rows):
            for row in range(nrows):
                for (c0, w) in col_chunks(ap.shape[1]):
                    gamma_jobs.append((qi, ap, row, c0, w))

        _gct = [0]

        def emit_gamma_slice(jobs):
            for (qi, ap, row, c0, w) in jobs:
                gamma_tile(qi, ap, row, c0, w, sl=_gct[0] % 3)
                _gct[0] += 1

        def emit_gamma_final():
            em.t(lambda e: e.matmul(PC[0][0:1, 0:4], c_ones1[:], gacc[:, 0:4],
                                    start=True, stop=True),
                 reads=("gacc", "c_ones1"), writes=("PC0",), tag="gsum")
            em.v(lambda e: e.tensor_copy(tiny[0:1, 0:4], PC[0][0:1, 0:4]),
                 reads=("PC0",), writes=("tiny",), tag="gtot")
            for qi, (nm, ap, nrows, gi) in enumerate(ffn_rows):
                numel = float(ap.shape[0] * ap.shape[1])
                em.v(lambda e, qi=qi, numel=numel: e.tensor_scalar(
                        out=tiny[0:1, qi:qi + 1], in0=tiny[0:1, qi:qi + 1],
                        scalar1=1.0 / numel, scalar2=Q_EPS, op0=ALU.mult, op1=ALU.add),
                     reads=("tiny",), writes=("tiny",), tag=f"gsc{qi}")
            em.v(lambda e: e.reciprocal(tiny[0:1, 0:3], tiny[0:1, 0:3]),
                 reads=("tiny",), writes=("tiny",), tag="ginv0")
            em.t(lambda e: e.matmul(PC[0][:, 0:3], c_onesrow[:], tiny[0:1, 0:3],
                                    start=True, stop=True),
                 reads=("tiny", "c_onesrow"), writes=("PC0",), tag="gbc")
            em.v(lambda e: e.tensor_copy(ginv[:, 0:3], PC[0][:, 0:3]),
                 reads=("PC0",), writes=("ginv",), tag="ginv")

        quant_jobs = []
        for qi, (nm, ap, nrows, gi) in enumerate(ffn_rows):
            for row in range(nrows):
                for (c0, w) in col_chunks(ap.shape[1]):
                    quant_jobs.append((qi, ap, row, c0, w))

        def quant_tile(qi, ap, row, c0, w, sl, tl):
            ikey = f"prepin:{sl}"
            bkey = f"prepbf:{tl}"
            okey = f"prepout:{tl}"
            tkey = "prept:0"
            pt_t = prep_t[0]
            em.dma("qscalar", prep_in[sl][:, 0:w],
                   ap[row * 128:(row + 1) * 128, c0:c0 + w],
                   reads=(), writes=(ikey,), semkey=f"pl{sl}", tag=f"ql{qi}_{row}_{c0}")
            em.v(lambda e, sl=sl, tl=tl, qi=qi, w=w: e.tensor_scalar(
                    out=prep_bf[tl][:, 0:w], in0=prep_in[sl][:, 0:w],
                    scalar1=ginv[:, qi:qi + 1], scalar2=0.5,
                    op0=ALU.mult, op1=ALU.is_ge),
                 reads=(ikey, "ginv"), writes=(bkey,), tag=f"qa{qi}_{row}_{c0}")
            em.v(lambda e, sl=sl, qi=qi, w=w, pt_t=pt_t: e.tensor_scalar(
                    out=pt_t[:, 0:w], in0=prep_in[sl][:, 0:w],
                    scalar1=ginv[:, qi:qi + 1], scalar2=-0.5,
                    op0=ALU.mult, op1=ALU.is_le),
                 reads=(ikey, "ginv"), writes=(tkey,), tag=f"qb{qi}_{row}_{c0}")
            em.v(lambda e, tl=tl, w=w, pt_t=pt_t: e.tensor_tensor(
                    prep_bf[tl][:, 0:w], prep_bf[tl][:, 0:w], pt_t[:, 0:w],
                    op=ALU.subtract),
                 reads=(bkey, tkey), writes=(bkey,), tag=f"qq{qi}_{row}_{c0}")
            nk = w // 128
            pt = PT[tl]
            ptk = f"PT{tl}"
            for j in range(nk):
                em.t(lambda e, pt=pt, j=j, tl=tl: e.transpose(
                        pt[:, j * 128:(j + 1) * 128],
                        prep_bf[tl][:, j * 128:(j + 1) * 128], c_i128[:]),
                     reads=(bkey, "c_i128"), writes=(ptk,), tag=f"qtt{qi}_{row}_{c0}_{j}")
            em.s(lambda e, pt=pt, tl=tl, nk=nk: e.copy(
                    prep_out[tl][:, 0:nk * 128], pt[:, 0:nk * 128]),
                 reads=(ptk,), writes=(okey,), tag=f"qe{qi}_{row}_{c0}")
            if qi < 2:
                em.dma("qsync", scrQ12[qi, row, :, c0:c0 + w], prep_out[tl][:, 0:w],
                       reads=(okey,), writes=(f"scrQ:{qi}:{row}:{c0}",),
                       semkey=f"ps{tl}", tag=f"qst{qi}_{row}_{c0}")
            else:
                em.dma("qsync", scrQ3[row, :, c0:c0 + w], prep_out[tl][:, 0:w],
                       reads=(okey,), writes=(f"scrQ3:{row}:{c0}",),
                       semkey=f"ps{tl}", tag=f"qst3_{row}_{c0}")

        _qdrip = {"i": 0, "on": False}

        def drip_quant(n=1):
            if not _qdrip["on"]:
                return
            i = _qdrip["i"]
            for _ in range(n):
                if i >= len(quant_jobs):
                    break
                qi, ap, row, c0, w = quant_jobs[i]
                quant_tile(qi, ap, row, c0, w, sl=i % 3, tl=i % 2)
                i += 1
            _qdrip["i"] = i

        w1sb = sb(stackA, "w1sb", [128, NT * 160], BF16)
        w2aug = [sb(stackA, f"w2aug{f}", [33, C], BF16) for f in range(5)]
        dw1sb = sb(stackA, "dw1sb", [128, NT * 64], BF16)
        dw2aug = sb(stackA, "dw2aug", [65, C], F32)
        c_ubc = sb(stackA, "c_ubc", [128, NT * 64], BF16)
        em.dma("qpool", dw2aug[0:64, :], dw2[:], writes=("dw2aug:a",),
               semkey="initp", batch=True, tag="ld_dw2a")
        em.dma("qpool", dw2aug[64:65, :], tdec[:].rearrange("(a c) -> a c", a=1),
               writes=("dw2aug:b",), semkey="initp", batch=True, tag="ld_dw2b")
        em.dma("qpool", c_ubc[:], u_bc_in[:], writes=("c_ubc",), semkey="initp",
               batch=True, tag="ld_ubc")
        em.dma("qpool", w1sb[:], maa_w1r[:],
               writes=("w1sb",), semkey="initp", batch=True, tag="ld_w1")
        for f, fname in enumerate("wkvrg"):
            em.dma("qpool", w2aug[f][0:32, :], w2f[f][:],
                   writes=(f"w2aug{f}:a",), semkey="initp", batch=True, tag=f"ld_w2{f}")
            em.dma("qpool", w2aug[f][32:33, :],
                   maa_vec[fname][:].rearrange("(a c) -> a c", a=1),
                   writes=(f"w2aug{f}:b",), semkey="initp", batch=True, tag=f"ld_maa{f}")
        em.dma("qpool", dw1sb[:], dw1r[:],
               writes=("dw1sb",), semkey="initp", batch=True, tag="ld_dw1")
        XCH = T // 8
        for i in range(8):
            em.dma("qpool", x_bf[i * XCH:(i + 1) * XCH, :],
                   x_in[i * XCH:(i + 1) * XCH, :],
                   reads=(), writes=(f"xbf:{i}",), semkey=f"xbf{i}", tag=f"xbf{i}")

        def xbf_keys(r0, r1):
            return tuple(f"xbf:{i}" for i in range(r0 // XCH, (r1 - 1) // XCH + 1))
        xt = [sb(stackA, f"xt{i}", [128, C], BF16) for i in range(2)]
        sqt = sb(stackA, "sqt", [128, C], BF16)
        xx_sb = sb(stackA, "xx_sb", [128, NT * TSA], BF16)
        xf = [sb(stackA, f"xf{i}", [128, NT * TSA], BF16) for i in range(2)]
        r_sb = sb(stackA, "r_sb", [128, NT * TSA], BF16)
        k_sb = sb(stackA, "k_sb", [128, NT * TSA], BF16)
        g2_sb = sb(stackA, "g2_sb", [128, NT * TSA], BF16)
        t1g = sb(stackA, "t1g", [128, TSA], BF16)
        Vtp = sb(stackA, "Vtp", [128, (NCC // 2) * C], BF16)
        ew_tp = sb(stackA, "ewtp", [128, C], F32)
        tanh_f = [sb(stackA, f"tanhf{f}", [33, TSA], BF16) for f in range(5)]
        tanh2 = sb(stackA, "tanh2", [65, TSA], F32)
        t2m = [sb(stackA, f"t2m{i}", [128, TSA], BF16) for i in range(2)]
        E1 = [sb(stackA, f"E1_{i}", [128, NT * 64], BF16) for i in range(2)]
        _e2 = sb(stackA, "E2_0", [128, NT * 64], BF16)
        E2 = [_e2, _e2]
        E3 = sb(stackA, "E3", [128, C], BF16)
        Gam = [sb(stackA, f"Gam{i}", [128, NT], F32) for i in range(2)]
        ktp = sb(stackA, "ktp", [128, C], BF16)
        P2 = sb(stackA, "P2", [128, NT * 64], BF16)
        A_sl = sb(stackA, "A_sl", [128, 512], BF16)
        yt2 = sb(stackA, "yt2", [128, 512], BF16)
        du_sb = sb(stackA, "du_sb", [32, TSA], BF16)
        y_sb = sb(stackA, "y_sb", [128, (TSA // 128) * C], BF16)
        stat = sb(stackA, "stat", [128, 64], F32)
        abT = sb(stackA, "abT", [32, 256], BF16)
        yn_tp = sb(stackA, "yn_tp", [128, C], BF16)
        xz_sb = sb(stackA, "xz_sb", [128, NT * TSA], BF16)
        ws = [sb(stackA, f"ws{i}", [128, C], BF16) for i in range(4)]

        def seg_ln1(s):
            t0 = s * TSA
            h3 = h_sb[:].rearrange("p (n t) -> p n t", t=TSA)
            for tt in range(TSA // 128):
                b = tt % 2
                xk = f"xt:{b}"
                em.dma("qsync", xt[b][:], x_bf[t0 + tt * 128: t0 + (tt + 1) * 128, :],
                       reads=xbf_keys(t0 + tt * 128, t0 + (tt + 1) * 128),
                       writes=(xk,), semkey=f"xin{b}", tag=f"xl{s}_{tt}")
                em.s(lambda e, b=b: e.square(sqt[:], xt[b][:]),
                     reads=(xk,), writes=("sqt",), tag=f"sq{s}_{tt}")
                em.v(lambda e, b=b: e.tensor_reduce(
                        out=stat[:, 0:1], in_=xt[b][:], axis=AX, op=ALU.add),
                     reads=(xk,), writes=("stat",), tag=f"r1{s}_{tt}")
                em.v(lambda e: e.tensor_reduce(
                        out=stat[:, 1:2], in_=sqt[:], axis=AX, op=ALU.add),
                     reads=("sqt",), writes=("stat",), tag=f"r2{s}_{tt}")
                em.v(lambda e: e.tensor_scalar_mul(stat[:, 2:3], stat[:, 0:1], 1.0 / C),
                     reads=("stat",), writes=("stat",), tag=f"mu{s}_{tt}")
                em.v(lambda e: e.tensor_tensor(stat[:, 3:4], stat[:, 2:3], stat[:, 2:3],
                                               op=ALU.mult),
                     reads=("stat",), writes=("stat",), tag=f"mu2{s}_{tt}")
                em.v(lambda e: e.tensor_scalar(
                        out=stat[:, 4:5], in0=stat[:, 1:2], scalar1=1.0 / C,
                        scalar2=stat[:, 3:4], op0=ALU.mult, op1=ALU.subtract),
                     reads=("stat",), writes=("stat",), tag=f"var{s}_{tt}")
                em.s(lambda e: e.activation(stat[:, 5:6], stat[:, 4:5], AF.Ln, bias=c_bias[:, 0:1]),
                     reads=("stat",), writes=("stat",), tag=f"ln{s}_{tt}")
                em.s(lambda e: e.activation(stat[:, 6:7], stat[:, 5:6], AF.Exp, scale=-0.5),
                     reads=("stat",), writes=("stat",), tag=f"ex{s}_{tt}")
                em.v(lambda e, b=b: e.tensor_scalar(
                        out=yn_tp[:], in0=xt[b][:], scalar1=stat[:, 2:3],
                        scalar2=stat[:, 6:7], op0=ALU.subtract, op1=ALU.mult),
                     reads=(xk, "stat"), writes=tuple(f"yn:{q}" for q in range(4)),
                     tag=f"hn{s}_{tt}")
                for half in range(2):
                    pt = PT[half]
                    ptk = f"PT{half}"
                    for j in range(8):
                        jt = half * 8 + j
                        em.t(lambda e, pt=pt, j=j, jt=jt: e.transpose(
                                pt[:, j * 128:(j + 1) * 128],
                                yn_tp[:, jt * 128:(jt + 1) * 128], c_i128[:]),
                             reads=(f"yn:{jt // 4}", "c_i128"), writes=(ptk,),
                             tag=f"ht{s}_{tt}_{jt}")
                    em.s(lambda e, pt=pt, half=half, tt=tt: e.copy(
                            h3[:, half * 8:(half + 1) * 8, tt * 128:(tt + 1) * 128],
                            pt[:].rearrange("p (j n) -> p j n", n=128)),
                         reads=(ptk,), writes=(f"h:{half}",), tag=f"he{s}_{tt}_{half}")
            xx3 = xx_sb[:].rearrange("p (n t) -> p n t", t=TSA)
            xxk = tuple(f"xx:{ct}" for ct in range(NT))
            hks = ("h:0", "h:1")
            em.v(lambda e, xx3=xx3: e.tensor_tensor(
                    xx3[:, :, 1:TSA], h3[:, :, 0:TSA - 1], h3[:, :, 1:TSA],
                    op=ALU.subtract),
                 reads=hks, writes=xxk, tag=f"xx{s}")
            em.v(lambda e, xx3=xx3: e.tensor_tensor(
                    xx3[:, :, 0:1], hc_sb[:].rearrange("p (n o) -> p n o", o=1),
                    h3[:, :, 0:1], op=ALU.subtract),
                 reads=hks + ("hc",), writes=xxk, tag=f"xxc{s}")
            em.v(lambda e: e.tensor_copy(
                    hc_sb[:].rearrange("p (n o) -> p n o", o=1), h3[:, :, TSA - 1:TSA]),
                 reads=hks + xxk, writes=("hc",), tag=f"hcs{s}")
            for jt in range(NT):
                em.v(lambda e, jt=jt: e.scalar_tensor_tensor(
                        out=xz_sb[:, jt * TSA:(jt + 1) * TSA],
                        in0=xx_sb[:, jt * TSA:(jt + 1) * TSA],
                        scalar=c_maa_x[:, jt:jt + 1],
                        in1=h3[:, jt, 0:TSA], op0=ALU.mult, op1=ALU.add),
                     reads=(f"xx:{jt}", f"h:{jt // 8}", "c_maa_x"),
                     writes=(f"xz:{jt}",), tag=f"xxx{s}_{jt}")

        def seg_maa(s):
            banks = [(PM[0], "PM0", 0), (PM[0], "PM0", 256), (PM[1], "PM1", 0),
                     (PM[1], "PM1", 256), (PC[0], "PC0", 0)]
            for f in range(5):
                pb, pbk, co = banks[f]
                for kt in range(NT):
                    em.t(lambda e, pb=pb, co=co, f=f, kt=kt: e.matmul(
                            pb[0:32, co:co + TSA],
                            w1sb[:, kt * 160 + f * 32: kt * 160 + (f + 1) * 32],
                            xz_sb[:, kt * TSA:(kt + 1) * TSA],
                            start=(kt == 0), stop=(kt == NT - 1)),
                         reads=(f"xz:{kt}", "w1sb"), writes=(pbk,), tag=f"maa{s}_{f}_{kt}")
                em.s(lambda e, pb=pb, co=co, f=f: e.activation(
                        tanh_f[f][0:32, :], pb[0:32, co:co + TSA], AF.Tanh),
                     reads=(pbk,), writes=(f"tanhf{f}",), tag=f"tanh{s}_{f}")

        def seg_xf(s, f, dst, dkey):
            h3 = h_sb[:].rearrange("p (n t) -> p n t", t=TSA)
            for ct in range(NT):
                pb, pbk = (PM[ct % 2], f"PM{ct % 2}")
                tm, tmk = (t2m[ct % 2], f"t2m:{ct % 2}")
                em.t(lambda e, pb=pb, f=f, ct=ct: e.matmul(
                        pb[:, 0:TSA],
                        w2aug[f][:, ct * 128:(ct + 1) * 128], tanh_f[f][:],
                        start=True, stop=True),
                     reads=(f"tanhf{f}", f"w2aug{f}:a", f"w2aug{f}:b"), writes=(pbk,), tag=f"mf{s}_{f}_{ct}")
                em.v(lambda e, pb=pb, ct=ct, tm=tm: e.tensor_tensor(
                        tm[:], pb[:, 0:TSA], xx_sb[:, ct * TSA:(ct + 1) * TSA],
                        op=ALU.mult),
                     reads=(pbk, f"xx:{ct}"), writes=(tmk,), tag=f"t2{s}_{f}_{ct}")
                em.v(lambda e, ct=ct, dst=dst, tm=tm: e.tensor_tensor(
                        dst[:, ct * TSA:(ct + 1) * TSA], tm[:],
                        h3[:, ct, 0:TSA], op=ALU.add),
                     reads=(tmk, f"h:{ct // 8}"), writes=(f"{dkey}:{ct}",), tag=f"xf{s}_{f}_{ct}")

        def stream_w(s, m, ot, slot):
            mi = MATS.index(m)
            wk = f"ws:{slot}"
            q = "qscalar" if slot % 2 == 1 else "qsync"
            em.dma(q, ws[slot][:], scrA[mi, ot], reads=(f"scrA:{m}:{ot}",),
                   writes=(wk,), semkey=f"ws{slot}", tag=f"wl{s}_{m}{ot}")
            return wk

        def seg_proj_chan(s, m, src_key, src, dst, dkey, evac):
            for ot in range(NT):
                slot = ot % 4
                wk = stream_w(s, m, ot, slot)
                pb, pbk = (PM[ot % 2], f"PM{ot % 2}")
                for kt in range(NT):
                    em.t(lambda e, pb=pb, slot=slot, kt=kt, src=src: e.matmul(
                            pb[:, 0:TSA], ws[slot][:, kt * 128:(kt + 1) * 128],
                            src[:, kt * TSA:(kt + 1) * TSA],
                            start=(kt == 0), stop=(kt == NT - 1)),
                         reads=(wk, f"{src_key}:{kt}"), writes=(pbk,),
                         tag=f"{m}mm{s}_{ot}_{kt}")
                evac(ot, pb, pbk)
                drip_quant(1)

        def seg_attn_matmuls(s):
            seg_xf(s, 0, xf[0], "xf:0")
            for kt in range(NT):
                em.t(lambda e, kt=kt: e.matmul(
                        PC[1][0:64, 0:TSA], dw1sb[:, kt * 64:(kt + 1) * 64],
                        xf[0][:, kt * TSA:(kt + 1) * TSA],
                        start=(kt == 0), stop=(kt == NT - 1)),
                     reads=(f"xf:0:{kt}", "dw1sb"), writes=("PC1",), tag=f"dw1{s}_{kt}")
            em.s(lambda e: e.activation(tanh2[0:64, :], PC[1][0:64, 0:TSA], AF.Tanh),
                 reads=("PC1",), writes=("tanh2",), tag=f"tanh2{s}")
            seg_xf(s, 1, xf[1], "xf:1")
            def evk(ot, pb, pbk):
                em.v(lambda e, pb=pb, ot=ot: e.tensor_copy(
                        k_sb[:, ot * TSA:(ot + 1) * TSA], pb[:, 0:TSA]),
                     reads=(pbk,), writes=(f"k:{ot}",), tag=f"kev{s}_{ot}")
            seg_proj_chan(s, "k", "xf:1", xf[1], k_sb, "k_sb", evk)
            seg_xf(s, 2, xf[0], "xf:0")
            for ot in range(NT):
                slot = ot % 4
                wk = stream_w(s, "v", ot, slot)
                pb, pbk = (PM[ot % 2], f"PM{ot % 2}")
                for tt in range(TSA // 128):
                    for kt in range(NT):
                        em.t(lambda e, pb=pb, slot=slot, kt=kt, tt=tt: e.matmul(
                                pb[:, tt * 128:(tt + 1) * 128],
                                xf[0][:, kt * TSA + tt * 128: kt * TSA + tt * 128 + 128],
                                ws[slot][:, kt * 128:(kt + 1) * 128],
                                start=(kt == 0), stop=(kt == NT - 1)),
                             reads=(wk, f"xf:0:{kt}"), writes=(pbk,), tag=f"vmm{s}_{tt}_{ot}_{kt}")
                for tt in range(TSA // 128):
                    for hh in range(2):
                        em.v(lambda e, pb=pb, hh=hh, tt=tt, ot=ot: e.tensor_copy(
                                Vtp[hh * 64:(hh + 1) * 64,
                                    tt * C + ot * 128: tt * C + (ot + 1) * 128],
                                pb[hh * 64:(hh + 1) * 64, tt * 128:(tt + 1) * 128]),
                             reads=(pbk,), writes=(f"Vtp:{2 * tt + hh}:{ot}",),
                             tag=f"vev{s}_{tt}_{ot}_{hh}")
                drip_quant(1)
            seg_xf(s, 3, xf[1], "xf:1")
            def evr(ot, pb, pbk):
                em.v(lambda e, pb=pb, ot=ot: e.tensor_copy(
                        r_sb[:, ot * TSA:(ot + 1) * TSA], pb[:, 0:TSA]),
                     reads=(pbk,), writes=(f"r:{ot}",), tag=f"rev{s}_{ot}")
            seg_proj_chan(s, "r", "xf:1", xf[1], r_sb, "r_sb", evr)
            seg_xf(s, 4, xf[0], "xf:0")
            def evg(ot, pb, pbk):
                em.s(lambda e, pb=pb: e.activation(t1g[:], pb[:, 0:TSA], AF.Tanh, scale=0.5),
                     reads=(pbk,), writes=("t1g",), tag=f"gt{s}_{ot}")
                em.s(lambda e: e.activation(t1g[:], t1g[:], AF.Copy, bias=1.0),
                     reads=("t1g",), writes=("t1g",), tag=f"g1{s}_{ot}")
                em.v(lambda e, pb=pb, ot=ot: e.tensor_tensor(
                        g2_sb[:, ot * TSA:(ot + 1) * TSA], pb[:, 0:TSA], t1g[:],
                        op=ALU.mult),
                     reads=(pbk, "t1g"), writes=(f"g2:{ot}",), tag=f"g2{s}_{ot}")
            seg_proj_chan(s, "g", "xf:0", xf[0], g2_sb, "g2_sb", evg)


        def seg_decay_tile(s, tt):
            for q4 in range(4):
                pb, pbk = (PM[q4 % 2], f"PM{q4 % 2}")
                em.t(lambda e, pb=pb, tt=tt, q4=q4: e.matmul(
                        pb[:, 0:512], tanh2[:, tt * 128:(tt + 1) * 128],
                        dw2aug[:, q4 * 512:(q4 + 1) * 512],
                        start=True, stop=True),
                     reads=("tanh2", "dw2aug:a", "dw2aug:b"), writes=(pbk,), tag=f"dw2{s}_{tt}_{q4}")
                em.s(lambda e, pb=pb, q4=q4: e.activation(
                        ew_tp[:, q4 * 512:(q4 + 1) * 512], pb[:, 0:512], AF.Exp),
                     reads=(pbk,), writes=(f"ew:{q4}",), tag=f"ew{s}_{tt}_{q4}")

        def seg_wkv(s):
            for c in range(NCC):
                tt = c // 2
                rb = 64 * (c % 2)
                cp = c % 2
                if rb == 0:
                    seg_decay_tile(s, tt)
                e1k = [f"E1:{cp}:0", f"E1:{cp}:1"]
                e2k = ["E2:0", "E2:1"]
                gmk = f"Gam:{cp}"
                for which, tri, dst, dkeys in [(0, c_triex, E1[cp], e1k),
                                               (1, c_triin, E2[cp], e2k)]:
                    for half in range(2):
                        pc, pck = (PC[half], f"PC{half}")
                        for j in range(8):
                            jt = half * 8 + j
                            em.t(lambda e, pc=pc, j=j, jt=jt, rb=rb, tri=tri: e.matmul(
                                    pc[:, j * 64:(j + 1) * 64],
                                    ew_tp[rb:rb + 64, jt * 128:(jt + 1) * 128],
                                    tri[rb:rb + 64, :], start=True, stop=True),
                                 reads=(f"ew:{jt // 4}", "c_triex", "c_triin"),
                                 writes=(pck,), tag=f"cs{s}_{c}_{which}_{jt}")
                        sc = -1.0 if which == 0 else 1.0
                        em.s(lambda e, pc=pc, dst=dst, half=half, sc=sc: e.activation(
                                dst[:, half * 512:(half + 1) * 512], pc[:], AF.Exp, scale=sc),
                             reads=(pck,), writes=(dkeys[half],), tag=f"Ee{s}_{c}_{which}_{half}")
                        if which == 1:
                            em.s(lambda e, pc=pc, half=half, cp=cp: e.activation(
                                    Gam[cp][:, half * 8:(half + 1) * 8],
                                    pc[:].rearrange("p (j x) -> p j x", x=64)[:, :, 63],
                                    AF.Exp, scale=-1.0),
                                 reads=(pck,), writes=(gmk,), tag=f"Gm{s}_{c}_{half}")
                for q4 in range(4):
                    pc, pck = (PC[q4 % 2], f"PC{q4 % 2}")
                    em.t(lambda e, pc=pc, q4=q4, rb=rb: e.matmul(
                            pc[rb:rb + 64, 0:512], c_trisuf[rb:rb + 64, :],
                            ew_tp[rb:rb + 64, q4 * 512:(q4 + 1) * 512],
                            start=True, stop=True),
                         reads=(f"ew:{q4}", "c_trisuf"), writes=(pck,), tag=f"sf{s}_{c}_{q4}")
                    em.s(lambda e, pc=pc, q4=q4, rb=rb: e.activation(
                            E3[rb:rb + 64, q4 * 512:(q4 + 1) * 512], pc[rb:rb + 64, 0:512],
                            AF.Exp, scale=-1.0),
                         reads=(pck,), writes=(f"E3:{q4}",), tag=f"E3{s}_{c}_{q4}")
                r3 = r_sb[:].rearrange("p (n t) -> p n t", t=TSA)
                k3 = k_sb[:].rearrange("p (n t) -> p n t", t=TSA)
                rks = tuple(f"r:{o}" for o in range(NT)) + tuple(f"k:{o}" for o in range(NT))
                em.v(lambda e, c=c, cp=cp: e.tensor_tensor(
                        E1[cp][:].rearrange("p (n t) -> p n t", t=64),
                        r3[:, :, c * 64:(c + 1) * 64],
                        E1[cp][:].rearrange("p (n t) -> p n t", t=64), op=ALU.mult),
                     reads=rks[:NT] + tuple(e1k), writes=tuple(e1k), tag=f"Qp{s}_{c}")
                em.v(lambda e, c=c, cp=cp: e.tensor_tensor(
                        E2[cp][:].rearrange("p (n t) -> p n t", t=64),
                        k3[:, :, c * 64:(c + 1) * 64],
                        E2[cp][:].rearrange("p (n t) -> p n t", t=64), op=ALU.mult),
                     reads=rks[NT:] + tuple(e2k), writes=tuple(e2k), tag=f"Kp{s}_{c}")
                for half in range(2):
                    pt, ptk = (PT[half], f"PT{half}")
                    for j in range(8):
                        jt = half * 8 + j
                        em.t(lambda e, pt=pt, j=j, jt=jt, c=c, rb=rb: e.transpose(
                                pt[rb:rb + 64, j * 128:(j + 1) * 128],
                                k_sb[:, jt * TSA + c * 64: jt * TSA + c * 64 + 64],
                                c_i128[:]),
                             reads=(f"k:{jt}", "c_i128"), writes=(ptk,), tag=f"kt{s}_{c}_{jt}")
                    em.s(lambda e, pt=pt, half=half, rb=rb: e.copy(
                            ktp[rb:rb + 64, half * 1024:(half + 1) * 1024],
                            pt[rb:rb + 64, :]),
                         reads=(ptk,), writes=(f"ktp:{half}",), tag=f"kte{s}_{c}_{half}")
                em.v(lambda e, rb=rb: e.tensor_tensor(
                        ktp[rb:rb + 64, :], ktp[rb:rb + 64, :], E3[rb:rb + 64, :],
                        op=ALU.mult),
                     reads=("ktp:0", "ktp:1") + tuple(f"E3:{q}" for q in range(4)),
                     writes=("ktp:0", "ktp:1"), tag=f"Kpp{s}_{c}")
                em.v(lambda e, c=c: e.tensor_tensor(
                        P2[:].rearrange("p (n t) -> p n t", t=64),
                        r3[:, :, c * 64:(c + 1) * 64], k3[:, :, c * 64:(c + 1) * 64],
                        op=ALU.mult),
                     reads=rks, writes=("P2",), tag=f"P1{s}_{c}")
                em.v(lambda e: e.tensor_tensor(P2[:], P2[:], c_ubc[:], op=ALU.mult),
                     reads=("P2", "c_ubc"), writes=("P2",), tag=f"P2{s}_{c}")
                for jt in range(NT):
                    em.t(lambda e, jt=jt: e.matmul(
                            PA[0:32, 448:512], c_onesblk[:, jt * 32:(jt + 1) * 32],
                            P2[:, jt * 64:(jt + 1) * 64],
                            start=(jt == 0), stop=(jt == NT - 1)),
                         reads=("P2", "c_onesblk"), writes=("PA",), tag=f"du{s}_{c}_{jt}")
                em.s(lambda e, c=c: e.copy(du_sb[:, c * 64:(c + 1) * 64],
                                           PA[0:32, 448:512]),
                     reads=("PA",), writes=(f"du:{c}",), tag=f"due{s}_{c}")
                vcol = (c // 2) * C
                for g in range(NT // 4):
                    jts = [g * 4 + i for i in range(4)]
                    vks = tuple(f"Vtp:{c}:{j}" for j in jts)
                    for hh, pab, pabk in ((0, PA, "PA"), (1, PB, "PB")):
                        for i, jt in enumerate(jts):
                            em.t(lambda e, jt=jt, hh=hh, rb=rb, pab=pab, i=i, cp=cp: e.matmul(
                                    pab[rb:rb + 64, i * 64:(i + 1) * 64],
                                    E2[cp][hh * 64:(hh + 1) * 64, jt * 64:(jt + 1) * 64],
                                    E1[cp][hh * 64:(hh + 1) * 64, jt * 64:(jt + 1) * 64],
                                    start=True, stop=True),
                                 reads=tuple(e1k) + tuple(e2k), writes=(pabk,),
                                 tag=f"A{s}_{c}_{jt}_{hh}")
                        em.v(lambda e, hh=hh, rb=rb, pab=pab: e.tensor_tensor(
                                A_sl[rb:rb + 64, :].rearrange(
                                    "p (i x) -> p i x", x=128)[:, :, hh * 64:hh * 64 + 64],
                                pab[rb:rb + 64, 0:256].rearrange("p (i x) -> p i x", x=64),
                                c_tri4[rb:rb + 64, :].rearrange("p (i x) -> p i x", x=64),
                                op=ALU.mult),
                             reads=(pabk, "c_tri4"), writes=("Agrp",),
                             tag=f"Am{s}_{c}_{g}_{hh}")
                    for i, jt in enumerate(jts):
                        for hh in range(2):
                            em.t(lambda e, jt=jt, hh=hh, rb=rb, vcol=vcol, i=i: e.matmul(
                                    PC[0][rb:rb + 64, i * 128 + hh * 64: i * 128 + hh * 64 + 64],
                                    A_sl[rb:rb + 64, i * 128 + hh * 64: i * 128 + hh * 64 + 64],
                                    Vtp[rb:rb + 64,
                                        vcol + jt * 128 + hh * 64: vcol + jt * 128 + hh * 64 + 64],
                                    start=True, stop=True),
                                 reads=("Agrp", f"Vtp:{c}:{jt}"), writes=("PC0",),
                                 tag=f"Yi{s}_{c}_{jt}_{hh}")
                        em.t(lambda e, jt=jt, rb=rb, i=i, cp=cp: e.matmul(
                                PC[1][rb:rb + 64, 256 + i * 64: 256 + (i + 1) * 64],
                                E1[cp][0:64, jt * 64:(jt + 1) * 64],
                                Sbf[0:64, jt * 64:(jt + 1) * 64],
                                start=True, stop=True),
                             reads=tuple(e1k) + (f"Sbf:{jt}",), writes=("PC1",),
                             tag=f"Ys{s}_{c}_{jt}_0")
                        em.t(lambda e, jt=jt, rb=rb, i=i, cp=cp: e.matmul(
                                PM[0][rb:rb + 64, 256 + i * 64: 256 + (i + 1) * 64],
                                E1[cp][64:128, jt * 64:(jt + 1) * 64],
                                Sbf[64:128, jt * 64:(jt + 1) * 64],
                                start=True, stop=True),
                             reads=tuple(e1k) + (f"Sbf:{jt}",), writes=("PM0",),
                             tag=f"Ys{s}_{c}_{jt}_1")
                    em.v(lambda e, rb=rb: e.tensor_copy(
                            yt2[rb:rb + 64, :], PC[0][rb:rb + 64, 0:512]),
                         reads=("PC0",), writes=("yt2",), tag=f"Yc{s}_{c}_{g}")
                    yb = y_sb[:].rearrange("p (t x) -> p t x", x=C)
                    em.v(lambda e, rb=rb, tt=tt, g=g, yb=yb: e.tensor_tensor(
                            yb[rb:rb + 64, tt, g * 512:(g + 1) * 512].rearrange(
                                "p (i x) -> p i x", x=128)[:, :, 0:64],
                            yt2[rb:rb + 64, :].rearrange("p (i x) -> p i x", x=128)[:, :, 0:64],
                            PC[1][rb:rb + 64, 256:512].rearrange("p (i x) -> p i x", x=64),
                            op=ALU.add),
                         reads=("yt2", "PC1"), writes=(f"y:{tt}:{g}",), tag=f"Ye0{s}_{c}_{g}")
                    em.v(lambda e, rb=rb, tt=tt, g=g, yb=yb: e.tensor_tensor(
                            yb[rb:rb + 64, tt, g * 512:(g + 1) * 512].rearrange(
                                "p (i x) -> p i x", x=128)[:, :, 64:128],
                            yt2[rb:rb + 64, :].rearrange("p (i x) -> p i x", x=128)[:, :, 64:128],
                            PM[0][rb:rb + 64, 256:512].rearrange("p (i x) -> p i x", x=64),
                            op=ALU.add),
                         reads=("yt2", "PM0"), writes=(f"y:{tt}:{g}",), tag=f"Ye1{s}_{c}_{g}")
                    for i, jt in enumerate(jts):
                        for hh in range(2):
                            em.t(lambda e, jt=jt, hh=hh, rb=rb, vcol=vcol, i=i: e.matmul(
                                    PM[1][hh * 64:(hh + 1) * 64, 256 + i * 64: 256 + (i + 1) * 64],
                                    ktp[rb:rb + 64,
                                        jt * 128 + hh * 64: jt * 128 + hh * 64 + 64],
                                    Vtp[rb:rb + 64,
                                        vcol + jt * 128 + hh * 64: vcol + jt * 128 + hh * 64 + 64],
                                    start=True, stop=True),
                                 reads=("ktp:0", "ktp:1", f"Vtp:{c}:{jt}"),
                                 writes=("PM1",), tag=f"KV{s}_{c}_{jt}_{hh}")
                    for i, jt in enumerate(jts):
                        em.v(lambda e, jt=jt, i=i, cp=cp: e.scalar_tensor_tensor(
                                out=S_sb[:, jt * 64:(jt + 1) * 64],
                                in0=S_sb[:, jt * 64:(jt + 1) * 64],
                                scalar=Gam[cp][:, jt:jt + 1],
                                in1=PM[1][:, 256 + i * 64: 256 + (i + 1) * 64],
                                op0=ALU.mult, op1=ALU.add),
                             reads=("PM1", gmk, f"S:{jt}"), writes=(f"S:{jt}",),
                             tag=f"Su{s}_{c}_{jt}")
                        em.v(lambda e, jt=jt: e.tensor_copy(
                                Sbf[:, jt * 64:(jt + 1) * 64], S_sb[:, jt * 64:(jt + 1) * 64]),
                             reads=(f"S:{jt}",), writes=(f"Sbf:{jt}",), tag=f"Sc{s}_{c}_{jt}")

        def seg_gn_gate(s):
            for tt in range(TSA // 128):
                ytk = tuple(f"y:{tt}:{g}" for g in range(4))
                for q4 in range(4):
                    pb, pbk = (PM[q4 % 2], f"PM{q4 % 2}")
                    em.t(lambda e, pb=pb, tt=tt, q4=q4: e.matmul(
                            pb[:, 0:512], du_sb[:, tt * 128:(tt + 1) * 128],
                            c_blockones[:, q4 * 512:(q4 + 1) * 512],
                            start=True, stop=True),
                         reads=(f"du:{2 * tt}", f"du:{2 * tt + 1}", "c_blockones"),
                         writes=(pbk,), tag=f"db{s}_{tt}_{q4}")
                    vk = tuple(f"Vtp:{2 * tt + hh}:{q4 * 4 + i}"
                               for hh in range(2) for i in range(4))
                    em.v(lambda e, pb=pb, tt=tt, q4=q4: e.tensor_tensor(
                            yt2[:, 0:512], pb[:, 0:512],
                            Vtp[:, tt * C + q4 * 512: tt * C + (q4 + 1) * 512],
                            op=ALU.mult),
                         reads=(pbk, "yt2") + vk, writes=("yt2",), tag=f"dm{s}_{tt}_{q4}")
                    em.v(lambda e, tt=tt, q4=q4: e.tensor_tensor(
                            y_sb[:, tt * C + q4 * 512: tt * C + (q4 + 1) * 512],
                            y_sb[:, tt * C + q4 * 512: tt * C + (q4 + 1) * 512],
                            yt2[:, 0:512], op=ALU.add),
                         reads=("yt2", f"y:{tt}:{q4}"), writes=(f"y:{tt}:{q4}",),
                         tag=f"da{s}_{tt}_{q4}")
                em.s(lambda e, tt=tt: e.square(sqt[:], y_sb[:, tt * C:(tt + 1) * C]),
                     reads=ytk, writes=("sqt",), tag=f"gs{s}_{tt}")
                em.v(lambda e, tt=tt: e.tensor_reduce(
                        out=stat[:, 0:32],
                        in_=y_sb[:, tt * C:(tt + 1) * C].rearrange("p (h n) -> p h n", n=64),
                        axis=AX, op=ALU.add),
                     reads=ytk, writes=("stat",), tag=f"g1{s}_{tt}")
                em.v(lambda e: e.tensor_reduce(
                        out=stat[:, 32:64],
                        in_=sqt[:].rearrange("p (h n) -> p h n", n=64),
                        axis=AX, op=ALU.add),
                     reads=("sqt",), writes=("stat",), tag=f"g2r{s}_{tt}")
                em.v(lambda e: e.tensor_scalar_mul(stat[:, 0:32], stat[:, 0:32], 1.0 / 64),
                     reads=("stat",), writes=("stat",), tag=f"gmu{s}_{tt}")
                em.v(lambda e: e.tensor_scalar_mul(stat[:, 32:64], stat[:, 32:64], 1.0 / 64),
                     reads=("stat",), writes=("stat",), tag=f"gm2{s}_{tt}")
                em.v(lambda e: e.tensor_tensor(sqt[:, 0:32], stat[:, 0:32], stat[:, 0:32],
                                               op=ALU.mult),
                     reads=("stat",), writes=("sqt",), tag=f"gmm{s}_{tt}")
                em.v(lambda e: e.tensor_tensor(stat[:, 32:64], stat[:, 32:64], sqt[:, 0:32],
                                               op=ALU.subtract),
                     reads=("stat", "sqt"), writes=("stat",), tag=f"gv{s}_{tt}")
                em.s(lambda e: e.activation(stat[:, 32:64], stat[:, 32:64], AF.Ln,
                                            bias=c_bias[:, 1:2]),
                     reads=("stat",), writes=("stat",), tag=f"gl{s}_{tt}")
                em.s(lambda e: e.activation(stat[:, 32:64], stat[:, 32:64], AF.Exp,
                                            scale=-0.5),
                     reads=("stat",), writes=("stat",), tag=f"ge{s}_{tt}")
                em.v(lambda e: e.tensor_tensor(stat[:, 0:32], stat[:, 0:32], stat[:, 32:64],
                                               op=ALU.mult),
                     reads=("stat",), writes=("stat",), tag=f"gb{s}_{tt}")
                em.t(lambda e: e.transpose(PC[0][0:32, 0:128], stat[:, 32:64], c_i128f[:]),
                     reads=("stat", "c_i128f"), writes=("PC0",), tag=f"gta{s}_{tt}")
                em.v(lambda e: e.tensor_copy(abT[:, 0:128], PC[0][0:32, 0:128]),
                     reads=("PC0",), writes=("abT",), tag=f"gea{s}_{tt}")
                em.t(lambda e: e.transpose(PC[1][0:32, 0:128], stat[:, 0:32], c_i128f[:]),
                     reads=("stat", "c_i128f"), writes=("PC1",), tag=f"gtb{s}_{tt}")
                em.v(lambda e: e.tensor_copy(abT[:, 128:256], PC[1][0:32, 0:128]),
                     reads=("PC1",), writes=("abT",), tag=f"geb{s}_{tt}")
                for q4 in range(4):
                    em.t(lambda e, q4=q4: e.matmul(
                            PC[0][:, 0:512], abT[:, 0:128],
                            c_blockones[:, q4 * 512:(q4 + 1) * 512], start=True, stop=True),
                         reads=("abT", "c_blockones"), writes=("PC0",), tag=f"ba{s}_{tt}_{q4}")
                    em.t(lambda e, q4=q4: e.matmul(
                            PC[1][:, 0:512], abT[:, 128:256],
                            c_blockones[:, q4 * 512:(q4 + 1) * 512], start=True, stop=True),
                         reads=("abT", "c_blockones"), writes=("PC1",), tag=f"bb{s}_{tt}_{q4}")
                    em.v(lambda e, tt=tt, q4=q4: e.tensor_tensor(
                            sqt[:, q4 * 512:(q4 + 1) * 512],
                            y_sb[:, tt * C + q4 * 512: tt * C + (q4 + 1) * 512],
                            PC[0][:, 0:512], op=ALU.mult),
                         reads=(f"y:{tt}:{q4}", "PC0", "sqt"), writes=("sqt",),
                         tag=f"yn1{s}_{tt}_{q4}")
                    em.v(lambda e, q4=q4: e.tensor_tensor(
                            yn_tp[:, q4 * 512:(q4 + 1) * 512],
                            sqt[:, q4 * 512:(q4 + 1) * 512], PC[1][:, 0:512],
                            op=ALU.subtract),
                         reads=("sqt", "PC1"), writes=(f"yn:{q4}",), tag=f"yn2{s}_{tt}_{q4}")
                for half in range(2):
                    pt, ptk = (PT[half], f"PT{half}")
                    for j in range(8):
                        jt = half * 8 + j
                        em.t(lambda e, pt=pt, j=j, jt=jt: e.transpose(
                                pt[:, j * 128:(j + 1) * 128],
                                yn_tp[:, jt * 128:(jt + 1) * 128], c_i128[:]),
                             reads=(f"yn:{jt // 4}", "c_i128"), writes=(ptk,),
                             tag=f"zt{s}_{tt}_{jt}")
                    em.s(lambda e, pt=pt, half=half, tt=tt: e.copy(
                            xz_sb[:].rearrange("p (n t) -> p n t", t=TSA)[
                                :, half * 8:(half + 1) * 8, tt * 128:(tt + 1) * 128],
                            pt[:].rearrange("p (j n) -> p j n", n=128)),
                         reads=(ptk,),
                         writes=tuple(f"xz:{half * 8 + j}" for j in range(8)),
                         tag=f"ze{s}_{tt}_{half}")
            for jt in range(NT):
                em.v(lambda e, jt=jt: e.scalar_tensor_tensor(
                        out=xz_sb[:, jt * TSA:(jt + 1) * TSA],
                        in0=xz_sb[:, jt * TSA:(jt + 1) * TSA],
                        scalar=c_lnbt[:, jt:jt + 1],
                        in1=g2_sb[:, jt * TSA:(jt + 1) * TSA],
                        op0=ALU.add, op1=ALU.mult),
                     reads=(f"xz:{jt}", f"g2:{jt}", "c_lnbt"), writes=(f"xz:{jt}",),
                     tag=f"gz{s}_{jt}")

        def seg_wo_residual(s):
            t0 = s * TSA
            for tt in range(TSA // 128):
                b = tt % 2
                em.dma("qsync", xt[b][:], x_bf[t0 + tt * 128: t0 + (tt + 1) * 128, :],
                       reads=xbf_keys(t0 + tt * 128, t0 + (tt + 1) * 128),
                       writes=(f"xt:{b}",), semkey=f"xre{b}", tag=f"xr{s}_{tt}")
            for ot in range(NT):
                slot = ot % 4
                wk = stream_w(s, "o", ot, slot)
                pb, pbk = (PM[ot % 2], f"PM{ot % 2}")
                for tt in range(TSA // 128):
                    for kt in range(NT):
                        em.t(lambda e, pb=pb, slot=slot, kt=kt, tt=tt: e.matmul(
                                pb[:, tt * 128:(tt + 1) * 128],
                                xz_sb[:, kt * TSA + tt * 128: kt * TSA + tt * 128 + 128],
                                ws[slot][:, kt * 128:(kt + 1) * 128],
                                start=(kt == 0), stop=(kt == NT - 1)),
                             reads=(wk, f"xz:{kt}"), writes=(pbk,), tag=f"omm{s}_{tt}_{ot}_{kt}")
                for tt in range(TSA // 128):
                    b = tt % 2
                    em.v(lambda e, pb=pb, b=b, ot=ot, tt=tt: e.tensor_tensor(
                            xt[b][:, ot * 128:(ot + 1) * 128],
                            xt[b][:, ot * 128:(ot + 1) * 128],
                            pb[:, tt * 128:(tt + 1) * 128], op=ALU.add),
                         reads=(pbk, f"xt:{b}"), writes=(f"xt:{b}",), tag=f"x2{s}_{tt}_{ot}")
                drip_quant(1)
            for tt in range(TSA // 128):
                b = tt % 2
                em.dma("qsync", x2s[t0 + tt * 128: t0 + (tt + 1) * 128, :], xt[b][:],
                       reads=(f"xt:{b}",), writes=(f"x2s:{(t0 // 128) + tt}",),
                       semkey=f"x2o{b}", tag=f"x2st{s}_{tt}")

        for f in range(5):
            em.v(lambda e, f=f: e.memset(tanh_f[f][32:33, :], 1.0),
                 writes=(f"tanhf{f}",), tag=f"tones{f}")
        em.v(lambda e: e.memset(tanh2[64:65, :], 1.0), writes=("tanh2",), tag="t2ones")

        n_gj = len(gamma_jobs)
        n_qj = len(quant_jobs)
        g_per = (n_gj + max(1, NSA // 3) - 1) // max(1, NSA // 3)
        gamma_done = False
        quant_base = 0
        qseg0 = max(2, NSA // 3 + 1)
        q_segs = max(1, NSA - 1 - qseg0)
        q_per = (n_qj + q_segs - 1) // q_segs

        astage = 9
        for ch in parts:
            if ch.isdigit():
                astage = int(ch)
        for s in (range(NSA) if "A" in parts else []):
            if s < NSA // 3 or (not gamma_done and s == NSA // 3):
                lo = s * g_per
                emit_gamma_slice(gamma_jobs[lo:lo + g_per])
                if lo + g_per >= n_gj and not gamma_done:
                    emit_gamma_final()
                    gamma_done = True
            _qdrip["on"] = gamma_done
            stages = [seg_ln1, seg_maa, seg_attn_matmuls, seg_wkv, seg_gn_gate,
                      seg_wo_residual]
            for i, fn in enumerate(stages):
                if i < astage:
                    fn(s)
        if not gamma_done:
            emit_gamma_final()
            gamma_done = True
        _qdrip["on"] = True
        drip_quant(n_qj)

        em.fence({
            "tensor": (lambda e: e.matmul(PC[0][0:1, 0:1], c_ones1[0:1, 0:1],
                                          c_ones1[0:1, 0:1], start=True, stop=True),
                       ("PC0",)),
            "vector": (lambda e: e.memset(tiny[:, 0:1], 0.0), ("tiny",)),
            "scalar": (lambda e: e.copy(tiny[:, 1:2], c_ones1[:, 0:1]), ("tiny",)),
            "gpsimd": (lambda e: e.memset(tinyb[:, 0:1], 0.0), ("tinyb",)),
        }, "FENCE_AB")

        stackA.close()
        stackP.close()
        NBLK = max(1, T // 512)
        BW = min(512, T)
        BW2 = min(512, TQ)
        NJ = max(1, TQ // BW2)
        NT8 = TQ // 128

        def emit_phase_b():
            stackB0 = contextlib.ExitStack()
            stackB1 = contextlib.ExitStack()
            xn_sb = sb(stackB1, "xn_sb", [128, NT * T], BF16)
            qws = [sb(stackB1, f"qws{i}", [128, C], BF16) for i in range(3)]
            gf = [sb(stackB1, f"gf{i}", [128, T], BF16) for i in range(2)]
            x2t = [sb(stackB0, f"x2t{i}", [128, C], BF16) for i in range(2)]
            statb = sb(stackB0, "statb", [128, 16], F32)
            h2tp = sb(stackB0, "h2tp", [128, C], BF16)

            xn3 = xn_sb[:].rearrange("p (n t) -> p n t", t=T)
            for tt in range(T // 128):
                b = tt % 2
                xk = f"x2t:{b}"
                em.dma("qsync" if b == 0 else "qscalar", x2t[b][:],
                       x2s[tt * 128:(tt + 1) * 128, :],
                       reads=(f"x2s:{tt}",), writes=(xk,), semkey=f"x2b{b}",
                       tag=f"x2l{tt}")
                em.v(lambda e, b=b: e.tensor_reduce(
                        out=statb[:, 0:1], in_=x2t[b][:], axis=AX, op=ALU.add),
                     reads=(xk,), writes=("statb",), tag=f"br1{tt}")
                for cc in range(4):
                    pm, pmk = (PM[cc % 2], f"PM{cc % 2}")
                    em.s(lambda e, b=b, cc=cc, pm=pm: e.square(
                            pm[:, 0:512], x2t[b][:, cc * 512:(cc + 1) * 512]),
                         reads=(xk,), writes=(pmk,), tag=f"bsq{tt}_{cc}")
                    em.v(lambda e, cc=cc, pm=pm: e.tensor_reduce(
                            out=statb[:, 8 + cc:9 + cc], in_=pm[:, 0:512],
                            axis=AX, op=ALU.add),
                         reads=(pmk,), writes=("statb",), tag=f"br2{tt}_{cc}")
                em.v(lambda e: e.tensor_reduce(
                        out=statb[:, 1:2], in_=statb[:, 8:12], axis=AX, op=ALU.add),
                     reads=("statb",), writes=("statb",), tag=f"br2s{tt}")
                em.v(lambda e: e.tensor_scalar_mul(statb[:, 2:3], statb[:, 0:1], 1.0 / C),
                     reads=("statb",), writes=("statb",), tag=f"bmu{tt}")
                em.v(lambda e: e.tensor_tensor(statb[:, 3:4], statb[:, 2:3], statb[:, 2:3],
                                               op=ALU.mult),
                     reads=("statb",), writes=("statb",), tag=f"bm2{tt}")
                em.v(lambda e: e.tensor_scalar(
                        out=statb[:, 4:5], in0=statb[:, 1:2], scalar1=1.0 / C,
                        scalar2=statb[:, 3:4], op0=ALU.mult, op1=ALU.subtract),
                     reads=("statb",), writes=("statb",), tag=f"bv{tt}")
                em.s(lambda e: e.activation(statb[:, 5:6], statb[:, 4:5], AF.Ln,
                                            bias=c_bias[:, 0:1]),
                     reads=("statb",), writes=("statb",), tag=f"bl{tt}")
                em.s(lambda e: e.activation(statb[:, 6:7], statb[:, 5:6], AF.Exp, scale=-0.5),
                     reads=("statb",), writes=("statb",), tag=f"be{tt}")
                em.v(lambda e, b=b: e.tensor_scalar(
                        out=h2tp[:], in0=x2t[b][:], scalar1=statb[:, 2:3],
                        scalar2=statb[:, 6:7], op0=ALU.subtract, op1=ALU.mult),
                     reads=(xk, "statb"), writes=("h2tp",), tag=f"bh{tt}")
                for half in range(2):
                    pt, ptk = (PT[half], f"PT{half}")
                    for j in range(8):
                        jt = half * 8 + j
                        em.t(lambda e, pt=pt, j=j, jt=jt: e.transpose(
                                pt[:, j * 128:(j + 1) * 128],
                                h2tp[:, jt * 128:(jt + 1) * 128], c_i128[:]),
                             reads=("h2tp", "c_i128"), writes=(ptk,), tag=f"bt{tt}_{jt}")
                    em.s(lambda e, pt=pt, half=half, tt=tt: e.copy(
                            xn3[:, half * 8:(half + 1) * 8, tt * 128:(tt + 1) * 128],
                            pt[:].rearrange("p (j n) -> p j n", n=128)),
                         reads=(ptk,), writes=(f"xn:{half}",), tag=f"bte{tt}_{half}")

            def q12load(qi, f, sl):
                em.dma("qscalar" if sl == 1 else "qsync", qws[sl][:], scrQ12[qi, f],
                       reads=tuple(f"scrQ:{qi}:{f}:{c0}" for (c0, w) in col_chunks(C)),
                       writes=(f"qws:{sl}",), semkey=f"qw{sl}", tag=f"q{qi}l_{f}")

            q12load(0, 0, 0)
            q12load(1, 0, 2)

            em.fence({
                "tensor": (lambda e: e.matmul(PC[0][0:1, 0:1], c_ones1[0:1, 0:1],
                                              c_ones1[0:1, 0:1], start=True, stop=True),
                           ("PC0",)),
                "vector": (lambda e: e.memset(tiny[:, 0:1], 0.0), ("tiny",)),
                "scalar": (lambda e: e.copy(tiny[:, 1:2], c_ones1[:, 0:1]), ("tiny",)),
                "gpsimd": (lambda e: e.memset(tinyb[:, 0:1], 0.0), ("tinyb",)),
            }, "FENCE_B1")
            stackB0.close()
            stackPS.close()
            PB8 = [stackB1.enter_context(nc.psum_tensor(f"PB8_{i}", [128, BW], F32))
                   for i in range(NBLK)]

            NHALF = (NBLK + 1) // 2
            halves = [list(range(0, NHALF)), list(range(NHALF, NBLK))]
            for f in range(NF):
                s1 = f % 2
                gsl = f % 2
                if f > 0:
                    q12load(0, f, s1)
                    q12load(1, f, 2)
                for qi, sl in ((0, s1), (1, 2)):
                    for hblk in halves:
                        if not hblk:
                            continue
                        for kt in range(NT):
                            for blk in hblk:
                                em.t(lambda e, sl=sl, kt=kt, blk=blk: e.matmul(
                                        PB8[blk][:, 0:BW],
                                        qws[sl][:, kt * 128:(kt + 1) * 128],
                                        xn_sb[:, kt * T + blk * BW: kt * T + (blk + 1) * BW],
                                        start=(kt == 0), stop=(kt == NT - 1)),
                                     reads=(f"qws:{sl}", f"xn:{kt // 8}"),
                                     writes=(f"PB8_{blk}",), tag=f"b1m{qi}_{f}_{kt}_{blk}")
                        if qi == 0:
                            for blk in hblk:
                                em.s(lambda e, gsl=gsl, blk=blk: e.activation(
                                        gf[gsl][:, blk * BW:(blk + 1) * BW],
                                        PB8[blk][:, 0:BW], AF.Tanh, scale=0.5),
                                     reads=(f"PB8_{blk}",), writes=(f"gf:{gsl}",),
                                     tag=f"b1t{f}_{blk}")
                                em.s(lambda e, gsl=gsl, blk=blk: e.activation(
                                        gf[gsl][:, blk * BW:(blk + 1) * BW],
                                        gf[gsl][:, blk * BW:(blk + 1) * BW],
                                        AF.Copy, scale=0.5, bias=0.5),
                                     reads=(f"gf:{gsl}",), writes=(f"gf:{gsl}",),
                                     tag=f"b1c{f}_{blk}")
                        else:
                            for blk in hblk:
                                em.v(lambda e, gsl=gsl, blk=blk: e.tensor_tensor(
                                        gf[gsl][:, blk * BW:(blk + 1) * BW],
                                        gf[gsl][:, blk * BW:(blk + 1) * BW],
                                        PB8[blk][:, 0:BW], op=ALU.mult),
                                     reads=(f"PB8_{blk}", f"gf:{gsl}"),
                                     writes=(f"gf:{gsl}",), tag=f"b1g{f}_{blk}")
                for qh in range(4):
                    em.dma("qpool", scrG[qh, f], gf[gsl][:, qh * TQ:(qh + 1) * TQ],
                           reads=(f"gf:{gsl}",), writes=(f"scrG:{f}:{qh}",),
                           semkey=f"gfo{gsl}", tag=f"gfo{f}_{qh}")

            em.fence({
                "tensor": (lambda e: e.matmul(PB8[0][0:1, 0:1], c_ones1[0:1, 0:1],
                                              c_ones1[0:1, 0:1], start=True, stop=True),
                           ("PB8_0",)),
                "vector": (lambda e: e.memset(tiny[:, 0:1], 0.0), ("tiny",)),
                "scalar": (lambda e: e.copy(tiny[:, 1:2], c_ones1[:, 0:1]), ("tiny",)),
                "gpsimd": (lambda e: e.memset(tinyb[:, 0:1], 0.0), ("tinyb",)),
            }, "FENCE_B2")
            stackB1.close()

            gh = sb(stackB, "gh", [128, NF * TQ], BF16)
            outq = sb(stackB, "outq", [128, NT * TQ], BF16)
            q3ws = [sb(stackB, f"q3ws{i}", [128, FF], BF16) for i in range(2)]
            x2u = [sb(stackB, f"x2u{i}", [128, C], BF16) for i in range(2)]
            yout = [sb(stackB, f"yout{i}", [128, C], F32) for i in range(2)]
            PSB2 = [stackB.enter_context(nc.psum_tensor(f"PS2_{i}", [128, BW2], F32))
                    for i in range(2 * NJ)]
            PTB = [stackB.enter_context(nc.psum_tensor(f"PTB{i}", [128, 1024], BF16))
                   for i in range(2)]

            for qh in range(4):
                for f in range(NF):
                    em.dma("qsync", gh[:, f * TQ:(f + 1) * TQ], scrG[qh, f],
                           reads=(), writes=("ghq",), semkey="ghl",
                           tag=f"ghl{qh}_{f}")
                for ot in range(NT):
                    sl = ot % 2
                    em.dma("qscalar", q3ws[sl][:], scrQ3[ot],
                           reads=tuple(f"scrQ3:{ot}:{c0}" for (c0, w) in col_chunks(FF)),
                           writes=(f"q3ws:{sl}",), semkey=f"q3w{sl}",
                           tag=f"q3l{qh}_{ot}")
                    bs = 0 if ot % 2 == 0 else NJ
                    for kt in range(NF):
                        for j in range(NJ):
                            em.t(lambda e, sl=sl, kt=kt, j=j, bs=bs: e.matmul(
                                    PSB2[bs + j][:, 0:BW2],
                                    q3ws[sl][:, kt * 128:(kt + 1) * 128],
                                    gh[:, kt * TQ + j * BW2: kt * TQ + (j + 1) * BW2],
                                    start=(kt == 0), stop=(kt == NF - 1)),
                                 reads=(f"q3ws:{sl}", "ghq"),
                                 writes=(f"PS2_{bs + j}",), tag=f"b2m{qh}_{ot}_{kt}_{j}")
                    for j in range(NJ):
                        em.v(lambda e, ot=ot, j=j, bs=bs: e.tensor_copy(
                                outq[:, ot * TQ + j * BW2: ot * TQ + (j + 1) * BW2],
                                PSB2[bs + j][:, 0:BW2]),
                             reads=(f"PS2_{bs + j}",), writes=(f"outq:{ot}",),
                             tag=f"b2e{qh}_{ot}_{j}")
                for t8 in range(NT8):
                    gt = qh * NT8 + t8
                    b = t8 % 2
                    em.dma("qsync" if b == 0 else "qscalar", x2u[b][:],
                           x2s[gt * 128:(gt + 1) * 128, :],
                           reads=(f"x2s:{gt}",), writes=(f"x2u:{b}",),
                           semkey=f"x2u{b}", tag=f"x2u{qh}_{t8}")
                    for half in range(2):
                        ptb, ptk = (PTB[half], f"PTB{half}")
                        for j in range(8):
                            ot = half * 8 + j
                            em.t(lambda e, ptb=ptb, j=j, ot=ot, t8=t8: e.transpose(
                                    ptb[:, j * 128:(j + 1) * 128],
                                    outq[:, ot * TQ + t8 * 128: ot * TQ + t8 * 128 + 128],
                                    c_i128[:]),
                                 reads=(f"outq:{ot}", "c_i128"), writes=(ptk,),
                                 tag=f"b2t{qh}_{t8}_{ot}")
                        em.v(lambda e, ptb=ptb, b=b, half=half: e.tensor_tensor(
                                yout[b][:, half * 1024:(half + 1) * 1024],
                                x2u[b][:, half * 1024:(half + 1) * 1024],
                                ptb[:, 0:1024], op=ALU.add),
                             reads=(ptk, f"x2u:{b}"), writes=(f"yout:{b}",),
                             tag=f"b2y{qh}_{t8}_{half}")
                    em.dma("qpool", y_out[gt * 128:(gt + 1) * 128, :], yout[b][:],
                           reads=(f"yout:{b}",), writes=(), semkey=f"yo{b}",
                           tag=f"yst{qh}_{t8}")

        with stackB:
            if "B" in parts:
                emit_phase_b()

            em.resolve()
            sems = {}
            for st_ in em.sem_streams():
                sems[st_] = stackG.enter_context(
                    nc.semaphore("sem_" + st_.replace(":", "_")))
            run = em.make_runner(sems)
            with nc.Block() as block:
                @block.tensor
                def _(e):
                    run(e, "tensor")

                @block.vector
                def _(e):
                    run(e, "vector")

                @block.scalar
                def _(e):
                    run(e, "scalar")

                @block.gpsimd
                def _(e):
                    run(e, "gpsimd")
                    if "B" in parts:
                        em.final_dma_waits(e, sems, [k for k in ("yo0", "yo1")
                                                     if "dma:" + k in em.dma_semkeys])
                    else:
                        em.final_dma_waits(e, sems, None)

                @block.sync
                def _(e):
                    run(e, "sync")
    return nc, em


def _consts():
    ii, jj = np.meshgrid(np.arange(64), np.arange(64), indexing="ij")
    tri_ex = np.vstack([(ii < jj)] * 2).astype(np.float32)
    tri_in = np.vstack([(ii <= jj)] * 2).astype(np.float32)
    tri_suf = np.vstack([(ii > jj)] * 2).astype(np.float32)
    i128 = np.eye(128, dtype=bfdt)
    i128f = np.eye(128, dtype=np.float32)
    triu2 = np.vstack([np.hstack([(ii < jj)] * 2)] * 2).astype(bfdt)
    tri4 = np.vstack([np.hstack([(ii < jj)] * 4)] * 2).astype(bfdt)
    onesblk = np.zeros((128, NT * 32), dtype=bfdt)
    for jt in range(NT):
        onesblk[0:64, jt * 32 + 2 * jt] = 1
        onesblk[64:128, jt * 32 + 2 * jt + 1] = 1
    blockones = np.zeros((32, C), dtype=bfdt)
    for h in range(32):
        blockones[h, h * 64:(h + 1) * 64] = 1
    return dict(tri_ex=tri_ex, tri_in=tri_in, tri_suf=tri_suf, i128=i128,
                i128f=i128f, triu2=triu2, tri4=tri4, onesblk=onesblk, blockones=blockones)


def make_inmap(xb, P, T):
    cs = _consts()
    u_flat = np.ascontiguousarray(P["time_faaaa"], dtype=np.float32).reshape(C)
    u_bc = np.repeat(u_flat.reshape(NT, 128).T[:, :, None], 64, axis=2).reshape(128, NT * 64)
    s0 = np.zeros((128, NT * 64), dtype=np.float32)
    ts = np.asarray(P["time_state"], dtype=np.float32)
    for jt in range(NT):
        s0[0:64, jt * 64:(jt + 1) * 64] = ts[2 * jt]
        s0[64:128, jt * 64:(jt + 1) * 64] = ts[2 * jt + 1]
    d = {
        "x": np.ascontiguousarray(xb, dtype=np.float32),
        "maa_w1r": np.ascontiguousarray(
            np.asarray(P["time_maa_w1"], dtype=np.float32).reshape(NT, 128, 160)
            .transpose(1, 0, 2).reshape(128, NT * 160)),
        "dw1r": np.ascontiguousarray(
            np.asarray(P["time_decay_w1"], dtype=np.float32).reshape(NT, 128, 64)
            .transpose(1, 0, 2).reshape(128, NT * 64)),
        "dw2": np.asarray(P["time_decay_w2"], dtype=np.float32),
        "tdec": np.asarray(P["time_decay"], dtype=np.float32),
        "u_bc": np.ascontiguousarray(u_bc, dtype=np.float32),
        "s0": s0,
        "lnw_chan": np.ascontiguousarray(
            np.asarray(P["ln_w"], dtype=np.float32).reshape(NT, 128).T),
        "maax_chan": np.ascontiguousarray(
            np.asarray(P["time_maa_x"], dtype=np.float32).reshape(NT, 128).T),
        "lnb_chan": np.ascontiguousarray(
            np.asarray(P["ln_b"], dtype=np.float32).reshape(NT, 128).T),
        "wf1": np.asarray(P["ffn_w1"], dtype=np.float32),
        "wf2": np.asarray(P["ffn_w2"], dtype=np.float32),
        "wf3": np.asarray(P["ffn_w3"], dtype=np.float32),
    }
    for m in "rkvgo":
        d["W" + m] = np.asarray(P["W_" + m], dtype=np.float32)
    w2 = np.asarray(P["time_maa_w2"], dtype=np.float32)
    for f in range(5):
        d[f"w2f{f}"] = np.ascontiguousarray(w2[f])
    for f, nm in zip("xwkvrg", ["time_maa_x", "time_maa_w", "time_maa_k",
                                "time_maa_v", "time_maa_r", "time_maa_g"]):
        d["maa_" + f] = np.asarray(P[nm], dtype=np.float32)
    d.update(cs)
    return d


_NC_CACHE = {}


def kernel(L0_attn_x, L0_attn_time_maa_x, L0_attn_time_maa_w, L0_attn_time_maa_k,
           L0_attn_time_maa_v, L0_attn_time_maa_r, L0_attn_time_maa_g,
           L0_attn_time_maa_w1, L0_attn_time_maa_w2, L0_attn_time_decay,
           L0_attn_time_decay_w1, L0_attn_time_decay_w2, L0_attn_time_faaaa,
           L0_attn_time_state, L0_attn_W_r, L0_attn_W_k, L0_attn_W_v,
           L0_attn_W_g, L0_attn_W_o, L0_attn_ln_w, L0_attn_ln_b,
           L0_ffn_w1, L0_ffn_w2, L0_ffn_w3):
    P = dict(
        time_maa_x=L0_attn_time_maa_x, time_maa_w=L0_attn_time_maa_w,
        time_maa_k=L0_attn_time_maa_k, time_maa_v=L0_attn_time_maa_v,
        time_maa_r=L0_attn_time_maa_r, time_maa_g=L0_attn_time_maa_g,
        time_maa_w1=L0_attn_time_maa_w1, time_maa_w2=L0_attn_time_maa_w2,
        time_decay=L0_attn_time_decay, time_decay_w1=L0_attn_time_decay_w1,
        time_decay_w2=L0_attn_time_decay_w2, time_faaaa=L0_attn_time_faaaa,
        time_state=L0_attn_time_state, W_r=L0_attn_W_r, W_k=L0_attn_W_k,
        W_v=L0_attn_W_v, W_g=L0_attn_W_g, W_o=L0_attn_W_o,
        ln_w=L0_attn_ln_w, ln_b=L0_attn_ln_b,
        ffn_w1=L0_ffn_w1, ffn_w2=L0_ffn_w2, ffn_w3=L0_ffn_w3)
    x = np.asarray(L0_attn_x, dtype=np.float32)
    T = x.shape[1]
    if T not in _NC_CACHE:
        _NC_CACHE[T] = build_program(T)
    nc, em = _NC_CACHE[T]
    in_maps = [make_inmap(x[i], P, T) for i in range(x.shape[0])]
    res = run_bass_kernel_spmd(nc, in_maps, core_ids=list(range(x.shape[0])))
    out = np.stack([np.asarray(r["y"], dtype=np.float32) for r in res.results], axis=0)
    return out


# version 19
# speedup vs baseline: 1.0786x; 1.0786x over previous
import math
import numpy as np
import ml_dtypes
from collections import defaultdict

import concourse.bass as bass
import concourse.mybir as mybir
from concourse.bass_utils import run_bass_kernel_spmd

F32 = mybir.dt.float32
BF16 = mybir.dt.bfloat16
AF = mybir.ActivationFunctionType
ALU = mybir.AluOpType
AX = mybir.AxisListType.X
bfdt = ml_dtypes.bfloat16

B, TFULL, C = 8, 4096, 2048
H, N, L = 32, 64, 64
NT = C // 128
FF = 5632
NF = FF // 128
TSA = 256
TSB = 512
GN_EPS = 1e-5 * 64.0
LN_EPS = 1e-5
Q_EPS = 1e-8

COMPUTE = ("tensor", "vector", "scalar", "gpsimd")
DMAQ = {"qsync": "sync", "qscalar": "scalar", "qpool": "gpsimd",
        "qvector": "vector"}
PHYS = ("tensor", "vector", "scalar", "gpsimd", "sync")


class _Op:
    __slots__ = ("stream", "phys", "fn", "reads", "writes", "idx", "waits",
                 "milestone", "drain_before", "tag", "_deps", "stream_pos",
                 "is_dma")

    def __init__(self, stream, phys, fn, reads, writes, idx, tag, is_dma):
        self.stream = stream
        self.phys = phys
        self.fn = fn
        self.reads = reads
        self.writes = writes
        self.idx = idx
        self.waits = {}
        self.milestone = False
        self.drain_before = False
        self.tag = tag
        self.is_dma = is_dma


class Emitter:
    def __init__(self):
        self.ops = []
        self.last_writer = {}
        self.readers = defaultdict(list)
        self.dma_semkeys = []
        self.batch_keys = set()
        self.semkey_queue = {}
        self.fence_keys = ()

    def _record(self, o):
        o.reads = list(o.reads) + list(self.fence_keys)
        self.ops.append(o)
        deps = []
        for k in o.reads:
            w = self.last_writer.get(k)
            if w is not None:
                deps.append(w)
        for k in o.writes:
            w = self.last_writer.get(k)
            if w is not None:
                deps.append(w)
            deps.extend(self.readers.get(k, ()))
        o._deps = [d for d in deps if d is not o]
        for k in o.reads:
            self.readers[k].append(o)
        for k in o.writes:
            self.last_writer[k] = o
            self.readers[k] = []
        return o

    def op(self, engine, fn, reads=(), writes=(), tag=""):
        assert engine in COMPUTE, engine
        return self._record(_Op(engine, engine, fn, list(reads), list(writes),
                                len(self.ops), tag, False))

    def t(self, fn, reads=(), writes=(), tag=""):
        return self.op("tensor", fn, reads, writes, tag)

    def v(self, fn, reads=(), writes=(), tag=""):
        return self.op("vector", fn, reads, writes, tag)

    def s(self, fn, reads=(), writes=(), tag=""):
        return self.op("scalar", fn, reads, writes, tag)

    def g(self, fn, reads=(), writes=(), tag=""):
        return self.op("gpsimd", fn, reads, writes, tag)

    def dma(self, queue, out, in_, reads=(), writes=(), semkey=None, batch=False,
            tag="", **kw):
        assert queue in DMAQ, queue
        if semkey is None:
            semkey = "dma:" + (writes[0] if writes else (reads[0] if reads else tag))
        else:
            semkey = "dma:" + semkey
        if semkey not in self.dma_semkeys:
            self.dma_semkeys.append(semkey)
            self.semkey_queue[semkey] = queue
        assert self.semkey_queue[semkey] == queue, (semkey, queue)
        if batch:
            self.batch_keys.add(semkey)

        def fn(eng):
            return eng.dma_start(out=out, in_=in_, **kw)
        return self._record(_Op(semkey, DMAQ[queue], fn, list(reads), list(writes),
                                len(self.ops), tag, True))

    def fence(self, tiny_ops, key):
        self._fence_marks = getattr(self, "_fence_marks", [])
        marks = []
        for eng, (fn, extra) in tiny_ops.items():
            o = self.op(eng, fn, reads=(), writes=(key + ":" + eng,) + tuple(extra),
                        tag="fence:" + eng)
            o.drain_before = True
            marks.append(o)
        self._fence_marks.append((key, [m.idx for m in marks], len(self.ops)))
        self.fence_keys = tuple(key + ":" + e for e in tiny_ops)

    def resolve(self):
        per_stream = defaultdict(list)
        for o in self.ops:
            per_stream[o.stream].append(o)
            o.stream_pos = len(per_stream[o.stream]) - 1

        for key, mark_idxs, _pt in getattr(self, "_fence_marks", []):
            for mi in mark_idxs:
                mark = self.ops[mi]
                for sk in self.dma_semkeys:
                    last = None
                    for o in per_stream[sk]:
                        if o.idx < mi:
                            last = o
                        else:
                            break
                    if last is not None:
                        mark._deps.append(last)

        know = {p: defaultdict(lambda: -1) for p in PHYS}
        drained_upto = {p: -1 for p in PHYS}
        last_compute = {p: -1 for p in PHYS}
        needed = defaultdict(set)

        for o in self.ops:
            kn = know[o.phys]
            want = defaultdict(lambda: -1)
            drain = o.drain_before
            for d in o._deps:
                if (not d.is_dma) and d.phys == o.phys:
                    if o.phys == "tensor":
                        continue
                    if d.stream_pos > drained_upto[o.phys]:
                        drain = True
                else:
                    if o.is_dma and d.is_dma and d.stream == o.stream:
                        continue
                    if d.stream_pos > kn[d.stream]:
                        want[d.stream] = max(want[d.stream], d.stream_pos)
            if drain:
                o.drain_before = True
                drained_upto[o.phys] = last_compute[o.phys]
            for ps, pos in want.items():
                needed[ps].add(pos)
                o.waits[ps] = pos
                kn[ps] = max(kn[ps], pos)
            if not o.is_dma:
                last_compute[o.phys] = o.stream_pos

        self.milestone_ord = {}
        self.milestone_count = {}
        for stream, lst in per_stream.items():
            if stream.startswith("dma:"):
                for o in lst:
                    o.milestone = True
                    self.milestone_ord[(stream, o.stream_pos)] = o.stream_pos + 1
                self.milestone_count[stream] = len(lst)
            else:
                poss = sorted(needed.get(stream, ()))
                for n, pos in enumerate(poss):
                    lst[pos].milestone = True
                    self.milestone_ord[(stream, pos)] = n + 1
                self.milestone_count[stream] = len(poss)
        for o in self.ops:
            o.waits = {ps: (self.milestone_count[ps] if ps in self.batch_keys
                            else self.milestone_ord[(ps, pos)])
                       for ps, pos in o.waits.items()}
        self.per_stream = per_stream
        self.stats = dict(
            ops=len(self.ops),
            drains=sum(1 for o in self.ops if o.drain_before),
            waits=sum(len(o.waits) for o in self.ops),
            milestones=sum(1 for o in self.ops if o.milestone and not o.is_dma),
            dma_sems=len(self.dma_semkeys))
        return self

    def sem_streams(self):
        return [s for s in COMPUTE if self.milestone_count.get(s, 0) > 0] + \
            list(self.dma_semkeys)

    def make_runner(self, sems):
        by_phys = defaultdict(list)
        for o in self.ops:
            by_phys[o.phys].append(o)

        def run(eng, phys):
            for o in by_phys.get(phys, ()):
                if o.drain_before:
                    eng.drain()
                for ps, ordinal in sorted(o.waits.items()):
                    mult = 16 if ps.startswith("dma:") else 1
                    eng.wait_ge(sems[ps], ordinal * mult)
                inst = o.fn(eng)
                if o.milestone:
                    mult = 16 if o.is_dma else 1
                    if inst is None:
                        raise RuntimeError(f"milestone op '{o.tag}' has no instruction")
                    inst.then_inc(sems[o.stream], mult)
        return run

    def final_dma_waits(self, eng, sems, keys=None):
        ks = (["dma:" + k for k in keys] if keys is not None else list(self.dma_semkeys))
        for k in ks:
            eng.wait_ge(sems[k], self.milestone_count[k] * 16)


def build_program(T, parts="PAB"):
    NSA = T // TSA
    NSB = T // TSB
    NCC = TSA // L

    nc = bass.Bass("TRN2", target_bir_lowering=False)

    def din(name, shape, dt=F32):
        return nc.dram_tensor(name, list(shape), dt, kind="ExternalInput").ap()

    x_in = din("x", [T, C])
    W_attn = {m: din("W" + m, [C, C]) for m in "rkvgo"}
    maa_w1r = din("maa_w1r", [128, NT * 160])
    w2f = [din(f"w2f{f}", [32, C]) for f in range(5)]
    maa_vec = {f: din("maa_" + f, [C]) for f in "xwkvrg"}
    dw1r = din("dw1r", [128, NT * 64])
    dw2 = din("dw2", [64, C])
    tdec = din("tdec", [C])
    u_bc_in = din("u_bc", [128, NT * 64])
    s0_in = din("s0", [128, NT * 64])
    lnw_in = din("lnw_chan", [128, NT])
    maax_chan = din("maax_chan", [128, NT])
    lnb_in = din("lnb_chan", [128, NT])
    wf1 = din("wf1", [FF, C])
    wf2 = din("wf2", [FF, C])
    wf3 = din("wf3", [C, FF])
    tri_ex = din("tri_ex", [128, 64])
    tri_in = din("tri_in", [128, 64])
    tri_suf = din("tri_suf", [128, 64])
    i128 = din("i128", [128, 128], BF16)
    i128f = din("i128f", [128, 128], F32)
    triu2 = din("triu2", [128, 128], BF16)
    tri4 = din("tri4", [128, 256], BF16)
    onesblk = din("onesblk", [128, NT * 32], BF16)
    blockones = din("blockones", [32, C], BF16)

    y_out = nc.dram_tensor("y", [T, C], F32, kind="ExternalOutput").ap()

    scrA = nc.dram_tensor("scrA", [5, NT, 128, C], BF16, kind="Internal").ap()
    scrQ12 = nc.dram_tensor("scrQ12", [2, NF, 128, C], BF16, kind="Internal").ap()
    scrQ3 = nc.dram_tensor("scrQ3", [NT, 128, FF], BF16, kind="Internal").ap()
    x2s = nc.dram_tensor("x2s", [T, C], BF16, kind="Internal").ap()
    x_bf = nc.dram_tensor("x_bf", [T, C], BF16, kind="Internal").ap()
    TQ = T // 4
    scrG = nc.dram_tensor("scrG", [4, NF, 128, TQ], BF16, kind="Internal").ap()

    em = Emitter()
    import contextlib
    stackG = contextlib.ExitStack()
    stackA = contextlib.ExitStack()
    stackB = contextlib.ExitStack()

    def sb(stack, name, shape, dt):
        return stack.enter_context(nc.sbuf_tensor(name, list(shape), dt))

    with stackG:
        c_triex = sb(stackG, "c_triex", [128, 64], F32)
        c_triin = sb(stackG, "c_triin", [128, 64], F32)
        c_trisuf = sb(stackG, "c_trisuf", [128, 64], F32)
        c_i128 = sb(stackG, "c_i128", [128, 128], BF16)
        c_i128f = sb(stackG, "c_i128f", [128, 128], F32)
        c_triu2 = sb(stackG, "c_triu2", [128, 128], BF16)
        c_tri4 = sb(stackG, "c_tri4", [128, 256], BF16)
        c_onesblk = sb(stackG, "c_onesblk", [128, NT * 32], BF16)
        c_blockones = sb(stackG, "c_blockones", [32, C], BF16)
        c_lnw = sb(stackG, "c_lnw", [128, NT], F32)
        c_lnb = sb(stackG, "c_lnb", [128, NT], F32)
        c_lnbt = sb(stackG, "c_lnbt", [128, NT], F32)
        c_lnw_half = sb(stackG, "c_lnw_half", [128, NT], F32)
        c_maa_x = sb(stackG, "c_maa_x", [128, NT], F32)
        S_sb = sb(stackG, "S_sb", [128, NT * 64], F32)
        Sbf = sb(stackG, "Sbf", [128, NT * 64], BF16)
        h_sb = sb(stackG, "h_sb", [128, NT * TSA], BF16)
        hc_sb = sb(stackG, "hc_sb", [128, NT], BF16)
        gacc = sb(stackG, "gacc", [128, 4], F32)
        ginv = sb(stackG, "ginv", [128, 4], F32)
        tiny = sb(stackG, "tiny", [128, 8], F32)
        tinyb = sb(stackG, "tinyb", [128, 8], BF16)
        c_ones1 = sb(stackG, "c_ones1", [128, 1], F32)
        c_onesrow = sb(stackG, "c_onesrow", [1, 128], F32)
        c_bias = sb(stackG, "c_bias", [128, 4], F32)
        stackP = contextlib.ExitStack()
        PW = 512
        prep_in = [sb(stackP, f"prep_in{i}", [128, PW], F32) for i in range(3)]
        prep_bf = [sb(stackP, f"prep_bf{i}", [128, PW], BF16) for i in range(2)]
        prep_out = [sb(stackP, f"prep_out{i}", [128, PW], BF16) for i in range(2)]
        prep_t = [sb(stackP, f"prep_t{i}", [128, PW], F32) for i in range(1)]

        stackPS = contextlib.ExitStack()
        PM = [stackPS.enter_context(nc.psum_tensor(f"PM{i}", [128, 512], F32)) for i in range(2)]
        PC = [stackPS.enter_context(nc.psum_tensor(f"PC{i}", [128, 512], F32)) for i in range(2)]
        PT = [stackPS.enter_context(nc.psum_tensor(f"PT{i}", [128, 1024], BF16)) for i in range(2)]
        PA = stackPS.enter_context(nc.psum_tensor("PA", [128, 512], F32))
        PB = stackPS.enter_context(nc.psum_tensor("PB", [128, 512], F32))

        def ld(dst, src, tag, queue="qsync"):
            em.dma(queue, dst, src, reads=(), writes=(tag,), semkey="init",
                   batch=True, tag="ld_" + tag)

        ld(c_triex[:], tri_ex[:], "c_triex")
        ld(c_triin[:], tri_in[:], "c_triin")
        ld(c_trisuf[:], tri_suf[:], "c_trisuf")
        ld(c_i128[:], i128[:], "c_i128")
        ld(c_i128f[:], i128f[:], "c_i128f")
        ld(c_triu2[:], triu2[:], "c_triu2")
        ld(c_tri4[:], tri4[:], "c_tri4")
        ld(c_onesblk[:], onesblk[:], "c_onesblk")
        ld(c_blockones[:], blockones[:], "c_blockones")
        ld(c_lnw[:], lnw_in[:], "c_lnw")
        ld(c_lnb[:], lnb_in[:], "c_lnb")
        em.dma("qsync", S_sb[:], s0_in[:], reads=(),
               writes=tuple(f"S:{j}" for j in range(NT)), semkey="init",
               batch=True, tag="ld_S")
        em.dma("qsync", c_maa_x[:], maax_chan[:],
               writes=("c_maa_x",), semkey="init", batch=True, tag="ld_maax")

        em.v(lambda e: e.tensor_copy(Sbf[:], S_sb[:]),
             reads=tuple(f"S:{j}" for j in range(NT)),
             writes=tuple(f"Sbf:{j}" for j in range(NT)), tag="Sbf0")
        em.v(lambda e: e.reciprocal(c_lnbt[:], c_lnw[:]), reads=("c_lnw",), writes=("c_lnbt",), tag="rlnw")
        em.v(lambda e: e.tensor_tensor(c_lnbt[:], c_lnbt[:], c_lnb[:], op=ALU.mult),
             reads=("c_lnb", "c_lnbt"), writes=("c_lnbt",), tag="lnbt")
        em.v(lambda e: e.tensor_scalar_mul(c_lnw_half[:], c_lnw[:], 0.5),
             reads=("c_lnw",), writes=("c_lnw_half",), tag="lnwh")
        em.v(lambda e: e.memset(c_ones1[:], 1.0), writes=("c_ones1",), tag="ones1")
        for col, val in [(0, LN_EPS), (1, GN_EPS), (2, 0.5), (3, 1.0)]:
            em.v(lambda e, col=col, val=val: e.memset(c_bias[:, col:col + 1], val),
                 writes=("c_bias",), tag=f"bias{col}")
        em.v(lambda e: e.memset(c_onesrow[:], 1.0), writes=("c_onesrow",), tag="onesr")
        em.v(lambda e: e.memset(hc_sb[:], 0.0), writes=("hc",), tag="hcz")

        MATS = "rkvgo"

        def attn_prepass():
            pc = 0
            for mi, m in enumerate(MATS):
                for ot in range(NT):
                    for piece in range(C // PW):
                        sl = pc % 3
                        pc += 1
                        ikey = f"prepin:{sl}"
                        osl = pc % 2
                        bkey = f"prepbf:{osl}"
                        okey = f"prepout:{osl}"
                        em.dma("qscalar", prep_in[sl][:],
                               W_attn[m][ot * 128:(ot + 1) * 128,
                                         piece * PW:(piece + 1) * PW],
                               reads=(), writes=(ikey,), semkey=f"pa{sl}",
                               tag=f"wp_{m}{ot}_{piece}")
                        em.v(lambda e, sl=sl, osl=osl: e.tensor_copy(
                                prep_bf[osl][:], prep_in[sl][:]),
                             reads=(ikey,), writes=(bkey,), tag=f"wc_{m}{ot}_{piece}")
                        pt = PT[osl]
                        ptk = f"PT{osl}"
                        for j in range(PW // 128):
                            kt = piece * (PW // 128) + j
                            em.t(lambda e, pt=pt, j=j, osl=osl, kt=kt, piece=piece: e.transpose(
                                    pt[:, j * 128:(j + 1) * 128],
                                    prep_bf[osl][:, j * 128:(j + 1) * 128], c_i128[:]),
                                 reads=(bkey, "c_i128"), writes=(ptk,),
                                 tag=f"wt_{m}{ot}_{piece}_{j}")
                        em.s(lambda e, pt=pt, osl=osl: e.copy(prep_out[osl][:], pt[:, 0:PW]),
                             reads=(ptk,), writes=(okey,), tag=f"we_{m}{ot}_{piece}")
                        if m == "o":
                            for j in range(PW // 128):
                                kt = piece * (PW // 128) + j
                                em.v(lambda e, osl=osl, j=j, kt=kt: e.tensor_scalar_mul(
                                        prep_out[osl][:, j * 128:(j + 1) * 128],
                                        prep_out[osl][:, j * 128:(j + 1) * 128],
                                        c_lnw_half[:, kt:kt + 1]),
                                     reads=(okey, "c_lnw_half"), writes=(okey,),
                                     tag=f"wf_{ot}_{piece}_{j}")
                        em.dma("qsync", scrA[mi, ot, :, piece * PW:(piece + 1) * PW],
                               prep_out[osl][:],
                               reads=(okey,), writes=(f"scrA:{m}:{ot}",),
                               semkey=f"ps{osl}", tag=f"ws_{m}{ot}_{piece}")

        attn_prepass()

        ffn_rows = [("wf1", wf1, FF // 128, 0), ("wf2", wf2, FF // 128, 1),
                    ("wf3", wf3, C // 128, 2)]

        def col_chunks(ncols):
            out, c0 = [], 0
            while c0 < ncols:
                w = min(PW, ncols - c0)
                out.append((c0, w))
                c0 += w
            return out

        def gamma_tile(qi, ap, row, c0, w, sl):
            key = f"prepin:{sl}"
            em.dma("qscalar" if sl == 1 else "qsync", prep_in[sl][:, 0:w],
                   ap[row * 128:(row + 1) * 128, c0:c0 + w],
                   reads=(), writes=(key,), semkey=f"pl{sl}", tag=f"gl{qi}_{row}_{c0}")
            em.v(lambda e, sl=sl, w=w: e.tensor_reduce(
                    out=tiny[:, 0:1], in_=prep_in[sl][:, 0:w], axis=AX, op=ALU.add,
                    apply_absolute_value=True),
                 reads=(key,), writes=("tiny",), tag=f"gr{qi}_{row}_{c0}")
            em.v(lambda e, qi=qi: e.tensor_tensor(
                    gacc[:, qi:qi + 1], gacc[:, qi:qi + 1], tiny[:, 0:1], op=ALU.add),
                 reads=("tiny", "gacc"), writes=("gacc",), tag=f"ga{qi}_{row}_{c0}")

        em.v(lambda e: e.memset(gacc[:], 0.0), writes=("gacc",), tag="gacc0")
        gamma_jobs = []
        for qi, (nm, ap, nrows, gi) in enumerate(ffn_rows):
            for row in range(nrows):
                for (c0, w) in col_chunks(ap.shape[1]):
                    gamma_jobs.append((qi, ap, row, c0, w))

        _gct = [0]

        def emit_gamma_slice(jobs):
            for (qi, ap, row, c0, w) in jobs:
                gamma_tile(qi, ap, row, c0, w, sl=_gct[0] % 3)
                _gct[0] += 1

        def emit_gamma_final():
            em.t(lambda e: e.matmul(PC[0][0:1, 0:4], c_ones1[:], gacc[:, 0:4],
                                    start=True, stop=True),
                 reads=("gacc", "c_ones1"), writes=("PC0",), tag="gsum")
            em.v(lambda e: e.tensor_copy(tiny[0:1, 0:4], PC[0][0:1, 0:4]),
                 reads=("PC0",), writes=("tiny",), tag="gtot")
            for qi, (nm, ap, nrows, gi) in enumerate(ffn_rows):
                numel = float(ap.shape[0] * ap.shape[1])
                em.v(lambda e, qi=qi, numel=numel: e.tensor_scalar(
                        out=tiny[0:1, qi:qi + 1], in0=tiny[0:1, qi:qi + 1],
                        scalar1=1.0 / numel, scalar2=Q_EPS, op0=ALU.mult, op1=ALU.add),
                     reads=("tiny",), writes=("tiny",), tag=f"gsc{qi}")
            em.v(lambda e: e.reciprocal(tiny[0:1, 0:3], tiny[0:1, 0:3]),
                 reads=("tiny",), writes=("tiny",), tag="ginv0")
            em.t(lambda e: e.matmul(PC[0][:, 0:3], c_onesrow[:], tiny[0:1, 0:3],
                                    start=True, stop=True),
                 reads=("tiny", "c_onesrow"), writes=("PC0",), tag="gbc")
            em.v(lambda e: e.tensor_copy(ginv[:, 0:3], PC[0][:, 0:3]),
                 reads=("PC0",), writes=("ginv",), tag="ginv")

        quant_jobs = []
        for qi, (nm, ap, nrows, gi) in enumerate(ffn_rows):
            for row in range(nrows):
                for (c0, w) in col_chunks(ap.shape[1]):
                    quant_jobs.append((qi, ap, row, c0, w))

        def quant_tile(qi, ap, row, c0, w, sl, tl):
            ikey = f"prepin:{sl}"
            bkey = f"prepbf:{tl}"
            okey = f"prepout:{tl}"
            tkey = "prept:0"
            pt_t = prep_t[0]
            em.dma("qscalar" if sl == 1 else "qsync", prep_in[sl][:, 0:w],
                   ap[row * 128:(row + 1) * 128, c0:c0 + w],
                   reads=(), writes=(ikey,), semkey=f"pl{sl}", tag=f"ql{qi}_{row}_{c0}")
            em.v(lambda e, sl=sl, tl=tl, qi=qi, w=w: e.tensor_scalar(
                    out=prep_bf[tl][:, 0:w], in0=prep_in[sl][:, 0:w],
                    scalar1=ginv[:, qi:qi + 1], scalar2=0.5,
                    op0=ALU.mult, op1=ALU.is_ge),
                 reads=(ikey, "ginv"), writes=(bkey,), tag=f"qa{qi}_{row}_{c0}")
            em.v(lambda e, sl=sl, qi=qi, w=w, pt_t=pt_t: e.tensor_scalar(
                    out=pt_t[:, 0:w], in0=prep_in[sl][:, 0:w],
                    scalar1=ginv[:, qi:qi + 1], scalar2=-0.5,
                    op0=ALU.mult, op1=ALU.is_le),
                 reads=(ikey, "ginv"), writes=(tkey,), tag=f"qb{qi}_{row}_{c0}")
            em.v(lambda e, tl=tl, w=w, pt_t=pt_t: e.tensor_tensor(
                    prep_bf[tl][:, 0:w], prep_bf[tl][:, 0:w], pt_t[:, 0:w],
                    op=ALU.subtract),
                 reads=(bkey, tkey), writes=(bkey,), tag=f"qq{qi}_{row}_{c0}")
            nk = w // 128
            pt = PT[tl]
            ptk = f"PT{tl}"
            for j in range(nk):
                em.t(lambda e, pt=pt, j=j, tl=tl: e.transpose(
                        pt[:, j * 128:(j + 1) * 128],
                        prep_bf[tl][:, j * 128:(j + 1) * 128], c_i128[:]),
                     reads=(bkey, "c_i128"), writes=(ptk,), tag=f"qtt{qi}_{row}_{c0}_{j}")
            em.s(lambda e, pt=pt, tl=tl, nk=nk: e.copy(
                    prep_out[tl][:, 0:nk * 128], pt[:, 0:nk * 128]),
                 reads=(ptk,), writes=(okey,), tag=f"qe{qi}_{row}_{c0}")
            if qi < 2:
                em.dma("qpool", scrQ12[qi, row, :, c0:c0 + w], prep_out[tl][:, 0:w],
                       reads=(okey,), writes=(f"scrQ:{qi}:{row}:{c0}",),
                       semkey=f"qsq{tl}", tag=f"qst{qi}_{row}_{c0}")
            else:
                em.dma("qpool", scrQ3[row, :, c0:c0 + w], prep_out[tl][:, 0:w],
                       reads=(okey,), writes=(f"scrQ3:{row}:{c0}",),
                       semkey=f"qsq{tl}", tag=f"qst3_{row}_{c0}")

        _qdrip = {"i": 0, "on": False}

        def drip_quant(n=1):
            if not _qdrip["on"]:
                return
            i = _qdrip["i"]
            for _ in range(n):
                if i >= len(quant_jobs):
                    break
                qi, ap, row, c0, w = quant_jobs[i]
                quant_tile(qi, ap, row, c0, w, sl=i % 3, tl=i % 2)
                i += 1
            _qdrip["i"] = i

        w1sb = sb(stackA, "w1sb", [128, NT * 160], BF16)
        w2aug = [sb(stackA, f"w2aug{f}", [33, C], BF16) for f in range(5)]
        dw1sb = sb(stackA, "dw1sb", [128, NT * 64], BF16)
        dw2aug = sb(stackA, "dw2aug", [65, C], F32)
        c_ubc = sb(stackA, "c_ubc", [128, NT * 64], BF16)
        em.dma("qpool", dw2aug[0:64, :], dw2[:], writes=("dw2aug:a",),
               semkey="initp", batch=True, tag="ld_dw2a")
        em.dma("qpool", dw2aug[64:65, :], tdec[:].rearrange("(a c) -> a c", a=1),
               writes=("dw2aug:b",), semkey="initp", batch=True, tag="ld_dw2b")
        em.dma("qpool", c_ubc[:], u_bc_in[:], writes=("c_ubc",), semkey="initp",
               batch=True, tag="ld_ubc")
        em.dma("qpool", w1sb[:], maa_w1r[:],
               writes=("w1sb",), semkey="initp", batch=True, tag="ld_w1")
        for f, fname in enumerate("wkvrg"):
            em.dma("qpool", w2aug[f][0:32, :], w2f[f][:],
                   writes=(f"w2aug{f}:a",), semkey="initp", batch=True, tag=f"ld_w2{f}")
            em.dma("qpool", w2aug[f][32:33, :],
                   maa_vec[fname][:].rearrange("(a c) -> a c", a=1),
                   writes=(f"w2aug{f}:b",), semkey="initp", batch=True, tag=f"ld_maa{f}")
        em.dma("qpool", dw1sb[:], dw1r[:],
               writes=("dw1sb",), semkey="initp", batch=True, tag="ld_dw1")
        XCH = T // 8
        for i in range(8):
            em.dma("qpool", x_bf[i * XCH:(i + 1) * XCH, :],
                   x_in[i * XCH:(i + 1) * XCH, :],
                   reads=(), writes=(f"xbf:{i}",), semkey=f"xbf{i}", tag=f"xbf{i}")

        def xbf_keys(r0, r1):
            return tuple(f"xbf:{i}" for i in range(r0 // XCH, (r1 - 1) // XCH + 1))
        xt = [sb(stackA, f"xt{i}", [128, C], BF16) for i in range(2)]
        sqt = sb(stackA, "sqt", [128, C], BF16)
        xx_sb = sb(stackA, "xx_sb", [128, NT * TSA], BF16)
        xf = [sb(stackA, f"xf{i}", [128, NT * TSA], BF16) for i in range(2)]
        r_sb = sb(stackA, "r_sb", [128, NT * TSA], BF16)
        k_sb = sb(stackA, "k_sb", [128, NT * TSA], BF16)
        g2_sb = sb(stackA, "g2_sb", [128, NT * TSA], BF16)
        t1g = sb(stackA, "t1g", [128, TSA], BF16)
        Vtp = sb(stackA, "Vtp", [128, (NCC // 2) * C], BF16)
        ew_tp = sb(stackA, "ewtp", [128, C], F32)
        tanh_f = [sb(stackA, f"tanhf{f}", [33, TSA], BF16) for f in range(5)]
        tanh2 = sb(stackA, "tanh2", [65, TSA], F32)
        t2m = [sb(stackA, f"t2m{i}", [128, TSA], BF16) for i in range(2)]
        E1 = [sb(stackA, f"E1_{i}", [128, NT * 64], BF16) for i in range(2)]
        _e2 = sb(stackA, "E2_0", [128, NT * 64], BF16)
        E2 = [_e2, _e2]
        E3 = sb(stackA, "E3", [128, C], BF16)
        Gam = [sb(stackA, f"Gam{i}", [128, NT], F32) for i in range(2)]
        ktp = sb(stackA, "ktp", [128, C], BF16)
        P2 = sb(stackA, "P2", [128, NT * 64], BF16)
        A_sl = sb(stackA, "A_sl", [128, 512], BF16)
        yt2 = sb(stackA, "yt2", [128, 512], BF16)
        du_sb = sb(stackA, "du_sb", [32, TSA], BF16)
        y_sb = sb(stackA, "y_sb", [128, (TSA // 128) * C], BF16)
        stat = sb(stackA, "stat", [128, 64], F32)
        abT = sb(stackA, "abT", [32, 256], BF16)
        yn_tp = sb(stackA, "yn_tp", [128, C], BF16)
        xz_sb = sb(stackA, "xz_sb", [128, NT * TSA], BF16)
        ws = [sb(stackA, f"ws{i}", [128, C], BF16) for i in range(4)]

        def seg_ln1(s):
            t0 = s * TSA
            h3 = h_sb[:].rearrange("p (n t) -> p n t", t=TSA)
            for tt in range(TSA // 128):
                b = tt % 2
                xk = f"xt:{b}"
                em.dma("qsync", xt[b][:], x_bf[t0 + tt * 128: t0 + (tt + 1) * 128, :],
                       reads=xbf_keys(t0 + tt * 128, t0 + (tt + 1) * 128),
                       writes=(xk,), semkey=f"xin{b}", tag=f"xl{s}_{tt}")
                em.s(lambda e, b=b: e.square(sqt[:], xt[b][:]),
                     reads=(xk,), writes=("sqt",), tag=f"sq{s}_{tt}")
                em.v(lambda e, b=b: e.tensor_reduce(
                        out=stat[:, 0:1], in_=xt[b][:], axis=AX, op=ALU.add),
                     reads=(xk,), writes=("stat",), tag=f"r1{s}_{tt}")
                em.v(lambda e: e.tensor_reduce(
                        out=stat[:, 1:2], in_=sqt[:], axis=AX, op=ALU.add),
                     reads=("sqt",), writes=("stat",), tag=f"r2{s}_{tt}")
                em.v(lambda e: e.tensor_scalar_mul(stat[:, 2:3], stat[:, 0:1], 1.0 / C),
                     reads=("stat",), writes=("stat",), tag=f"mu{s}_{tt}")
                em.v(lambda e: e.tensor_tensor(stat[:, 3:4], stat[:, 2:3], stat[:, 2:3],
                                               op=ALU.mult),
                     reads=("stat",), writes=("stat",), tag=f"mu2{s}_{tt}")
                em.v(lambda e: e.tensor_scalar(
                        out=stat[:, 4:5], in0=stat[:, 1:2], scalar1=1.0 / C,
                        scalar2=stat[:, 3:4], op0=ALU.mult, op1=ALU.subtract),
                     reads=("stat",), writes=("stat",), tag=f"var{s}_{tt}")
                em.s(lambda e: e.activation(stat[:, 5:6], stat[:, 4:5], AF.Ln, bias=c_bias[:, 0:1]),
                     reads=("stat",), writes=("stat",), tag=f"ln{s}_{tt}")
                em.s(lambda e: e.activation(stat[:, 6:7], stat[:, 5:6], AF.Exp, scale=-0.5),
                     reads=("stat",), writes=("stat",), tag=f"ex{s}_{tt}")
                em.v(lambda e, b=b: e.tensor_scalar(
                        out=yn_tp[:], in0=xt[b][:], scalar1=stat[:, 2:3],
                        scalar2=stat[:, 6:7], op0=ALU.subtract, op1=ALU.mult),
                     reads=(xk, "stat"), writes=tuple(f"yn:{q}" for q in range(4)),
                     tag=f"hn{s}_{tt}")
                for half in range(2):
                    pt = PT[half]
                    ptk = f"PT{half}"
                    for j in range(8):
                        jt = half * 8 + j
                        em.t(lambda e, pt=pt, j=j, jt=jt: e.transpose(
                                pt[:, j * 128:(j + 1) * 128],
                                yn_tp[:, jt * 128:(jt + 1) * 128], c_i128[:]),
                             reads=(f"yn:{jt // 4}", "c_i128"), writes=(ptk,),
                             tag=f"ht{s}_{tt}_{jt}")
                    em.s(lambda e, pt=pt, half=half, tt=tt: e.copy(
                            h3[:, half * 8:(half + 1) * 8, tt * 128:(tt + 1) * 128],
                            pt[:].rearrange("p (j n) -> p j n", n=128)),
                         reads=(ptk,), writes=(f"h:{half}",), tag=f"he{s}_{tt}_{half}")
            xx3 = xx_sb[:].rearrange("p (n t) -> p n t", t=TSA)
            xxk = tuple(f"xx:{ct}" for ct in range(NT))
            hks = ("h:0", "h:1")
            em.v(lambda e, xx3=xx3: e.tensor_tensor(
                    xx3[:, :, 1:TSA], h3[:, :, 0:TSA - 1], h3[:, :, 1:TSA],
                    op=ALU.subtract),
                 reads=hks, writes=xxk, tag=f"xx{s}")
            em.v(lambda e, xx3=xx3: e.tensor_tensor(
                    xx3[:, :, 0:1], hc_sb[:].rearrange("p (n o) -> p n o", o=1),
                    h3[:, :, 0:1], op=ALU.subtract),
                 reads=hks + ("hc",), writes=xxk, tag=f"xxc{s}")
            em.v(lambda e: e.tensor_copy(
                    hc_sb[:].rearrange("p (n o) -> p n o", o=1), h3[:, :, TSA - 1:TSA]),
                 reads=hks + xxk, writes=("hc",), tag=f"hcs{s}")
            for jt in range(NT):
                em.v(lambda e, jt=jt: e.scalar_tensor_tensor(
                        out=xz_sb[:, jt * TSA:(jt + 1) * TSA],
                        in0=xx_sb[:, jt * TSA:(jt + 1) * TSA],
                        scalar=c_maa_x[:, jt:jt + 1],
                        in1=h3[:, jt, 0:TSA], op0=ALU.mult, op1=ALU.add),
                     reads=(f"xx:{jt}", f"h:{jt // 8}", "c_maa_x"),
                     writes=(f"xz:{jt}",), tag=f"xxx{s}_{jt}")

        def seg_maa(s):
            banks = [(PM[0], "PM0", 0), (PM[0], "PM0", 256), (PM[1], "PM1", 0),
                     (PM[1], "PM1", 256), (PC[0], "PC0", 0)]
            for f in range(5):
                pb, pbk, co = banks[f]
                for kt in range(NT):
                    em.t(lambda e, pb=pb, co=co, f=f, kt=kt: e.matmul(
                            pb[0:32, co:co + TSA],
                            w1sb[:, kt * 160 + f * 32: kt * 160 + (f + 1) * 32],
                            xz_sb[:, kt * TSA:(kt + 1) * TSA],
                            start=(kt == 0), stop=(kt == NT - 1)),
                         reads=(f"xz:{kt}", "w1sb"), writes=(pbk,), tag=f"maa{s}_{f}_{kt}")
                em.s(lambda e, pb=pb, co=co, f=f: e.activation(
                        tanh_f[f][0:32, :], pb[0:32, co:co + TSA], AF.Tanh),
                     reads=(pbk,), writes=(f"tanhf{f}",), tag=f"tanh{s}_{f}")

        def seg_xf(s, f, dst, dkey):
            h3 = h_sb[:].rearrange("p (n t) -> p n t", t=TSA)
            for ct in range(NT):
                pb, pbk = (PM[ct % 2], f"PM{ct % 2}")
                tm, tmk = (t2m[ct % 2], f"t2m:{ct % 2}")
                em.t(lambda e, pb=pb, f=f, ct=ct: e.matmul(
                        pb[:, 0:TSA],
                        w2aug[f][:, ct * 128:(ct + 1) * 128], tanh_f[f][:],
                        start=True, stop=True),
                     reads=(f"tanhf{f}", f"w2aug{f}:a", f"w2aug{f}:b"), writes=(pbk,), tag=f"mf{s}_{f}_{ct}")
                em.v(lambda e, pb=pb, ct=ct, tm=tm: e.tensor_tensor(
                        tm[:], pb[:, 0:TSA], xx_sb[:, ct * TSA:(ct + 1) * TSA],
                        op=ALU.mult),
                     reads=(pbk, f"xx:{ct}"), writes=(tmk,), tag=f"t2{s}_{f}_{ct}")
                em.v(lambda e, ct=ct, dst=dst, tm=tm: e.tensor_tensor(
                        dst[:, ct * TSA:(ct + 1) * TSA], tm[:],
                        h3[:, ct, 0:TSA], op=ALU.add),
                     reads=(tmk, f"h:{ct // 8}"), writes=(f"{dkey}:{ct}",), tag=f"xf{s}_{f}_{ct}")

        def stream_w(s, m, ot, slot):
            mi = MATS.index(m)
            wk = f"ws:{slot}"
            q = "qscalar" if slot % 2 == 1 else "qsync"
            em.dma(q, ws[slot][:], scrA[mi, ot], reads=(f"scrA:{m}:{ot}",),
                   writes=(wk,), semkey=f"ws{slot}", tag=f"wl{s}_{m}{ot}")
            return wk

        def seg_proj_chan(s, m, src_key, src, dst, dkey, evac):
            for ot in range(NT):
                slot = ot % 4
                wk = stream_w(s, m, ot, slot)
                pb, pbk = (PM[ot % 2], f"PM{ot % 2}")
                for kt in range(NT):
                    em.t(lambda e, pb=pb, slot=slot, kt=kt, src=src: e.matmul(
                            pb[:, 0:TSA], ws[slot][:, kt * 128:(kt + 1) * 128],
                            src[:, kt * TSA:(kt + 1) * TSA],
                            start=(kt == 0), stop=(kt == NT - 1)),
                         reads=(wk, f"{src_key}:{kt}"), writes=(pbk,),
                         tag=f"{m}mm{s}_{ot}_{kt}")
                evac(ot, pb, pbk)
                drip_quant(1)

        def seg_attn_matmuls(s):
            seg_xf(s, 0, xf[0], "xf:0")
            for kt in range(NT):
                em.t(lambda e, kt=kt: e.matmul(
                        PC[1][0:64, 0:TSA], dw1sb[:, kt * 64:(kt + 1) * 64],
                        xf[0][:, kt * TSA:(kt + 1) * TSA],
                        start=(kt == 0), stop=(kt == NT - 1)),
                     reads=(f"xf:0:{kt}", "dw1sb"), writes=("PC1",), tag=f"dw1{s}_{kt}")
            em.s(lambda e: e.activation(tanh2[0:64, :], PC[1][0:64, 0:TSA], AF.Tanh),
                 reads=("PC1",), writes=("tanh2",), tag=f"tanh2{s}")
            seg_xf(s, 1, xf[1], "xf:1")
            def evk(ot, pb, pbk):
                em.v(lambda e, pb=pb, ot=ot: e.tensor_copy(
                        k_sb[:, ot * TSA:(ot + 1) * TSA], pb[:, 0:TSA]),
                     reads=(pbk,), writes=(f"k:{ot}",), tag=f"kev{s}_{ot}")
            seg_proj_chan(s, "k", "xf:1", xf[1], k_sb, "k_sb", evk)
            seg_xf(s, 2, xf[0], "xf:0")
            for ot in range(NT):
                slot = ot % 4
                wk = stream_w(s, "v", ot, slot)
                pb, pbk = (PM[ot % 2], f"PM{ot % 2}")
                for tt in range(TSA // 128):
                    for kt in range(NT):
                        em.t(lambda e, pb=pb, slot=slot, kt=kt, tt=tt: e.matmul(
                                pb[:, tt * 128:(tt + 1) * 128],
                                xf[0][:, kt * TSA + tt * 128: kt * TSA + tt * 128 + 128],
                                ws[slot][:, kt * 128:(kt + 1) * 128],
                                start=(kt == 0), stop=(kt == NT - 1)),
                             reads=(wk, f"xf:0:{kt}"), writes=(pbk,), tag=f"vmm{s}_{tt}_{ot}_{kt}")
                for tt in range(TSA // 128):
                    for hh in range(2):
                        em.v(lambda e, pb=pb, hh=hh, tt=tt, ot=ot: e.tensor_copy(
                                Vtp[hh * 64:(hh + 1) * 64,
                                    tt * C + ot * 128: tt * C + (ot + 1) * 128],
                                pb[hh * 64:(hh + 1) * 64, tt * 128:(tt + 1) * 128]),
                             reads=(pbk,), writes=(f"Vtp:{2 * tt + hh}:{ot}",),
                             tag=f"vev{s}_{tt}_{ot}_{hh}")
                drip_quant(1)
            seg_xf(s, 3, xf[1], "xf:1")
            def evr(ot, pb, pbk):
                em.v(lambda e, pb=pb, ot=ot: e.tensor_copy(
                        r_sb[:, ot * TSA:(ot + 1) * TSA], pb[:, 0:TSA]),
                     reads=(pbk,), writes=(f"r:{ot}",), tag=f"rev{s}_{ot}")
            seg_proj_chan(s, "r", "xf:1", xf[1], r_sb, "r_sb", evr)
            seg_xf(s, 4, xf[0], "xf:0")
            def evg(ot, pb, pbk):
                em.s(lambda e, pb=pb: e.activation(t1g[:], pb[:, 0:TSA], AF.Tanh, scale=0.5),
                     reads=(pbk,), writes=("t1g",), tag=f"gt{s}_{ot}")
                em.s(lambda e: e.activation(t1g[:], t1g[:], AF.Copy, bias=1.0),
                     reads=("t1g",), writes=("t1g",), tag=f"g1{s}_{ot}")
                em.v(lambda e, pb=pb, ot=ot: e.tensor_tensor(
                        g2_sb[:, ot * TSA:(ot + 1) * TSA], pb[:, 0:TSA], t1g[:],
                        op=ALU.mult),
                     reads=(pbk, "t1g"), writes=(f"g2:{ot}",), tag=f"g2{s}_{ot}")
            seg_proj_chan(s, "g", "xf:0", xf[0], g2_sb, "g2_sb", evg)


        def seg_decay_tile(s, tt):
            for q4 in range(4):
                pb, pbk = (PM[q4 % 2], f"PM{q4 % 2}")
                em.t(lambda e, pb=pb, tt=tt, q4=q4: e.matmul(
                        pb[:, 0:512], tanh2[:, tt * 128:(tt + 1) * 128],
                        dw2aug[:, q4 * 512:(q4 + 1) * 512],
                        start=True, stop=True),
                     reads=("tanh2", "dw2aug:a", "dw2aug:b"), writes=(pbk,), tag=f"dw2{s}_{tt}_{q4}")
                em.s(lambda e, pb=pb, q4=q4: e.activation(
                        ew_tp[:, q4 * 512:(q4 + 1) * 512], pb[:, 0:512], AF.Exp),
                     reads=(pbk,), writes=(f"ew:{q4}",), tag=f"ew{s}_{tt}_{q4}")

        def seg_wkv(s):
            for c in range(NCC):
                tt = c // 2
                rb = 64 * (c % 2)
                cp = c % 2
                if rb == 0:
                    seg_decay_tile(s, tt)
                e1k = [f"E1:{cp}:0", f"E1:{cp}:1"]
                e2k = ["E2:0", "E2:1"]
                gmk = f"Gam:{cp}"
                for which, tri, dst, dkeys in [(0, c_triex, E1[cp], e1k),
                                               (1, c_triin, E2[cp], e2k)]:
                    for half in range(2):
                        pc, pck = (PC[half], f"PC{half}")
                        for j in range(8):
                            jt = half * 8 + j
                            em.t(lambda e, pc=pc, j=j, jt=jt, rb=rb, tri=tri: e.matmul(
                                    pc[:, j * 64:(j + 1) * 64],
                                    ew_tp[rb:rb + 64, jt * 128:(jt + 1) * 128],
                                    tri[rb:rb + 64, :], start=True, stop=True),
                                 reads=(f"ew:{jt // 4}", "c_triex", "c_triin"),
                                 writes=(pck,), tag=f"cs{s}_{c}_{which}_{jt}")
                        sc = -1.0 if which == 0 else 1.0
                        em.s(lambda e, pc=pc, dst=dst, half=half, sc=sc: e.activation(
                                dst[:, half * 512:(half + 1) * 512], pc[:], AF.Exp, scale=sc),
                             reads=(pck,), writes=(dkeys[half],), tag=f"Ee{s}_{c}_{which}_{half}")
                        if which == 1:
                            em.s(lambda e, pc=pc, half=half, cp=cp: e.activation(
                                    Gam[cp][:, half * 8:(half + 1) * 8],
                                    pc[:].rearrange("p (j x) -> p j x", x=64)[:, :, 63],
                                    AF.Exp, scale=-1.0),
                                 reads=(pck,), writes=(gmk,), tag=f"Gm{s}_{c}_{half}")
                for q4 in range(4):
                    pc, pck = (PC[q4 % 2], f"PC{q4 % 2}")
                    em.t(lambda e, pc=pc, q4=q4, rb=rb: e.matmul(
                            pc[rb:rb + 64, 0:512], c_trisuf[rb:rb + 64, :],
                            ew_tp[rb:rb + 64, q4 * 512:(q4 + 1) * 512],
                            start=True, stop=True),
                         reads=(f"ew:{q4}", "c_trisuf"), writes=(pck,), tag=f"sf{s}_{c}_{q4}")
                    em.s(lambda e, pc=pc, q4=q4, rb=rb: e.activation(
                            E3[rb:rb + 64, q4 * 512:(q4 + 1) * 512], pc[rb:rb + 64, 0:512],
                            AF.Exp, scale=-1.0),
                         reads=(pck,), writes=(f"E3:{q4}",), tag=f"E3{s}_{c}_{q4}")
                r3 = r_sb[:].rearrange("p (n t) -> p n t", t=TSA)
                k3 = k_sb[:].rearrange("p (n t) -> p n t", t=TSA)
                rks = tuple(f"r:{o}" for o in range(NT)) + tuple(f"k:{o}" for o in range(NT))
                em.v(lambda e, c=c, cp=cp: e.tensor_tensor(
                        E1[cp][:].rearrange("p (n t) -> p n t", t=64),
                        r3[:, :, c * 64:(c + 1) * 64],
                        E1[cp][:].rearrange("p (n t) -> p n t", t=64), op=ALU.mult),
                     reads=rks[:NT] + tuple(e1k), writes=tuple(e1k), tag=f"Qp{s}_{c}")
                em.v(lambda e, c=c, cp=cp: e.tensor_tensor(
                        E2[cp][:].rearrange("p (n t) -> p n t", t=64),
                        k3[:, :, c * 64:(c + 1) * 64],
                        E2[cp][:].rearrange("p (n t) -> p n t", t=64), op=ALU.mult),
                     reads=rks[NT:] + tuple(e2k), writes=tuple(e2k), tag=f"Kp{s}_{c}")
                for half in range(2):
                    pt, ptk = (PT[half], f"PT{half}")
                    for j in range(8):
                        jt = half * 8 + j
                        em.t(lambda e, pt=pt, j=j, jt=jt, c=c, rb=rb: e.transpose(
                                pt[rb:rb + 64, j * 128:(j + 1) * 128],
                                k_sb[:, jt * TSA + c * 64: jt * TSA + c * 64 + 64],
                                c_i128[:]),
                             reads=(f"k:{jt}", "c_i128"), writes=(ptk,), tag=f"kt{s}_{c}_{jt}")
                    em.s(lambda e, pt=pt, half=half, rb=rb: e.copy(
                            ktp[rb:rb + 64, half * 1024:(half + 1) * 1024],
                            pt[rb:rb + 64, :]),
                         reads=(ptk,), writes=(f"ktp:{half}",), tag=f"kte{s}_{c}_{half}")
                em.v(lambda e, rb=rb: e.tensor_tensor(
                        ktp[rb:rb + 64, :], ktp[rb:rb + 64, :], E3[rb:rb + 64, :],
                        op=ALU.mult),
                     reads=("ktp:0", "ktp:1") + tuple(f"E3:{q}" for q in range(4)),
                     writes=("ktp:0", "ktp:1"), tag=f"Kpp{s}_{c}")
                em.v(lambda e, c=c: e.tensor_tensor(
                        P2[:].rearrange("p (n t) -> p n t", t=64),
                        r3[:, :, c * 64:(c + 1) * 64], k3[:, :, c * 64:(c + 1) * 64],
                        op=ALU.mult),
                     reads=rks, writes=("P2",), tag=f"P1{s}_{c}")
                em.v(lambda e: e.tensor_tensor(P2[:], P2[:], c_ubc[:], op=ALU.mult),
                     reads=("P2", "c_ubc"), writes=("P2",), tag=f"P2{s}_{c}")
                for jt in range(NT):
                    em.t(lambda e, jt=jt: e.matmul(
                            PA[0:32, 448:512], c_onesblk[:, jt * 32:(jt + 1) * 32],
                            P2[:, jt * 64:(jt + 1) * 64],
                            start=(jt == 0), stop=(jt == NT - 1)),
                         reads=("P2", "c_onesblk"), writes=("PA",), tag=f"du{s}_{c}_{jt}")
                em.s(lambda e, c=c: e.copy(du_sb[:, c * 64:(c + 1) * 64],
                                           PA[0:32, 448:512]),
                     reads=("PA",), writes=(f"du:{c}",), tag=f"due{s}_{c}")
                vcol = (c // 2) * C
                for g in range(NT // 4):
                    jts = [g * 4 + i for i in range(4)]
                    vks = tuple(f"Vtp:{c}:{j}" for j in jts)
                    for hh, pab, pabk in ((0, PA, "PA"), (1, PB, "PB")):
                        for i, jt in enumerate(jts):
                            em.t(lambda e, jt=jt, hh=hh, rb=rb, pab=pab, i=i, cp=cp: e.matmul(
                                    pab[rb:rb + 64, i * 64:(i + 1) * 64],
                                    E2[cp][hh * 64:(hh + 1) * 64, jt * 64:(jt + 1) * 64],
                                    E1[cp][hh * 64:(hh + 1) * 64, jt * 64:(jt + 1) * 64],
                                    start=True, stop=True),
                                 reads=tuple(e1k) + tuple(e2k), writes=(pabk,),
                                 tag=f"A{s}_{c}_{jt}_{hh}")
                        em.v(lambda e, hh=hh, rb=rb, pab=pab: e.tensor_tensor(
                                A_sl[rb:rb + 64, :].rearrange(
                                    "p (i x) -> p i x", x=128)[:, :, hh * 64:hh * 64 + 64],
                                pab[rb:rb + 64, 0:256].rearrange("p (i x) -> p i x", x=64),
                                c_tri4[rb:rb + 64, :].rearrange("p (i x) -> p i x", x=64),
                                op=ALU.mult),
                             reads=(pabk, "c_tri4"), writes=("Agrp",),
                             tag=f"Am{s}_{c}_{g}_{hh}")
                    for i, jt in enumerate(jts):
                        for hh in range(2):
                            em.t(lambda e, jt=jt, hh=hh, rb=rb, vcol=vcol, i=i: e.matmul(
                                    PC[0][rb:rb + 64, i * 128 + hh * 64: i * 128 + hh * 64 + 64],
                                    A_sl[rb:rb + 64, i * 128 + hh * 64: i * 128 + hh * 64 + 64],
                                    Vtp[rb:rb + 64,
                                        vcol + jt * 128 + hh * 64: vcol + jt * 128 + hh * 64 + 64],
                                    start=True, stop=True),
                                 reads=("Agrp", f"Vtp:{c}:{jt}"), writes=("PC0",),
                                 tag=f"Yi{s}_{c}_{jt}_{hh}")
                        em.t(lambda e, jt=jt, rb=rb, i=i, cp=cp: e.matmul(
                                PC[1][rb:rb + 64, 256 + i * 64: 256 + (i + 1) * 64],
                                E1[cp][0:64, jt * 64:(jt + 1) * 64],
                                Sbf[0:64, jt * 64:(jt + 1) * 64],
                                start=True, stop=True),
                             reads=tuple(e1k) + (f"Sbf:{jt}",), writes=("PC1",),
                             tag=f"Ys{s}_{c}_{jt}_0")
                        em.t(lambda e, jt=jt, rb=rb, i=i, cp=cp: e.matmul(
                                PM[0][rb:rb + 64, 256 + i * 64: 256 + (i + 1) * 64],
                                E1[cp][64:128, jt * 64:(jt + 1) * 64],
                                Sbf[64:128, jt * 64:(jt + 1) * 64],
                                start=True, stop=True),
                             reads=tuple(e1k) + (f"Sbf:{jt}",), writes=("PM0",),
                             tag=f"Ys{s}_{c}_{jt}_1")
                    em.v(lambda e, rb=rb: e.tensor_copy(
                            yt2[rb:rb + 64, :], PC[0][rb:rb + 64, 0:512]),
                         reads=("PC0",), writes=("yt2",), tag=f"Yc{s}_{c}_{g}")
                    yb = y_sb[:].rearrange("p (t x) -> p t x", x=C)
                    em.v(lambda e, rb=rb, tt=tt, g=g, yb=yb: e.tensor_tensor(
                            yb[rb:rb + 64, tt, g * 512:(g + 1) * 512].rearrange(
                                "p (i x) -> p i x", x=128)[:, :, 0:64],
                            yt2[rb:rb + 64, :].rearrange("p (i x) -> p i x", x=128)[:, :, 0:64],
                            PC[1][rb:rb + 64, 256:512].rearrange("p (i x) -> p i x", x=64),
                            op=ALU.add),
                         reads=("yt2", "PC1"), writes=(f"y:{tt}:{g}",), tag=f"Ye0{s}_{c}_{g}")
                    em.v(lambda e, rb=rb, tt=tt, g=g, yb=yb: e.tensor_tensor(
                            yb[rb:rb + 64, tt, g * 512:(g + 1) * 512].rearrange(
                                "p (i x) -> p i x", x=128)[:, :, 64:128],
                            yt2[rb:rb + 64, :].rearrange("p (i x) -> p i x", x=128)[:, :, 64:128],
                            PM[0][rb:rb + 64, 256:512].rearrange("p (i x) -> p i x", x=64),
                            op=ALU.add),
                         reads=("yt2", "PM0"), writes=(f"y:{tt}:{g}",), tag=f"Ye1{s}_{c}_{g}")
                    for i, jt in enumerate(jts):
                        for hh in range(2):
                            em.t(lambda e, jt=jt, hh=hh, rb=rb, vcol=vcol, i=i: e.matmul(
                                    PM[1][hh * 64:(hh + 1) * 64, 256 + i * 64: 256 + (i + 1) * 64],
                                    ktp[rb:rb + 64,
                                        jt * 128 + hh * 64: jt * 128 + hh * 64 + 64],
                                    Vtp[rb:rb + 64,
                                        vcol + jt * 128 + hh * 64: vcol + jt * 128 + hh * 64 + 64],
                                    start=True, stop=True),
                                 reads=("ktp:0", "ktp:1", f"Vtp:{c}:{jt}"),
                                 writes=("PM1",), tag=f"KV{s}_{c}_{jt}_{hh}")
                    for i, jt in enumerate(jts):
                        em.v(lambda e, jt=jt, i=i, cp=cp: e.scalar_tensor_tensor(
                                out=S_sb[:, jt * 64:(jt + 1) * 64],
                                in0=S_sb[:, jt * 64:(jt + 1) * 64],
                                scalar=Gam[cp][:, jt:jt + 1],
                                in1=PM[1][:, 256 + i * 64: 256 + (i + 1) * 64],
                                op0=ALU.mult, op1=ALU.add),
                             reads=("PM1", gmk, f"S:{jt}"), writes=(f"S:{jt}",),
                             tag=f"Su{s}_{c}_{jt}")
                        em.v(lambda e, jt=jt: e.tensor_copy(
                                Sbf[:, jt * 64:(jt + 1) * 64], S_sb[:, jt * 64:(jt + 1) * 64]),
                             reads=(f"S:{jt}",), writes=(f"Sbf:{jt}",), tag=f"Sc{s}_{c}_{jt}")

        def seg_gn_gate(s):
            for tt in range(TSA // 128):
                ytk = tuple(f"y:{tt}:{g}" for g in range(4))
                for q4 in range(4):
                    pb, pbk = (PM[q4 % 2], f"PM{q4 % 2}")
                    em.t(lambda e, pb=pb, tt=tt, q4=q4: e.matmul(
                            pb[:, 0:512], du_sb[:, tt * 128:(tt + 1) * 128],
                            c_blockones[:, q4 * 512:(q4 + 1) * 512],
                            start=True, stop=True),
                         reads=(f"du:{2 * tt}", f"du:{2 * tt + 1}", "c_blockones"),
                         writes=(pbk,), tag=f"db{s}_{tt}_{q4}")
                    vk = tuple(f"Vtp:{2 * tt + hh}:{q4 * 4 + i}"
                               for hh in range(2) for i in range(4))
                    em.v(lambda e, pb=pb, tt=tt, q4=q4: e.tensor_tensor(
                            yt2[:, 0:512], pb[:, 0:512],
                            Vtp[:, tt * C + q4 * 512: tt * C + (q4 + 1) * 512],
                            op=ALU.mult),
                         reads=(pbk, "yt2") + vk, writes=("yt2",), tag=f"dm{s}_{tt}_{q4}")
                    em.v(lambda e, tt=tt, q4=q4: e.tensor_tensor(
                            y_sb[:, tt * C + q4 * 512: tt * C + (q4 + 1) * 512],
                            y_sb[:, tt * C + q4 * 512: tt * C + (q4 + 1) * 512],
                            yt2[:, 0:512], op=ALU.add),
                         reads=("yt2", f"y:{tt}:{q4}"), writes=(f"y:{tt}:{q4}",),
                         tag=f"da{s}_{tt}_{q4}")
                em.s(lambda e, tt=tt: e.square(sqt[:], y_sb[:, tt * C:(tt + 1) * C]),
                     reads=ytk, writes=("sqt",), tag=f"gs{s}_{tt}")
                em.v(lambda e, tt=tt: e.tensor_reduce(
                        out=stat[:, 0:32],
                        in_=y_sb[:, tt * C:(tt + 1) * C].rearrange("p (h n) -> p h n", n=64),
                        axis=AX, op=ALU.add),
                     reads=ytk, writes=("stat",), tag=f"g1{s}_{tt}")
                em.v(lambda e: e.tensor_reduce(
                        out=stat[:, 32:64],
                        in_=sqt[:].rearrange("p (h n) -> p h n", n=64),
                        axis=AX, op=ALU.add),
                     reads=("sqt",), writes=("stat",), tag=f"g2r{s}_{tt}")
                em.v(lambda e: e.tensor_scalar_mul(stat[:, 0:32], stat[:, 0:32], 1.0 / 64),
                     reads=("stat",), writes=("stat",), tag=f"gmu{s}_{tt}")
                em.v(lambda e: e.tensor_scalar_mul(stat[:, 32:64], stat[:, 32:64], 1.0 / 64),
                     reads=("stat",), writes=("stat",), tag=f"gm2{s}_{tt}")
                em.v(lambda e: e.tensor_tensor(sqt[:, 0:32], stat[:, 0:32], stat[:, 0:32],
                                               op=ALU.mult),
                     reads=("stat",), writes=("sqt",), tag=f"gmm{s}_{tt}")
                em.v(lambda e: e.tensor_tensor(stat[:, 32:64], stat[:, 32:64], sqt[:, 0:32],
                                               op=ALU.subtract),
                     reads=("stat", "sqt"), writes=("stat",), tag=f"gv{s}_{tt}")
                em.s(lambda e: e.activation(stat[:, 32:64], stat[:, 32:64], AF.Ln,
                                            bias=c_bias[:, 1:2]),
                     reads=("stat",), writes=("stat",), tag=f"gl{s}_{tt}")
                em.s(lambda e: e.activation(stat[:, 32:64], stat[:, 32:64], AF.Exp,
                                            scale=-0.5),
                     reads=("stat",), writes=("stat",), tag=f"ge{s}_{tt}")
                em.v(lambda e: e.tensor_tensor(stat[:, 0:32], stat[:, 0:32], stat[:, 32:64],
                                               op=ALU.mult),
                     reads=("stat",), writes=("stat",), tag=f"gb{s}_{tt}")
                em.t(lambda e: e.transpose(PC[0][0:32, 0:128], stat[:, 32:64], c_i128f[:]),
                     reads=("stat", "c_i128f"), writes=("PC0",), tag=f"gta{s}_{tt}")
                em.v(lambda e: e.tensor_copy(abT[:, 0:128], PC[0][0:32, 0:128]),
                     reads=("PC0",), writes=("abT",), tag=f"gea{s}_{tt}")
                em.t(lambda e: e.transpose(PC[1][0:32, 0:128], stat[:, 0:32], c_i128f[:]),
                     reads=("stat", "c_i128f"), writes=("PC1",), tag=f"gtb{s}_{tt}")
                em.v(lambda e: e.tensor_copy(abT[:, 128:256], PC[1][0:32, 0:128]),
                     reads=("PC1",), writes=("abT",), tag=f"geb{s}_{tt}")
                for q4 in range(4):
                    em.t(lambda e, q4=q4: e.matmul(
                            PC[0][:, 0:512], abT[:, 0:128],
                            c_blockones[:, q4 * 512:(q4 + 1) * 512], start=True, stop=True),
                         reads=("abT", "c_blockones"), writes=("PC0",), tag=f"ba{s}_{tt}_{q4}")
                    em.t(lambda e, q4=q4: e.matmul(
                            PC[1][:, 0:512], abT[:, 128:256],
                            c_blockones[:, q4 * 512:(q4 + 1) * 512], start=True, stop=True),
                         reads=("abT", "c_blockones"), writes=("PC1",), tag=f"bb{s}_{tt}_{q4}")
                    em.v(lambda e, tt=tt, q4=q4: e.tensor_tensor(
                            sqt[:, q4 * 512:(q4 + 1) * 512],
                            y_sb[:, tt * C + q4 * 512: tt * C + (q4 + 1) * 512],
                            PC[0][:, 0:512], op=ALU.mult),
                         reads=(f"y:{tt}:{q4}", "PC0", "sqt"), writes=("sqt",),
                         tag=f"yn1{s}_{tt}_{q4}")
                    em.v(lambda e, q4=q4: e.tensor_tensor(
                            yn_tp[:, q4 * 512:(q4 + 1) * 512],
                            sqt[:, q4 * 512:(q4 + 1) * 512], PC[1][:, 0:512],
                            op=ALU.subtract),
                         reads=("sqt", "PC1"), writes=(f"yn:{q4}",), tag=f"yn2{s}_{tt}_{q4}")
                for half in range(2):
                    pt, ptk = (PT[half], f"PT{half}")
                    for j in range(8):
                        jt = half * 8 + j
                        em.t(lambda e, pt=pt, j=j, jt=jt: e.transpose(
                                pt[:, j * 128:(j + 1) * 128],
                                yn_tp[:, jt * 128:(jt + 1) * 128], c_i128[:]),
                             reads=(f"yn:{jt // 4}", "c_i128"), writes=(ptk,),
                             tag=f"zt{s}_{tt}_{jt}")
                    em.s(lambda e, pt=pt, half=half, tt=tt: e.copy(
                            xz_sb[:].rearrange("p (n t) -> p n t", t=TSA)[
                                :, half * 8:(half + 1) * 8, tt * 128:(tt + 1) * 128],
                            pt[:].rearrange("p (j n) -> p j n", n=128)),
                         reads=(ptk,),
                         writes=tuple(f"xz:{half * 8 + j}" for j in range(8)),
                         tag=f"ze{s}_{tt}_{half}")
            for jt in range(NT):
                em.v(lambda e, jt=jt: e.scalar_tensor_tensor(
                        out=xz_sb[:, jt * TSA:(jt + 1) * TSA],
                        in0=xz_sb[:, jt * TSA:(jt + 1) * TSA],
                        scalar=c_lnbt[:, jt:jt + 1],
                        in1=g2_sb[:, jt * TSA:(jt + 1) * TSA],
                        op0=ALU.add, op1=ALU.mult),
                     reads=(f"xz:{jt}", f"g2:{jt}", "c_lnbt"), writes=(f"xz:{jt}",),
                     tag=f"gz{s}_{jt}")

        def seg_wo_residual(s):
            t0 = s * TSA
            for tt in range(TSA // 128):
                b = tt % 2
                em.dma("qsync", xt[b][:], x_bf[t0 + tt * 128: t0 + (tt + 1) * 128, :],
                       reads=xbf_keys(t0 + tt * 128, t0 + (tt + 1) * 128),
                       writes=(f"xt:{b}",), semkey=f"xre{b}", tag=f"xr{s}_{tt}")
            for ot in range(NT):
                slot = ot % 4
                wk = stream_w(s, "o", ot, slot)
                pb, pbk = (PM[ot % 2], f"PM{ot % 2}")
                for tt in range(TSA // 128):
                    for kt in range(NT):
                        em.t(lambda e, pb=pb, slot=slot, kt=kt, tt=tt: e.matmul(
                                pb[:, tt * 128:(tt + 1) * 128],
                                xz_sb[:, kt * TSA + tt * 128: kt * TSA + tt * 128 + 128],
                                ws[slot][:, kt * 128:(kt + 1) * 128],
                                start=(kt == 0), stop=(kt == NT - 1)),
                             reads=(wk, f"xz:{kt}"), writes=(pbk,), tag=f"omm{s}_{tt}_{ot}_{kt}")
                for tt in range(TSA // 128):
                    b = tt % 2
                    em.v(lambda e, pb=pb, b=b, ot=ot, tt=tt: e.tensor_tensor(
                            xt[b][:, ot * 128:(ot + 1) * 128],
                            xt[b][:, ot * 128:(ot + 1) * 128],
                            pb[:, tt * 128:(tt + 1) * 128], op=ALU.add),
                         reads=(pbk, f"xt:{b}"), writes=(f"xt:{b}",), tag=f"x2{s}_{tt}_{ot}")
                drip_quant(1)
            for tt in range(TSA // 128):
                b = tt % 2
                em.dma("qpool", x2s[t0 + tt * 128: t0 + (tt + 1) * 128, :], xt[b][:],
                       reads=(f"xt:{b}",), writes=(f"x2s:{(t0 // 128) + tt}",),
                       semkey=f"x2o{b}", tag=f"x2st{s}_{tt}")

        for f in range(5):
            em.v(lambda e, f=f: e.memset(tanh_f[f][32:33, :], 1.0),
                 writes=(f"tanhf{f}",), tag=f"tones{f}")
        em.v(lambda e: e.memset(tanh2[64:65, :], 1.0), writes=("tanh2",), tag="t2ones")

        n_gj = len(gamma_jobs)
        n_qj = len(quant_jobs)
        g_per = (n_gj + max(1, NSA // 3) - 1) // max(1, NSA // 3)
        gamma_done = False
        quant_base = 0
        qseg0 = max(2, NSA // 3 + 1)
        q_segs = max(1, NSA - 1 - qseg0)
        q_per = (n_qj + q_segs - 1) // q_segs

        astage = 9
        for ch in parts:
            if ch.isdigit():
                astage = int(ch)
        for s in (range(NSA) if "A" in parts else []):
            if s < NSA // 3 or (not gamma_done and s == NSA // 3):
                lo = s * g_per
                emit_gamma_slice(gamma_jobs[lo:lo + g_per])
                if lo + g_per >= n_gj and not gamma_done:
                    emit_gamma_final()
                    gamma_done = True
            _qdrip["on"] = gamma_done
            stages = [seg_ln1, seg_maa, seg_attn_matmuls, seg_wkv, seg_gn_gate,
                      seg_wo_residual]
            for i, fn in enumerate(stages):
                if i < astage:
                    fn(s)
        if not gamma_done:
            emit_gamma_final()
            gamma_done = True
        _qdrip["on"] = True
        drip_quant(n_qj)

        em.fence({
            "tensor": (lambda e: e.matmul(PC[0][0:1, 0:1], c_ones1[0:1, 0:1],
                                          c_ones1[0:1, 0:1], start=True, stop=True),
                       ("PC0",)),
            "vector": (lambda e: e.memset(tiny[:, 0:1], 0.0), ("tiny",)),
            "scalar": (lambda e: e.copy(tiny[:, 1:2], c_ones1[:, 0:1]), ("tiny",)),
            "gpsimd": (lambda e: e.memset(tinyb[:, 0:1], 0.0), ("tinyb",)),
        }, "FENCE_AB")

        stackA.close()
        stackP.close()
        NBLK = max(1, T // 512)
        BW = min(512, T)
        BW2 = min(512, TQ)
        NJ = max(1, TQ // BW2)
        NT8 = TQ // 128

        def emit_phase_b():
            stackB0 = contextlib.ExitStack()
            stackB1 = contextlib.ExitStack()
            xn_sb = sb(stackB1, "xn_sb", [128, NT * T], BF16)
            qws = [sb(stackB1, f"qws{i}", [128, C], BF16) for i in range(3)]
            gf = [sb(stackB1, f"gf{i}", [128, T], BF16) for i in range(2)]
            x2t = [sb(stackB0, f"x2t{i}", [128, C], BF16) for i in range(2)]
            statb = sb(stackB0, "statb", [128, 16], F32)
            h2tp = sb(stackB0, "h2tp", [128, C], BF16)

            xn3 = xn_sb[:].rearrange("p (n t) -> p n t", t=T)
            for tt in range(T // 128):
                b = tt % 2
                xk = f"x2t:{b}"
                em.dma("qsync" if b == 0 else "qscalar", x2t[b][:],
                       x2s[tt * 128:(tt + 1) * 128, :],
                       reads=(f"x2s:{tt}",), writes=(xk,), semkey=f"x2b{b}",
                       tag=f"x2l{tt}")
                em.v(lambda e, b=b: e.tensor_reduce(
                        out=statb[:, 0:1], in_=x2t[b][:], axis=AX, op=ALU.add),
                     reads=(xk,), writes=("statb",), tag=f"br1{tt}")
                for cc in range(4):
                    pm, pmk = (PM[cc % 2], f"PM{cc % 2}")
                    em.s(lambda e, b=b, cc=cc, pm=pm: e.square(
                            pm[:, 0:512], x2t[b][:, cc * 512:(cc + 1) * 512]),
                         reads=(xk,), writes=(pmk,), tag=f"bsq{tt}_{cc}")
                    em.v(lambda e, cc=cc, pm=pm: e.tensor_reduce(
                            out=statb[:, 8 + cc:9 + cc], in_=pm[:, 0:512],
                            axis=AX, op=ALU.add),
                         reads=(pmk,), writes=("statb",), tag=f"br2{tt}_{cc}")
                em.v(lambda e: e.tensor_reduce(
                        out=statb[:, 1:2], in_=statb[:, 8:12], axis=AX, op=ALU.add),
                     reads=("statb",), writes=("statb",), tag=f"br2s{tt}")
                em.v(lambda e: e.tensor_scalar_mul(statb[:, 2:3], statb[:, 0:1], 1.0 / C),
                     reads=("statb",), writes=("statb",), tag=f"bmu{tt}")
                em.v(lambda e: e.tensor_tensor(statb[:, 3:4], statb[:, 2:3], statb[:, 2:3],
                                               op=ALU.mult),
                     reads=("statb",), writes=("statb",), tag=f"bm2{tt}")
                em.v(lambda e: e.tensor_scalar(
                        out=statb[:, 4:5], in0=statb[:, 1:2], scalar1=1.0 / C,
                        scalar2=statb[:, 3:4], op0=ALU.mult, op1=ALU.subtract),
                     reads=("statb",), writes=("statb",), tag=f"bv{tt}")
                em.s(lambda e: e.activation(statb[:, 5:6], statb[:, 4:5], AF.Ln,
                                            bias=c_bias[:, 0:1]),
                     reads=("statb",), writes=("statb",), tag=f"bl{tt}")
                em.s(lambda e: e.activation(statb[:, 6:7], statb[:, 5:6], AF.Exp, scale=-0.5),
                     reads=("statb",), writes=("statb",), tag=f"be{tt}")
                em.v(lambda e, b=b: e.tensor_scalar(
                        out=h2tp[:], in0=x2t[b][:], scalar1=statb[:, 2:3],
                        scalar2=statb[:, 6:7], op0=ALU.subtract, op1=ALU.mult),
                     reads=(xk, "statb"), writes=("h2tp",), tag=f"bh{tt}")
                for half in range(2):
                    pt, ptk = (PT[half], f"PT{half}")
                    for j in range(8):
                        jt = half * 8 + j
                        em.t(lambda e, pt=pt, j=j, jt=jt: e.transpose(
                                pt[:, j * 128:(j + 1) * 128],
                                h2tp[:, jt * 128:(jt + 1) * 128], c_i128[:]),
                             reads=("h2tp", "c_i128"), writes=(ptk,), tag=f"bt{tt}_{jt}")
                    em.s(lambda e, pt=pt, half=half, tt=tt: e.copy(
                            xn3[:, half * 8:(half + 1) * 8, tt * 128:(tt + 1) * 128],
                            pt[:].rearrange("p (j n) -> p j n", n=128)),
                         reads=(ptk,), writes=(f"xn:{half}",), tag=f"bte{tt}_{half}")

            def q12load(qi, f, sl):
                em.dma("qscalar" if sl == 1 else "qsync", qws[sl][:], scrQ12[qi, f],
                       reads=tuple(f"scrQ:{qi}:{f}:{c0}" for (c0, w) in col_chunks(C)),
                       writes=(f"qws:{sl}",), semkey=f"qw{sl}", tag=f"q{qi}l_{f}")

            q12load(0, 0, 0)
            q12load(1, 0, 2)

            em.fence({
                "tensor": (lambda e: e.matmul(PC[0][0:1, 0:1], c_ones1[0:1, 0:1],
                                              c_ones1[0:1, 0:1], start=True, stop=True),
                           ("PC0",)),
                "vector": (lambda e: e.memset(tiny[:, 0:1], 0.0), ("tiny",)),
                "scalar": (lambda e: e.copy(tiny[:, 1:2], c_ones1[:, 0:1]), ("tiny",)),
                "gpsimd": (lambda e: e.memset(tinyb[:, 0:1], 0.0), ("tinyb",)),
            }, "FENCE_B1")
            stackB0.close()
            stackPS.close()
            PB8 = [stackB1.enter_context(nc.psum_tensor(f"PB8_{i}", [128, BW], F32))
                   for i in range(NBLK)]

            NHALF = (NBLK + 1) // 2
            halves = [list(range(0, NHALF)), list(range(NHALF, NBLK))]
            for f in range(NF):
                s1 = f % 2
                gsl = f % 2
                if f > 0:
                    q12load(0, f, s1)
                    q12load(1, f, 2)
                for qi, sl in ((0, s1), (1, 2)):
                    for hblk in halves:
                        if not hblk:
                            continue
                        for kt in range(NT):
                            for blk in hblk:
                                em.t(lambda e, sl=sl, kt=kt, blk=blk: e.matmul(
                                        PB8[blk][:, 0:BW],
                                        qws[sl][:, kt * 128:(kt + 1) * 128],
                                        xn_sb[:, kt * T + blk * BW: kt * T + (blk + 1) * BW],
                                        start=(kt == 0), stop=(kt == NT - 1)),
                                     reads=(f"qws:{sl}", f"xn:{kt // 8}"),
                                     writes=(f"PB8_{blk}",), tag=f"b1m{qi}_{f}_{kt}_{blk}")
                        if qi == 0:
                            for blk in hblk:
                                em.s(lambda e, gsl=gsl, blk=blk: e.activation(
                                        gf[gsl][:, blk * BW:(blk + 1) * BW],
                                        PB8[blk][:, 0:BW], AF.Tanh, scale=0.5),
                                     reads=(f"PB8_{blk}",), writes=(f"gf:{gsl}",),
                                     tag=f"b1t{f}_{blk}")
                                em.s(lambda e, gsl=gsl, blk=blk: e.activation(
                                        gf[gsl][:, blk * BW:(blk + 1) * BW],
                                        gf[gsl][:, blk * BW:(blk + 1) * BW],
                                        AF.Copy, scale=0.5, bias=0.5),
                                     reads=(f"gf:{gsl}",), writes=(f"gf:{gsl}",),
                                     tag=f"b1c{f}_{blk}")
                        else:
                            for blk in hblk:
                                em.v(lambda e, gsl=gsl, blk=blk: e.tensor_tensor(
                                        gf[gsl][:, blk * BW:(blk + 1) * BW],
                                        gf[gsl][:, blk * BW:(blk + 1) * BW],
                                        PB8[blk][:, 0:BW], op=ALU.mult),
                                     reads=(f"PB8_{blk}", f"gf:{gsl}"),
                                     writes=(f"gf:{gsl}",), tag=f"b1g{f}_{blk}")
                for qh in range(4):
                    em.dma("qpool", scrG[qh, f], gf[gsl][:, qh * TQ:(qh + 1) * TQ],
                           reads=(f"gf:{gsl}",), writes=(f"scrG:{f}:{qh}",),
                           semkey=f"gfo{gsl}", tag=f"gfo{f}_{qh}")

            em.fence({
                "tensor": (lambda e: e.matmul(PB8[0][0:1, 0:1], c_ones1[0:1, 0:1],
                                              c_ones1[0:1, 0:1], start=True, stop=True),
                           ("PB8_0",)),
                "vector": (lambda e: e.memset(tiny[:, 0:1], 0.0), ("tiny",)),
                "scalar": (lambda e: e.copy(tiny[:, 1:2], c_ones1[:, 0:1]), ("tiny",)),
                "gpsimd": (lambda e: e.memset(tinyb[:, 0:1], 0.0), ("tinyb",)),
            }, "FENCE_B2")
            stackB1.close()

            gh = sb(stackB, "gh", [128, NF * TQ], BF16)
            outq = sb(stackB, "outq", [128, NT * TQ], BF16)
            q3ws = [sb(stackB, f"q3ws{i}", [128, FF], BF16) for i in range(2)]
            x2u = [sb(stackB, f"x2u{i}", [128, C], BF16) for i in range(2)]
            yout = [sb(stackB, f"yout{i}", [128, C], F32) for i in range(2)]
            PSB2 = [stackB.enter_context(nc.psum_tensor(f"PS2_{i}", [128, BW2], F32))
                    for i in range(2 * NJ)]
            PTB = [stackB.enter_context(nc.psum_tensor(f"PTB{i}", [128, 1024], BF16))
                   for i in range(2)]

            for qh in range(4):
                for f in range(NF):
                    em.dma("qsync", gh[:, f * TQ:(f + 1) * TQ], scrG[qh, f],
                           reads=(), writes=("ghq",), semkey="ghl",
                           tag=f"ghl{qh}_{f}")
                for ot in range(NT):
                    sl = ot % 2
                    em.dma("qscalar", q3ws[sl][:], scrQ3[ot],
                           reads=tuple(f"scrQ3:{ot}:{c0}" for (c0, w) in col_chunks(FF)),
                           writes=(f"q3ws:{sl}",), semkey=f"q3w{sl}",
                           tag=f"q3l{qh}_{ot}")
                    bs = 0 if ot % 2 == 0 else NJ
                    for kt in range(NF):
                        for j in range(NJ):
                            em.t(lambda e, sl=sl, kt=kt, j=j, bs=bs: e.matmul(
                                    PSB2[bs + j][:, 0:BW2],
                                    q3ws[sl][:, kt * 128:(kt + 1) * 128],
                                    gh[:, kt * TQ + j * BW2: kt * TQ + (j + 1) * BW2],
                                    start=(kt == 0), stop=(kt == NF - 1)),
                                 reads=(f"q3ws:{sl}", "ghq"),
                                 writes=(f"PS2_{bs + j}",), tag=f"b2m{qh}_{ot}_{kt}_{j}")
                    for j in range(NJ):
                        em.v(lambda e, ot=ot, j=j, bs=bs: e.tensor_copy(
                                outq[:, ot * TQ + j * BW2: ot * TQ + (j + 1) * BW2],
                                PSB2[bs + j][:, 0:BW2]),
                             reads=(f"PS2_{bs + j}",), writes=(f"outq:{ot}",),
                             tag=f"b2e{qh}_{ot}_{j}")
                for t8 in range(NT8):
                    gt = qh * NT8 + t8
                    b = t8 % 2
                    em.dma("qsync" if b == 0 else "qscalar", x2u[b][:],
                           x2s[gt * 128:(gt + 1) * 128, :],
                           reads=(f"x2s:{gt}",), writes=(f"x2u:{b}",),
                           semkey=f"x2u{b}", tag=f"x2u{qh}_{t8}")
                    for half in range(2):
                        ptb, ptk = (PTB[half], f"PTB{half}")
                        for j in range(8):
                            ot = half * 8 + j
                            em.t(lambda e, ptb=ptb, j=j, ot=ot, t8=t8: e.transpose(
                                    ptb[:, j * 128:(j + 1) * 128],
                                    outq[:, ot * TQ + t8 * 128: ot * TQ + t8 * 128 + 128],
                                    c_i128[:]),
                                 reads=(f"outq:{ot}", "c_i128"), writes=(ptk,),
                                 tag=f"b2t{qh}_{t8}_{ot}")
                        em.v(lambda e, ptb=ptb, b=b, half=half: e.tensor_tensor(
                                yout[b][:, half * 1024:(half + 1) * 1024],
                                x2u[b][:, half * 1024:(half + 1) * 1024],
                                ptb[:, 0:1024], op=ALU.add),
                             reads=(ptk, f"x2u:{b}"), writes=(f"yout:{b}",),
                             tag=f"b2y{qh}_{t8}_{half}")
                    em.dma("qpool", y_out[gt * 128:(gt + 1) * 128, :], yout[b][:],
                           reads=(f"yout:{b}",), writes=(), semkey=f"yo{b}",
                           tag=f"yst{qh}_{t8}")

        with stackB:
            if "B" in parts:
                emit_phase_b()

            em.resolve()
            sems = {}
            for st_ in em.sem_streams():
                sems[st_] = stackG.enter_context(
                    nc.semaphore("sem_" + st_.replace(":", "_")))
            run = em.make_runner(sems)
            with nc.Block() as block:
                @block.tensor
                def _(e):
                    run(e, "tensor")

                @block.vector
                def _(e):
                    run(e, "vector")

                @block.scalar
                def _(e):
                    run(e, "scalar")

                @block.gpsimd
                def _(e):
                    run(e, "gpsimd")
                    if "B" in parts:
                        em.final_dma_waits(e, sems, [k for k in ("yo0", "yo1")
                                                     if "dma:" + k in em.dma_semkeys])
                    else:
                        em.final_dma_waits(e, sems, None)

                @block.sync
                def _(e):
                    run(e, "sync")
    return nc, em


def _consts():
    ii, jj = np.meshgrid(np.arange(64), np.arange(64), indexing="ij")
    tri_ex = np.vstack([(ii < jj)] * 2).astype(np.float32)
    tri_in = np.vstack([(ii <= jj)] * 2).astype(np.float32)
    tri_suf = np.vstack([(ii > jj)] * 2).astype(np.float32)
    i128 = np.eye(128, dtype=bfdt)
    i128f = np.eye(128, dtype=np.float32)
    triu2 = np.vstack([np.hstack([(ii < jj)] * 2)] * 2).astype(bfdt)
    tri4 = np.vstack([np.hstack([(ii < jj)] * 4)] * 2).astype(bfdt)
    onesblk = np.zeros((128, NT * 32), dtype=bfdt)
    for jt in range(NT):
        onesblk[0:64, jt * 32 + 2 * jt] = 1
        onesblk[64:128, jt * 32 + 2 * jt + 1] = 1
    blockones = np.zeros((32, C), dtype=bfdt)
    for h in range(32):
        blockones[h, h * 64:(h + 1) * 64] = 1
    return dict(tri_ex=tri_ex, tri_in=tri_in, tri_suf=tri_suf, i128=i128,
                i128f=i128f, triu2=triu2, tri4=tri4, onesblk=onesblk, blockones=blockones)


def make_inmap(xb, P, T):
    cs = _consts()
    u_flat = np.ascontiguousarray(P["time_faaaa"], dtype=np.float32).reshape(C)
    u_bc = np.repeat(u_flat.reshape(NT, 128).T[:, :, None], 64, axis=2).reshape(128, NT * 64)
    s0 = np.zeros((128, NT * 64), dtype=np.float32)
    ts = np.asarray(P["time_state"], dtype=np.float32)
    for jt in range(NT):
        s0[0:64, jt * 64:(jt + 1) * 64] = ts[2 * jt]
        s0[64:128, jt * 64:(jt + 1) * 64] = ts[2 * jt + 1]
    d = {
        "x": np.ascontiguousarray(xb, dtype=np.float32),
        "maa_w1r": np.ascontiguousarray(
            np.asarray(P["time_maa_w1"], dtype=np.float32).reshape(NT, 128, 160)
            .transpose(1, 0, 2).reshape(128, NT * 160)),
        "dw1r": np.ascontiguousarray(
            np.asarray(P["time_decay_w1"], dtype=np.float32).reshape(NT, 128, 64)
            .transpose(1, 0, 2).reshape(128, NT * 64)),
        "dw2": np.asarray(P["time_decay_w2"], dtype=np.float32),
        "tdec": np.asarray(P["time_decay"], dtype=np.float32),
        "u_bc": np.ascontiguousarray(u_bc, dtype=np.float32),
        "s0": s0,
        "lnw_chan": np.ascontiguousarray(
            np.asarray(P["ln_w"], dtype=np.float32).reshape(NT, 128).T),
        "maax_chan": np.ascontiguousarray(
            np.asarray(P["time_maa_x"], dtype=np.float32).reshape(NT, 128).T),
        "lnb_chan": np.ascontiguousarray(
            np.asarray(P["ln_b"], dtype=np.float32).reshape(NT, 128).T),
        "wf1": np.asarray(P["ffn_w1"], dtype=np.float32),
        "wf2": np.asarray(P["ffn_w2"], dtype=np.float32),
        "wf3": np.asarray(P["ffn_w3"], dtype=np.float32),
    }
    for m in "rkvgo":
        d["W" + m] = np.asarray(P["W_" + m], dtype=np.float32)
    w2 = np.asarray(P["time_maa_w2"], dtype=np.float32)
    for f in range(5):
        d[f"w2f{f}"] = np.ascontiguousarray(w2[f])
    for f, nm in zip("xwkvrg", ["time_maa_x", "time_maa_w", "time_maa_k",
                                "time_maa_v", "time_maa_r", "time_maa_g"]):
        d["maa_" + f] = np.asarray(P[nm], dtype=np.float32)
    d.update(cs)
    return d


_NC_CACHE = {}


def kernel(L0_attn_x, L0_attn_time_maa_x, L0_attn_time_maa_w, L0_attn_time_maa_k,
           L0_attn_time_maa_v, L0_attn_time_maa_r, L0_attn_time_maa_g,
           L0_attn_time_maa_w1, L0_attn_time_maa_w2, L0_attn_time_decay,
           L0_attn_time_decay_w1, L0_attn_time_decay_w2, L0_attn_time_faaaa,
           L0_attn_time_state, L0_attn_W_r, L0_attn_W_k, L0_attn_W_v,
           L0_attn_W_g, L0_attn_W_o, L0_attn_ln_w, L0_attn_ln_b,
           L0_ffn_w1, L0_ffn_w2, L0_ffn_w3):
    P = dict(
        time_maa_x=L0_attn_time_maa_x, time_maa_w=L0_attn_time_maa_w,
        time_maa_k=L0_attn_time_maa_k, time_maa_v=L0_attn_time_maa_v,
        time_maa_r=L0_attn_time_maa_r, time_maa_g=L0_attn_time_maa_g,
        time_maa_w1=L0_attn_time_maa_w1, time_maa_w2=L0_attn_time_maa_w2,
        time_decay=L0_attn_time_decay, time_decay_w1=L0_attn_time_decay_w1,
        time_decay_w2=L0_attn_time_decay_w2, time_faaaa=L0_attn_time_faaaa,
        time_state=L0_attn_time_state, W_r=L0_attn_W_r, W_k=L0_attn_W_k,
        W_v=L0_attn_W_v, W_g=L0_attn_W_g, W_o=L0_attn_W_o,
        ln_w=L0_attn_ln_w, ln_b=L0_attn_ln_b,
        ffn_w1=L0_ffn_w1, ffn_w2=L0_ffn_w2, ffn_w3=L0_ffn_w3)
    x = np.asarray(L0_attn_x, dtype=np.float32)
    T = x.shape[1]
    if T not in _NC_CACHE:
        _NC_CACHE[T] = build_program(T)
    nc, em = _NC_CACHE[T]
    in_maps = [make_inmap(x[i], P, T) for i in range(x.shape[0])]
    res = run_bass_kernel_spmd(nc, in_maps, core_ids=list(range(x.shape[0])))
    out = np.stack([np.asarray(r["y"], dtype=np.float32) for r in res.results], axis=0)
    return out


# version 20
# speedup vs baseline: 1.1790x; 1.0931x over previous
import math
import numpy as np
import ml_dtypes
from collections import defaultdict

import concourse.bass as bass
import concourse.mybir as mybir
from concourse.bass_utils import run_bass_kernel_spmd

F32 = mybir.dt.float32
BF16 = mybir.dt.bfloat16
AF = mybir.ActivationFunctionType
ALU = mybir.AluOpType
AX = mybir.AxisListType.X
bfdt = ml_dtypes.bfloat16

B, TFULL, C = 8, 4096, 2048
H, N, L = 32, 64, 64
NT = C // 128
FF = 5632
NF = FF // 128
TSA = 256
TSB = 512
GN_EPS = 1e-5 * 64.0
LN_EPS = 1e-5
Q_EPS = 1e-8

COMPUTE = ("tensor", "vector", "scalar", "gpsimd")
DMAQ = {"qsync": "sync", "qscalar": "scalar", "qpool": "gpsimd",
        "qvector": "vector"}
PHYS = ("tensor", "vector", "scalar", "gpsimd", "sync")


class _Op:
    __slots__ = ("stream", "phys", "fn", "reads", "writes", "idx", "waits",
                 "milestone", "drain_before", "tag", "_deps", "stream_pos",
                 "is_dma")

    def __init__(self, stream, phys, fn, reads, writes, idx, tag, is_dma):
        self.stream = stream
        self.phys = phys
        self.fn = fn
        self.reads = reads
        self.writes = writes
        self.idx = idx
        self.waits = {}
        self.milestone = False
        self.drain_before = False
        self.tag = tag
        self.is_dma = is_dma


class Emitter:
    def __init__(self):
        self.ops = []
        self.last_writer = {}
        self.readers = defaultdict(list)
        self.dma_semkeys = []
        self.batch_keys = set()
        self.semkey_queue = {}
        self.fence_keys = ()

    def _record(self, o):
        o.reads = list(o.reads) + list(self.fence_keys)
        self.ops.append(o)
        deps = []
        for k in o.reads:
            w = self.last_writer.get(k)
            if w is not None:
                deps.append(w)
        for k in o.writes:
            w = self.last_writer.get(k)
            if w is not None:
                deps.append(w)
            deps.extend(self.readers.get(k, ()))
        o._deps = [d for d in deps if d is not o]
        for k in o.reads:
            self.readers[k].append(o)
        for k in o.writes:
            self.last_writer[k] = o
            self.readers[k] = []
        return o

    def op(self, engine, fn, reads=(), writes=(), tag=""):
        assert engine in COMPUTE, engine
        return self._record(_Op(engine, engine, fn, list(reads), list(writes),
                                len(self.ops), tag, False))

    def t(self, fn, reads=(), writes=(), tag=""):
        return self.op("tensor", fn, reads, writes, tag)

    def v(self, fn, reads=(), writes=(), tag=""):
        return self.op("vector", fn, reads, writes, tag)

    def s(self, fn, reads=(), writes=(), tag=""):
        return self.op("scalar", fn, reads, writes, tag)

    def g(self, fn, reads=(), writes=(), tag=""):
        return self.op("gpsimd", fn, reads, writes, tag)

    def dma(self, queue, out, in_, reads=(), writes=(), semkey=None, batch=False,
            tag="", **kw):
        assert queue in DMAQ, queue
        if semkey is None:
            semkey = "dma:" + (writes[0] if writes else (reads[0] if reads else tag))
        else:
            semkey = "dma:" + semkey
        if semkey not in self.dma_semkeys:
            self.dma_semkeys.append(semkey)
            self.semkey_queue[semkey] = queue
        assert self.semkey_queue[semkey] == queue, (semkey, queue)
        if batch:
            self.batch_keys.add(semkey)

        def fn(eng):
            return eng.dma_start(out=out, in_=in_, **kw)
        return self._record(_Op(semkey, DMAQ[queue], fn, list(reads), list(writes),
                                len(self.ops), tag, True))

    def fence(self, tiny_ops, key):
        self._fence_marks = getattr(self, "_fence_marks", [])
        marks = []
        for eng, (fn, extra) in tiny_ops.items():
            o = self.op(eng, fn, reads=(), writes=(key + ":" + eng,) + tuple(extra),
                        tag="fence:" + eng)
            o.drain_before = True
            marks.append(o)
        self._fence_marks.append((key, [m.idx for m in marks], len(self.ops)))
        self.fence_keys = tuple(key + ":" + e for e in tiny_ops)

    def resolve(self):
        per_stream = defaultdict(list)
        for o in self.ops:
            per_stream[o.stream].append(o)
            o.stream_pos = len(per_stream[o.stream]) - 1

        for key, mark_idxs, _pt in getattr(self, "_fence_marks", []):
            for mi in mark_idxs:
                mark = self.ops[mi]
                for sk in self.dma_semkeys:
                    last = None
                    for o in per_stream[sk]:
                        if o.idx < mi:
                            last = o
                        else:
                            break
                    if last is not None:
                        mark._deps.append(last)

        know = {p: defaultdict(lambda: -1) for p in PHYS}
        drained_upto = {p: -1 for p in PHYS}
        last_compute = {p: -1 for p in PHYS}
        needed = defaultdict(set)

        for o in self.ops:
            kn = know[o.phys]
            want = defaultdict(lambda: -1)
            drain = o.drain_before
            for d in o._deps:
                if (not d.is_dma) and d.phys == o.phys:
                    if o.phys == "tensor":
                        continue
                    if d.stream_pos > drained_upto[o.phys]:
                        drain = True
                else:
                    if o.is_dma and d.is_dma and d.stream == o.stream:
                        continue
                    if d.stream_pos > kn[d.stream]:
                        want[d.stream] = max(want[d.stream], d.stream_pos)
            if drain:
                o.drain_before = True
                drained_upto[o.phys] = last_compute[o.phys]
            for ps, pos in want.items():
                needed[ps].add(pos)
                o.waits[ps] = pos
                kn[ps] = max(kn[ps], pos)
            if not o.is_dma:
                last_compute[o.phys] = o.stream_pos

        self.milestone_ord = {}
        self.milestone_count = {}
        for stream, lst in per_stream.items():
            if stream.startswith("dma:"):
                for o in lst:
                    o.milestone = True
                    self.milestone_ord[(stream, o.stream_pos)] = o.stream_pos + 1
                self.milestone_count[stream] = len(lst)
            else:
                poss = sorted(needed.get(stream, ()))
                for n, pos in enumerate(poss):
                    lst[pos].milestone = True
                    self.milestone_ord[(stream, pos)] = n + 1
                self.milestone_count[stream] = len(poss)
        for o in self.ops:
            o.waits = {ps: (self.milestone_count[ps] if ps in self.batch_keys
                            else self.milestone_ord[(ps, pos)])
                       for ps, pos in o.waits.items()}
        self.per_stream = per_stream
        self.stats = dict(
            ops=len(self.ops),
            drains=sum(1 for o in self.ops if o.drain_before),
            waits=sum(len(o.waits) for o in self.ops),
            milestones=sum(1 for o in self.ops if o.milestone and not o.is_dma),
            dma_sems=len(self.dma_semkeys))
        return self

    def sem_streams(self):
        return [s for s in COMPUTE if self.milestone_count.get(s, 0) > 0] + \
            list(self.dma_semkeys)

    def make_runner(self, sems):
        by_phys = defaultdict(list)
        for o in self.ops:
            by_phys[o.phys].append(o)

        def run(eng, phys):
            for o in by_phys.get(phys, ()):
                if o.drain_before:
                    eng.drain()
                for ps, ordinal in sorted(o.waits.items()):
                    mult = 16 if ps.startswith("dma:") else 1
                    eng.wait_ge(sems[ps], ordinal * mult)
                inst = o.fn(eng)
                if o.milestone:
                    mult = 16 if o.is_dma else 1
                    if inst is None:
                        raise RuntimeError(f"milestone op '{o.tag}' has no instruction")
                    inst.then_inc(sems[o.stream], mult)
        return run

    def final_dma_waits(self, eng, sems, keys=None):
        ks = (["dma:" + k for k in keys] if keys is not None else list(self.dma_semkeys))
        for k in ks:
            eng.wait_ge(sems[k], self.milestone_count[k] * 16)


def build_program(T, parts="PAB"):
    NSA = T // TSA
    NSB = T // TSB
    NCC = TSA // L

    nc = bass.Bass("TRN2", target_bir_lowering=False)

    def din(name, shape, dt=F32):
        return nc.dram_tensor(name, list(shape), dt, kind="ExternalInput").ap()

    x_in = din("x", [T, C])
    W_attn = {m: din("W" + m, [C, C]) for m in "rkvgo"}
    maa_w1r = din("maa_w1r", [128, NT * 160])
    w2f = [din(f"w2f{f}", [32, C]) for f in range(5)]
    maa_vec = {f: din("maa_" + f, [C]) for f in "xwkvrg"}
    dw1r = din("dw1r", [128, NT * 64])
    dw2 = din("dw2", [64, C])
    tdec = din("tdec", [C])
    u_bc_in = din("u_bc", [128, NT * 64])
    s0_in = din("s0", [128, NT * 64])
    lnw_in = din("lnw_chan", [128, NT])
    maax_chan = din("maax_chan", [128, NT])
    lnb_in = din("lnb_chan", [128, NT])
    wf1 = din("wf1", [FF, C])
    wf2 = din("wf2", [FF, C])
    wf3 = din("wf3", [C, FF])
    tri_ex = din("tri_ex", [128, 64])
    tri_in = din("tri_in", [128, 64])
    tri_suf = din("tri_suf", [128, 64])
    i128 = din("i128", [128, 128], BF16)
    i128f = din("i128f", [128, 128], F32)
    triu2 = din("triu2", [128, 128], BF16)
    tri4 = din("tri4", [128, 256], BF16)
    onesblk = din("onesblk", [128, NT * 32], BF16)
    blockones = din("blockones", [32, C], BF16)

    y_out = nc.dram_tensor("y", [T, C], F32, kind="ExternalOutput").ap()

    scrA = nc.dram_tensor("scrA", [5, NT, 128, C], BF16, kind="Internal").ap()
    scrQ12 = nc.dram_tensor("scrQ12", [2, NF, 128, C], BF16, kind="Internal").ap()
    scrQ3 = nc.dram_tensor("scrQ3", [NT, 128, FF], BF16, kind="Internal").ap()
    x2s = nc.dram_tensor("x2s", [T, C], BF16, kind="Internal").ap()
    x_bf = nc.dram_tensor("x_bf", [T, C], BF16, kind="Internal").ap()
    TQ = T // 4
    scrG = nc.dram_tensor("scrG", [4, NF, 128, TQ], BF16, kind="Internal").ap()

    em = Emitter()
    import contextlib
    stackG = contextlib.ExitStack()
    stackA = contextlib.ExitStack()
    stackB = contextlib.ExitStack()

    def sb(stack, name, shape, dt):
        return stack.enter_context(nc.sbuf_tensor(name, list(shape), dt))

    with stackG:
        c_triex = sb(stackG, "c_triex", [128, 64], F32)
        c_triin = sb(stackG, "c_triin", [128, 64], F32)
        c_trisuf = sb(stackG, "c_trisuf", [128, 64], F32)
        c_i128 = sb(stackG, "c_i128", [128, 128], BF16)
        c_i128f = sb(stackG, "c_i128f", [128, 128], F32)
        c_triu2 = sb(stackG, "c_triu2", [128, 128], BF16)
        c_tri4 = sb(stackG, "c_tri4", [128, 256], BF16)
        c_onesblk = sb(stackG, "c_onesblk", [128, NT * 32], BF16)
        c_blockones = sb(stackG, "c_blockones", [32, C], BF16)
        c_lnw = sb(stackG, "c_lnw", [128, NT], F32)
        c_lnb = sb(stackG, "c_lnb", [128, NT], F32)
        c_lnbt = sb(stackG, "c_lnbt", [128, NT], F32)
        c_lnw_half = sb(stackG, "c_lnw_half", [128, NT], F32)
        c_maa_x = sb(stackG, "c_maa_x", [128, NT], F32)
        S_sb = sb(stackG, "S_sb", [128, NT * 64], F32)
        Sbf = sb(stackG, "Sbf", [128, NT * 64], BF16)
        h_sb = sb(stackG, "h_sb", [128, NT * TSA], BF16)
        hc_sb = sb(stackG, "hc_sb", [128, NT], BF16)
        gacc = sb(stackG, "gacc", [128, 4], F32)
        ginv = sb(stackG, "ginv", [128, 4], F32)
        tiny = sb(stackG, "tiny", [128, 8], F32)
        tinyb = sb(stackG, "tinyb", [128, 8], BF16)
        c_ones1 = sb(stackG, "c_ones1", [128, 1], F32)
        c_onesrow = sb(stackG, "c_onesrow", [1, 128], F32)
        c_bias = sb(stackG, "c_bias", [128, 4], F32)
        stackP = contextlib.ExitStack()
        PW = 512
        prep_in = [sb(stackP, f"prep_in{i}", [128, PW], F32) for i in range(3)]
        prep_bf = [sb(stackP, f"prep_bf{i}", [128, PW], BF16) for i in range(2)]
        prep_out = [sb(stackP, f"prep_out{i}", [128, PW], BF16) for i in range(2)]
        prep_t = [sb(stackP, f"prep_t{i}", [128, PW], F32) for i in range(1)]

        stackPS = contextlib.ExitStack()
        PM = [stackPS.enter_context(nc.psum_tensor(f"PM{i}", [128, 512], F32)) for i in range(2)]
        PC = [stackPS.enter_context(nc.psum_tensor(f"PC{i}", [128, 512], F32)) for i in range(2)]
        PT = [stackPS.enter_context(nc.psum_tensor(f"PT{i}", [128, 1024], BF16)) for i in range(2)]
        PA = stackPS.enter_context(nc.psum_tensor("PA", [128, 512], F32))
        PB = stackPS.enter_context(nc.psum_tensor("PB", [128, 512], F32))

        def ld(dst, src, tag, queue="qsync"):
            em.dma(queue, dst, src, reads=(), writes=(tag,), semkey="init",
                   batch=True, tag="ld_" + tag)

        ld(c_triex[:], tri_ex[:], "c_triex")
        ld(c_triin[:], tri_in[:], "c_triin")
        ld(c_trisuf[:], tri_suf[:], "c_trisuf")
        ld(c_i128[:], i128[:], "c_i128")
        ld(c_i128f[:], i128f[:], "c_i128f")
        ld(c_triu2[:], triu2[:], "c_triu2")
        ld(c_tri4[:], tri4[:], "c_tri4")
        ld(c_onesblk[:], onesblk[:], "c_onesblk")
        ld(c_blockones[:], blockones[:], "c_blockones")
        ld(c_lnw[:], lnw_in[:], "c_lnw")
        ld(c_lnb[:], lnb_in[:], "c_lnb")
        em.dma("qsync", S_sb[:], s0_in[:], reads=(),
               writes=tuple(f"S:{j}" for j in range(NT)), semkey="init",
               batch=True, tag="ld_S")
        em.dma("qsync", c_maa_x[:], maax_chan[:],
               writes=("c_maa_x",), semkey="init", batch=True, tag="ld_maax")

        em.v(lambda e: e.tensor_copy(Sbf[:], S_sb[:]),
             reads=tuple(f"S:{j}" for j in range(NT)),
             writes=tuple(f"Sbf:{j}" for j in range(NT)), tag="Sbf0")
        em.v(lambda e: e.reciprocal(c_lnbt[:], c_lnw[:]), reads=("c_lnw",), writes=("c_lnbt",), tag="rlnw")
        em.v(lambda e: e.tensor_tensor(c_lnbt[:], c_lnbt[:], c_lnb[:], op=ALU.mult),
             reads=("c_lnb", "c_lnbt"), writes=("c_lnbt",), tag="lnbt")
        em.v(lambda e: e.tensor_scalar_mul(c_lnw_half[:], c_lnw[:], 0.5),
             reads=("c_lnw",), writes=("c_lnw_half",), tag="lnwh")
        em.v(lambda e: e.memset(c_ones1[:], 1.0), writes=("c_ones1",), tag="ones1")
        for col, val in [(0, LN_EPS), (1, GN_EPS), (2, 0.5), (3, 1.0)]:
            em.v(lambda e, col=col, val=val: e.memset(c_bias[:, col:col + 1], val),
                 writes=("c_bias",), tag=f"bias{col}")
        em.v(lambda e: e.memset(c_onesrow[:], 1.0), writes=("c_onesrow",), tag="onesr")
        em.v(lambda e: e.memset(hc_sb[:], 0.0), writes=("hc",), tag="hcz")

        MATS = "rkvgo"

        def attn_prepass():
            pc = 0
            for m in "kvrgo":
                mi = MATS.index(m)
                for ot in range(NT):
                    for piece in range(C // PW):
                        sl = pc % 3
                        pc += 1
                        ikey = f"prepin:{sl}"
                        osl = pc % 2
                        bkey = f"prepbf:{osl}"
                        okey = f"prepout:{osl}"
                        em.dma("qscalar", prep_in[sl][:],
                               W_attn[m][ot * 128:(ot + 1) * 128,
                                         piece * PW:(piece + 1) * PW],
                               reads=(), writes=(ikey,), semkey=f"pa{sl}",
                               tag=f"wp_{m}{ot}_{piece}")
                        em.v(lambda e, sl=sl, osl=osl: e.tensor_copy(
                                prep_bf[osl][:], prep_in[sl][:]),
                             reads=(ikey,), writes=(bkey,), tag=f"wc_{m}{ot}_{piece}")
                        pt = PT[osl]
                        ptk = f"PT{osl}"
                        for j in range(PW // 128):
                            kt = piece * (PW // 128) + j
                            em.t(lambda e, pt=pt, j=j, osl=osl, kt=kt, piece=piece: e.transpose(
                                    pt[:, j * 128:(j + 1) * 128],
                                    prep_bf[osl][:, j * 128:(j + 1) * 128], c_i128[:]),
                                 reads=(bkey, "c_i128"), writes=(ptk,),
                                 tag=f"wt_{m}{ot}_{piece}_{j}")
                        em.s(lambda e, pt=pt, osl=osl: e.copy(prep_out[osl][:], pt[:, 0:PW]),
                             reads=(ptk,), writes=(okey,), tag=f"we_{m}{ot}_{piece}")
                        if m == "o":
                            for j in range(PW // 128):
                                kt = piece * (PW // 128) + j
                                em.v(lambda e, osl=osl, j=j, kt=kt: e.tensor_scalar_mul(
                                        prep_out[osl][:, j * 128:(j + 1) * 128],
                                        prep_out[osl][:, j * 128:(j + 1) * 128],
                                        c_lnw_half[:, kt:kt + 1]),
                                     reads=(okey, "c_lnw_half"), writes=(okey,),
                                     tag=f"wf_{ot}_{piece}_{j}")
                        em.dma("qsync", scrA[mi, ot, :, piece * PW:(piece + 1) * PW],
                               prep_out[osl][:],
                               reads=(okey,), writes=(f"scrA:{m}:{ot}",),
                               semkey=f"ps{osl}", tag=f"ws_{m}{ot}_{piece}")


        ffn_rows = [("wf1", wf1, FF // 128, 0), ("wf2", wf2, FF // 128, 1),
                    ("wf3", wf3, C // 128, 2)]

        def col_chunks(ncols):
            out, c0 = [], 0
            while c0 < ncols:
                w = min(PW, ncols - c0)
                out.append((c0, w))
                c0 += w
            return out

        def gamma_tile(qi, ap, row, c0, w, sl):
            key = f"prepin:{sl}"
            em.dma("qscalar" if sl == 1 else "qsync", prep_in[sl][:, 0:w],
                   ap[row * 128:(row + 1) * 128, c0:c0 + w],
                   reads=(), writes=(key,), semkey=f"pl{sl}", tag=f"gl{qi}_{row}_{c0}")
            em.v(lambda e, sl=sl, w=w: e.tensor_reduce(
                    out=tiny[:, 0:1], in_=prep_in[sl][:, 0:w], axis=AX, op=ALU.add,
                    apply_absolute_value=True),
                 reads=(key,), writes=("tiny",), tag=f"gr{qi}_{row}_{c0}")
            em.v(lambda e, qi=qi: e.tensor_tensor(
                    gacc[:, qi:qi + 1], gacc[:, qi:qi + 1], tiny[:, 0:1], op=ALU.add),
                 reads=("tiny", "gacc"), writes=("gacc",), tag=f"ga{qi}_{row}_{c0}")

        em.v(lambda e: e.memset(gacc[:], 0.0), writes=("gacc",), tag="gacc0")
        gamma_jobs = []
        for qi, (nm, ap, nrows, gi) in enumerate(ffn_rows):
            for row in range(nrows):
                for (c0, w) in col_chunks(ap.shape[1]):
                    gamma_jobs.append((qi, ap, row, c0, w))

        _gct = [0]

        def emit_gamma_slice(jobs):
            for (qi, ap, row, c0, w) in jobs:
                gamma_tile(qi, ap, row, c0, w, sl=_gct[0] % 3)
                _gct[0] += 1

        def emit_gamma_final():
            em.t(lambda e: e.matmul(PC[0][0:1, 0:4], c_ones1[:], gacc[:, 0:4],
                                    start=True, stop=True),
                 reads=("gacc", "c_ones1"), writes=("PC0",), tag="gsum")
            em.v(lambda e: e.tensor_copy(tiny[0:1, 0:4], PC[0][0:1, 0:4]),
                 reads=("PC0",), writes=("tiny",), tag="gtot")
            for qi, (nm, ap, nrows, gi) in enumerate(ffn_rows):
                numel = float(ap.shape[0] * ap.shape[1])
                em.v(lambda e, qi=qi, numel=numel: e.tensor_scalar(
                        out=tiny[0:1, qi:qi + 1], in0=tiny[0:1, qi:qi + 1],
                        scalar1=1.0 / numel, scalar2=Q_EPS, op0=ALU.mult, op1=ALU.add),
                     reads=("tiny",), writes=("tiny",), tag=f"gsc{qi}")
            em.v(lambda e: e.reciprocal(tiny[0:1, 0:3], tiny[0:1, 0:3]),
                 reads=("tiny",), writes=("tiny",), tag="ginv0")
            em.t(lambda e: e.matmul(PC[0][:, 0:3], c_onesrow[:], tiny[0:1, 0:3],
                                    start=True, stop=True),
                 reads=("tiny", "c_onesrow"), writes=("PC0",), tag="gbc")
            em.v(lambda e: e.tensor_copy(ginv[:, 0:3], PC[0][:, 0:3]),
                 reads=("PC0",), writes=("ginv",), tag="ginv")

        quant_jobs = []
        for qi, (nm, ap, nrows, gi) in enumerate(ffn_rows):
            for row in range(nrows):
                for (c0, w) in col_chunks(ap.shape[1]):
                    quant_jobs.append((qi, ap, row, c0, w))

        def quant_tile(qi, ap, row, c0, w, sl, tl):
            ikey = f"prepin:{sl}"
            bkey = f"prepbf:{tl}"
            okey = f"prepout:{tl}"
            tkey = "prept:0"
            pt_t = prep_t[0]
            em.dma("qscalar" if sl == 1 else "qsync", prep_in[sl][:, 0:w],
                   ap[row * 128:(row + 1) * 128, c0:c0 + w],
                   reads=(), writes=(ikey,), semkey=f"pl{sl}", tag=f"ql{qi}_{row}_{c0}")
            em.v(lambda e, sl=sl, tl=tl, qi=qi, w=w: e.tensor_scalar(
                    out=prep_bf[tl][:, 0:w], in0=prep_in[sl][:, 0:w],
                    scalar1=ginv[:, qi:qi + 1], scalar2=0.5,
                    op0=ALU.mult, op1=ALU.is_ge),
                 reads=(ikey, "ginv"), writes=(bkey,), tag=f"qa{qi}_{row}_{c0}")
            em.v(lambda e, sl=sl, qi=qi, w=w, pt_t=pt_t: e.tensor_scalar(
                    out=pt_t[:, 0:w], in0=prep_in[sl][:, 0:w],
                    scalar1=ginv[:, qi:qi + 1], scalar2=-0.5,
                    op0=ALU.mult, op1=ALU.is_le),
                 reads=(ikey, "ginv"), writes=(tkey,), tag=f"qb{qi}_{row}_{c0}")
            em.v(lambda e, tl=tl, w=w, pt_t=pt_t: e.tensor_tensor(
                    prep_bf[tl][:, 0:w], prep_bf[tl][:, 0:w], pt_t[:, 0:w],
                    op=ALU.subtract),
                 reads=(bkey, tkey), writes=(bkey,), tag=f"qq{qi}_{row}_{c0}")
            nk = w // 128
            pt = PT[tl]
            ptk = f"PT{tl}"
            for j in range(nk):
                em.t(lambda e, pt=pt, j=j, tl=tl: e.transpose(
                        pt[:, j * 128:(j + 1) * 128],
                        prep_bf[tl][:, j * 128:(j + 1) * 128], c_i128[:]),
                     reads=(bkey, "c_i128"), writes=(ptk,), tag=f"qtt{qi}_{row}_{c0}_{j}")
            em.s(lambda e, pt=pt, tl=tl, nk=nk: e.copy(
                    prep_out[tl][:, 0:nk * 128], pt[:, 0:nk * 128]),
                 reads=(ptk,), writes=(okey,), tag=f"qe{qi}_{row}_{c0}")
            if qi < 2:
                em.dma("qpool", scrQ12[qi, row, :, c0:c0 + w], prep_out[tl][:, 0:w],
                       reads=(okey,), writes=(f"scrQ:{qi}:{row}:{c0}",),
                       semkey=f"qsq{tl}", tag=f"qst{qi}_{row}_{c0}")
            else:
                em.dma("qpool", scrQ3[row, :, c0:c0 + w], prep_out[tl][:, 0:w],
                       reads=(okey,), writes=(f"scrQ3:{row}:{c0}",),
                       semkey=f"qsq{tl}", tag=f"qst3_{row}_{c0}")

        _qdrip = {"i": 0, "on": False}

        def drip_quant(n=1):
            if not _qdrip["on"]:
                return
            i = _qdrip["i"]
            for _ in range(n):
                if i >= len(quant_jobs):
                    break
                qi, ap, row, c0, w = quant_jobs[i]
                quant_tile(qi, ap, row, c0, w, sl=i % 3, tl=i % 2)
                i += 1
            _qdrip["i"] = i

        w1sb = sb(stackA, "w1sb", [128, NT * 160], BF16)
        w2aug = [sb(stackA, f"w2aug{f}", [33, C], BF16) for f in range(5)]
        dw1sb = sb(stackA, "dw1sb", [128, NT * 64], BF16)
        dw2aug = sb(stackA, "dw2aug", [65, C], F32)
        c_ubc = sb(stackA, "c_ubc", [128, NT * 64], BF16)
        em.dma("qpool", dw2aug[0:64, :], dw2[:], writes=("dw2aug:a",),
               semkey="initp", batch=True, tag="ld_dw2a")
        em.dma("qpool", dw2aug[64:65, :], tdec[:].rearrange("(a c) -> a c", a=1),
               writes=("dw2aug:b",), semkey="initp", batch=True, tag="ld_dw2b")
        em.dma("qpool", c_ubc[:], u_bc_in[:], writes=("c_ubc",), semkey="initp",
               batch=True, tag="ld_ubc")
        em.dma("qpool", w1sb[:], maa_w1r[:],
               writes=("w1sb",), semkey="initp", batch=True, tag="ld_w1")
        for f, fname in enumerate("wkvrg"):
            em.dma("qpool", w2aug[f][0:32, :], w2f[f][:],
                   writes=(f"w2aug{f}:a",), semkey="initp", batch=True, tag=f"ld_w2{f}")
            em.dma("qpool", w2aug[f][32:33, :],
                   maa_vec[fname][:].rearrange("(a c) -> a c", a=1),
                   writes=(f"w2aug{f}:b",), semkey="initp", batch=True, tag=f"ld_maa{f}")
        em.dma("qpool", dw1sb[:], dw1r[:],
               writes=("dw1sb",), semkey="initp", batch=True, tag="ld_dw1")
        XCH = T // 8
        for i in range(8):
            em.dma("qpool", x_bf[i * XCH:(i + 1) * XCH, :],
                   x_in[i * XCH:(i + 1) * XCH, :],
                   reads=(), writes=(f"xbf:{i}",), semkey=f"xbf{i}", tag=f"xbf{i}")

        def xbf_keys(r0, r1):
            return tuple(f"xbf:{i}" for i in range(r0 // XCH, (r1 - 1) // XCH + 1))
        xt = [sb(stackA, f"xt{i}", [128, C], BF16) for i in range(2)]
        sqt = sb(stackA, "sqt", [128, C], BF16)
        xx_sb = sb(stackA, "xx_sb", [128, NT * TSA], BF16)
        xf = [sb(stackA, f"xf{i}", [128, NT * TSA], BF16) for i in range(2)]
        r_sb = sb(stackA, "r_sb", [128, NT * TSA], BF16)
        k_sb = sb(stackA, "k_sb", [128, NT * TSA], BF16)
        g2_sb = sb(stackA, "g2_sb", [128, NT * TSA], BF16)
        t1g = sb(stackA, "t1g", [128, TSA], BF16)
        Vtp = sb(stackA, "Vtp", [128, (NCC // 2) * C], BF16)
        ew_tp = sb(stackA, "ewtp", [128, C], F32)
        tanh_f = [sb(stackA, f"tanhf{f}", [33, TSA], BF16) for f in range(5)]
        tanh2 = sb(stackA, "tanh2", [65, TSA], F32)
        t2m = [sb(stackA, f"t2m{i}", [128, TSA], BF16) for i in range(2)]
        E1 = [sb(stackA, f"E1_{i}", [128, NT * 64], BF16) for i in range(2)]
        _e2 = sb(stackA, "E2_0", [128, NT * 64], BF16)
        E2 = [_e2, _e2]
        E3 = sb(stackA, "E3", [128, C], BF16)
        Gam = [sb(stackA, f"Gam{i}", [128, NT], F32) for i in range(2)]
        ktp = sb(stackA, "ktp", [128, C], BF16)
        P2 = sb(stackA, "P2", [128, NT * 64], BF16)
        A_sl = sb(stackA, "A_sl", [128, 512], BF16)
        yt2 = sb(stackA, "yt2", [128, 512], BF16)
        du_sb = sb(stackA, "du_sb", [32, TSA], BF16)
        y_sb = sb(stackA, "y_sb", [128, (TSA // 128) * C], BF16)
        stat = sb(stackA, "stat", [128, 64], F32)
        abT = sb(stackA, "abT", [32, 256], BF16)
        yn_tp = sb(stackA, "yn_tp", [128, C], BF16)
        xz_sb = sb(stackA, "xz_sb", [128, NT * TSA], BF16)
        ws = [sb(stackA, f"ws{i}", [128, C], BF16) for i in range(4)]

        def seg_ln1(s):
            t0 = s * TSA
            h3 = h_sb[:].rearrange("p (n t) -> p n t", t=TSA)
            for tt in range(TSA // 128):
                b = tt % 2
                xk = f"xt:{b}"
                em.dma("qsync", xt[b][:], x_bf[t0 + tt * 128: t0 + (tt + 1) * 128, :],
                       reads=xbf_keys(t0 + tt * 128, t0 + (tt + 1) * 128),
                       writes=(xk,), semkey=f"xin{b}", tag=f"xl{s}_{tt}")
                em.s(lambda e, b=b: e.square(sqt[:], xt[b][:]),
                     reads=(xk,), writes=("sqt",), tag=f"sq{s}_{tt}")
                em.v(lambda e, b=b: e.tensor_reduce(
                        out=stat[:, 0:1], in_=xt[b][:], axis=AX, op=ALU.add),
                     reads=(xk,), writes=("stat",), tag=f"r1{s}_{tt}")
                em.v(lambda e: e.tensor_reduce(
                        out=stat[:, 1:2], in_=sqt[:], axis=AX, op=ALU.add),
                     reads=("sqt",), writes=("stat",), tag=f"r2{s}_{tt}")
                em.v(lambda e: e.tensor_scalar_mul(stat[:, 2:3], stat[:, 0:1], 1.0 / C),
                     reads=("stat",), writes=("stat",), tag=f"mu{s}_{tt}")
                em.v(lambda e: e.tensor_tensor(stat[:, 3:4], stat[:, 2:3], stat[:, 2:3],
                                               op=ALU.mult),
                     reads=("stat",), writes=("stat",), tag=f"mu2{s}_{tt}")
                em.v(lambda e: e.tensor_scalar(
                        out=stat[:, 4:5], in0=stat[:, 1:2], scalar1=1.0 / C,
                        scalar2=stat[:, 3:4], op0=ALU.mult, op1=ALU.subtract),
                     reads=("stat",), writes=("stat",), tag=f"var{s}_{tt}")
                em.s(lambda e: e.activation(stat[:, 5:6], stat[:, 4:5], AF.Ln, bias=c_bias[:, 0:1]),
                     reads=("stat",), writes=("stat",), tag=f"ln{s}_{tt}")
                em.s(lambda e: e.activation(stat[:, 6:7], stat[:, 5:6], AF.Exp, scale=-0.5),
                     reads=("stat",), writes=("stat",), tag=f"ex{s}_{tt}")
                em.v(lambda e, b=b: e.tensor_scalar(
                        out=yn_tp[:], in0=xt[b][:], scalar1=stat[:, 2:3],
                        scalar2=stat[:, 6:7], op0=ALU.subtract, op1=ALU.mult),
                     reads=(xk, "stat"), writes=tuple(f"yn:{q}" for q in range(4)),
                     tag=f"hn{s}_{tt}")
                for half in range(2):
                    pt = PT[half]
                    ptk = f"PT{half}"
                    for j in range(8):
                        jt = half * 8 + j
                        em.t(lambda e, pt=pt, j=j, jt=jt: e.transpose(
                                pt[:, j * 128:(j + 1) * 128],
                                yn_tp[:, jt * 128:(jt + 1) * 128], c_i128[:]),
                             reads=(f"yn:{jt // 4}", "c_i128"), writes=(ptk,),
                             tag=f"ht{s}_{tt}_{jt}")
                    em.s(lambda e, pt=pt, half=half, tt=tt: e.copy(
                            h3[:, half * 8:(half + 1) * 8, tt * 128:(tt + 1) * 128],
                            pt[:].rearrange("p (j n) -> p j n", n=128)),
                         reads=(ptk,), writes=(f"h:{half}",), tag=f"he{s}_{tt}_{half}")
            xx3 = xx_sb[:].rearrange("p (n t) -> p n t", t=TSA)
            xxk = tuple(f"xx:{ct}" for ct in range(NT))
            hks = ("h:0", "h:1")
            em.v(lambda e, xx3=xx3: e.tensor_tensor(
                    xx3[:, :, 1:TSA], h3[:, :, 0:TSA - 1], h3[:, :, 1:TSA],
                    op=ALU.subtract),
                 reads=hks, writes=xxk, tag=f"xx{s}")
            em.v(lambda e, xx3=xx3: e.tensor_tensor(
                    xx3[:, :, 0:1], hc_sb[:].rearrange("p (n o) -> p n o", o=1),
                    h3[:, :, 0:1], op=ALU.subtract),
                 reads=hks + ("hc",), writes=xxk, tag=f"xxc{s}")
            em.v(lambda e: e.tensor_copy(
                    hc_sb[:].rearrange("p (n o) -> p n o", o=1), h3[:, :, TSA - 1:TSA]),
                 reads=hks + xxk, writes=("hc",), tag=f"hcs{s}")
            for jt in range(NT):
                em.v(lambda e, jt=jt: e.scalar_tensor_tensor(
                        out=xz_sb[:, jt * TSA:(jt + 1) * TSA],
                        in0=xx_sb[:, jt * TSA:(jt + 1) * TSA],
                        scalar=c_maa_x[:, jt:jt + 1],
                        in1=h3[:, jt, 0:TSA], op0=ALU.mult, op1=ALU.add),
                     reads=(f"xx:{jt}", f"h:{jt // 8}", "c_maa_x"),
                     writes=(f"xz:{jt}",), tag=f"xxx{s}_{jt}")

        def seg_maa(s):
            banks = [(PM[0], "PM0", 0), (PM[0], "PM0", 256), (PM[1], "PM1", 0),
                     (PM[1], "PM1", 256), (PC[0], "PC0", 0)]
            for f in range(5):
                pb, pbk, co = banks[f]
                for kt in range(NT):
                    em.t(lambda e, pb=pb, co=co, f=f, kt=kt: e.matmul(
                            pb[0:32, co:co + TSA],
                            w1sb[:, kt * 160 + f * 32: kt * 160 + (f + 1) * 32],
                            xz_sb[:, kt * TSA:(kt + 1) * TSA],
                            start=(kt == 0), stop=(kt == NT - 1)),
                         reads=(f"xz:{kt}", "w1sb"), writes=(pbk,), tag=f"maa{s}_{f}_{kt}")
                em.s(lambda e, pb=pb, co=co, f=f: e.activation(
                        tanh_f[f][0:32, :], pb[0:32, co:co + TSA], AF.Tanh),
                     reads=(pbk,), writes=(f"tanhf{f}",), tag=f"tanh{s}_{f}")

        def seg_xf(s, f, dst, dkey):
            h3 = h_sb[:].rearrange("p (n t) -> p n t", t=TSA)
            for ct in range(NT):
                pb, pbk = (PM[ct % 2], f"PM{ct % 2}")
                tm, tmk = (t2m[ct % 2], f"t2m:{ct % 2}")
                em.t(lambda e, pb=pb, f=f, ct=ct: e.matmul(
                        pb[:, 0:TSA],
                        w2aug[f][:, ct * 128:(ct + 1) * 128], tanh_f[f][:],
                        start=True, stop=True),
                     reads=(f"tanhf{f}", f"w2aug{f}:a", f"w2aug{f}:b"), writes=(pbk,), tag=f"mf{s}_{f}_{ct}")
                em.v(lambda e, pb=pb, ct=ct, tm=tm: e.tensor_tensor(
                        tm[:], pb[:, 0:TSA], xx_sb[:, ct * TSA:(ct + 1) * TSA],
                        op=ALU.mult),
                     reads=(pbk, f"xx:{ct}"), writes=(tmk,), tag=f"t2{s}_{f}_{ct}")
                em.v(lambda e, ct=ct, dst=dst, tm=tm: e.tensor_tensor(
                        dst[:, ct * TSA:(ct + 1) * TSA], tm[:],
                        h3[:, ct, 0:TSA], op=ALU.add),
                     reads=(tmk, f"h:{ct // 8}"), writes=(f"{dkey}:{ct}",), tag=f"xf{s}_{f}_{ct}")

        def stream_w(s, m, ot, slot):
            mi = MATS.index(m)
            wk = f"ws:{slot}"
            q = "qscalar" if slot % 2 == 1 else "qsync"
            em.dma(q, ws[slot][:], scrA[mi, ot], reads=(f"scrA:{m}:{ot}",),
                   writes=(wk,), semkey=f"ws{slot}", tag=f"wl{s}_{m}{ot}")
            return wk

        def seg_proj_chan(s, m, src_key, src, dst, dkey, evac):
            for ot in range(NT):
                slot = ot % 4
                wk = stream_w(s, m, ot, slot)
                pb, pbk = (PM[ot % 2], f"PM{ot % 2}")
                for kt in range(NT):
                    em.t(lambda e, pb=pb, slot=slot, kt=kt, src=src: e.matmul(
                            pb[:, 0:TSA], ws[slot][:, kt * 128:(kt + 1) * 128],
                            src[:, kt * TSA:(kt + 1) * TSA],
                            start=(kt == 0), stop=(kt == NT - 1)),
                         reads=(wk, f"{src_key}:{kt}"), writes=(pbk,),
                         tag=f"{m}mm{s}_{ot}_{kt}")
                evac(ot, pb, pbk)
                drip_quant(1)

        def seg_attn_matmuls(s):
            seg_xf(s, 0, xf[0], "xf:0")
            for kt in range(NT):
                em.t(lambda e, kt=kt: e.matmul(
                        PC[1][0:64, 0:TSA], dw1sb[:, kt * 64:(kt + 1) * 64],
                        xf[0][:, kt * TSA:(kt + 1) * TSA],
                        start=(kt == 0), stop=(kt == NT - 1)),
                     reads=(f"xf:0:{kt}", "dw1sb"), writes=("PC1",), tag=f"dw1{s}_{kt}")
            em.s(lambda e: e.activation(tanh2[0:64, :], PC[1][0:64, 0:TSA], AF.Tanh),
                 reads=("PC1",), writes=("tanh2",), tag=f"tanh2{s}")
            seg_xf(s, 1, xf[1], "xf:1")
            def evk(ot, pb, pbk):
                em.v(lambda e, pb=pb, ot=ot: e.tensor_copy(
                        k_sb[:, ot * TSA:(ot + 1) * TSA], pb[:, 0:TSA]),
                     reads=(pbk,), writes=(f"k:{ot}",), tag=f"kev{s}_{ot}")
            seg_proj_chan(s, "k", "xf:1", xf[1], k_sb, "k_sb", evk)
            seg_xf(s, 2, xf[0], "xf:0")
            for ot in range(NT):
                slot = ot % 4
                wk = stream_w(s, "v", ot, slot)
                pb, pbk = (PM[ot % 2], f"PM{ot % 2}")
                for tt in range(TSA // 128):
                    for kt in range(NT):
                        em.t(lambda e, pb=pb, slot=slot, kt=kt, tt=tt: e.matmul(
                                pb[:, tt * 128:(tt + 1) * 128],
                                xf[0][:, kt * TSA + tt * 128: kt * TSA + tt * 128 + 128],
                                ws[slot][:, kt * 128:(kt + 1) * 128],
                                start=(kt == 0), stop=(kt == NT - 1)),
                             reads=(wk, f"xf:0:{kt}"), writes=(pbk,), tag=f"vmm{s}_{tt}_{ot}_{kt}")
                for tt in range(TSA // 128):
                    for hh in range(2):
                        em.v(lambda e, pb=pb, hh=hh, tt=tt, ot=ot: e.tensor_copy(
                                Vtp[hh * 64:(hh + 1) * 64,
                                    tt * C + ot * 128: tt * C + (ot + 1) * 128],
                                pb[hh * 64:(hh + 1) * 64, tt * 128:(tt + 1) * 128]),
                             reads=(pbk,), writes=(f"Vtp:{2 * tt + hh}:{ot}",),
                             tag=f"vev{s}_{tt}_{ot}_{hh}")
                drip_quant(1)
            seg_xf(s, 3, xf[1], "xf:1")
            def evr(ot, pb, pbk):
                em.v(lambda e, pb=pb, ot=ot: e.tensor_copy(
                        r_sb[:, ot * TSA:(ot + 1) * TSA], pb[:, 0:TSA]),
                     reads=(pbk,), writes=(f"r:{ot}",), tag=f"rev{s}_{ot}")
            seg_proj_chan(s, "r", "xf:1", xf[1], r_sb, "r_sb", evr)
            seg_xf(s, 4, xf[0], "xf:0")
            def evg(ot, pb, pbk):
                em.s(lambda e, pb=pb: e.activation(t1g[:], pb[:, 0:TSA], AF.Tanh, scale=0.5),
                     reads=(pbk,), writes=("t1g",), tag=f"gt{s}_{ot}")
                em.s(lambda e: e.activation(t1g[:], t1g[:], AF.Copy, bias=1.0),
                     reads=("t1g",), writes=("t1g",), tag=f"g1{s}_{ot}")
                em.v(lambda e, pb=pb, ot=ot: e.tensor_tensor(
                        g2_sb[:, ot * TSA:(ot + 1) * TSA], pb[:, 0:TSA], t1g[:],
                        op=ALU.mult),
                     reads=(pbk, "t1g"), writes=(f"g2:{ot}",), tag=f"g2{s}_{ot}")
            seg_proj_chan(s, "g", "xf:0", xf[0], g2_sb, "g2_sb", evg)


        def seg_decay_tile(s, tt):
            for q4 in range(4):
                pb, pbk = (PM[q4 % 2], f"PM{q4 % 2}")
                em.t(lambda e, pb=pb, tt=tt, q4=q4: e.matmul(
                        pb[:, 0:512], tanh2[:, tt * 128:(tt + 1) * 128],
                        dw2aug[:, q4 * 512:(q4 + 1) * 512],
                        start=True, stop=True),
                     reads=("tanh2", "dw2aug:a", "dw2aug:b"), writes=(pbk,), tag=f"dw2{s}_{tt}_{q4}")
                em.s(lambda e, pb=pb, q4=q4: e.activation(
                        ew_tp[:, q4 * 512:(q4 + 1) * 512], pb[:, 0:512], AF.Exp),
                     reads=(pbk,), writes=(f"ew:{q4}",), tag=f"ew{s}_{tt}_{q4}")

        def seg_wkv(s):
            for c in range(NCC):
                tt = c // 2
                rb = 64 * (c % 2)
                cp = c % 2
                if rb == 0:
                    seg_decay_tile(s, tt)
                e1k = [f"E1:{cp}:0", f"E1:{cp}:1"]
                e2k = ["E2:0", "E2:1"]
                gmk = f"Gam:{cp}"
                for which, tri, dst, dkeys in [(0, c_triex, E1[cp], e1k),
                                               (1, c_triin, E2[cp], e2k)]:
                    for half in range(2):
                        pc, pck = (PC[half], f"PC{half}")
                        for j in range(8):
                            jt = half * 8 + j
                            em.t(lambda e, pc=pc, j=j, jt=jt, rb=rb, tri=tri: e.matmul(
                                    pc[:, j * 64:(j + 1) * 64],
                                    ew_tp[rb:rb + 64, jt * 128:(jt + 1) * 128],
                                    tri[rb:rb + 64, :], start=True, stop=True),
                                 reads=(f"ew:{jt // 4}", "c_triex", "c_triin"),
                                 writes=(pck,), tag=f"cs{s}_{c}_{which}_{jt}")
                        sc = -1.0 if which == 0 else 1.0
                        em.s(lambda e, pc=pc, dst=dst, half=half, sc=sc: e.activation(
                                dst[:, half * 512:(half + 1) * 512], pc[:], AF.Exp, scale=sc),
                             reads=(pck,), writes=(dkeys[half],), tag=f"Ee{s}_{c}_{which}_{half}")
                        if which == 1:
                            em.s(lambda e, pc=pc, half=half, cp=cp: e.activation(
                                    Gam[cp][:, half * 8:(half + 1) * 8],
                                    pc[:].rearrange("p (j x) -> p j x", x=64)[:, :, 63],
                                    AF.Exp, scale=-1.0),
                                 reads=(pck,), writes=(gmk,), tag=f"Gm{s}_{c}_{half}")
                for q4 in range(4):
                    pc, pck = (PC[q4 % 2], f"PC{q4 % 2}")
                    em.t(lambda e, pc=pc, q4=q4, rb=rb: e.matmul(
                            pc[rb:rb + 64, 0:512], c_trisuf[rb:rb + 64, :],
                            ew_tp[rb:rb + 64, q4 * 512:(q4 + 1) * 512],
                            start=True, stop=True),
                         reads=(f"ew:{q4}", "c_trisuf"), writes=(pck,), tag=f"sf{s}_{c}_{q4}")
                    em.s(lambda e, pc=pc, q4=q4, rb=rb: e.activation(
                            E3[rb:rb + 64, q4 * 512:(q4 + 1) * 512], pc[rb:rb + 64, 0:512],
                            AF.Exp, scale=-1.0),
                         reads=(pck,), writes=(f"E3:{q4}",), tag=f"E3{s}_{c}_{q4}")
                r3 = r_sb[:].rearrange("p (n t) -> p n t", t=TSA)
                k3 = k_sb[:].rearrange("p (n t) -> p n t", t=TSA)
                rks = tuple(f"r:{o}" for o in range(NT)) + tuple(f"k:{o}" for o in range(NT))
                em.v(lambda e, c=c, cp=cp: e.tensor_tensor(
                        E1[cp][:].rearrange("p (n t) -> p n t", t=64),
                        r3[:, :, c * 64:(c + 1) * 64],
                        E1[cp][:].rearrange("p (n t) -> p n t", t=64), op=ALU.mult),
                     reads=rks[:NT] + tuple(e1k), writes=tuple(e1k), tag=f"Qp{s}_{c}")
                em.v(lambda e, c=c, cp=cp: e.tensor_tensor(
                        E2[cp][:].rearrange("p (n t) -> p n t", t=64),
                        k3[:, :, c * 64:(c + 1) * 64],
                        E2[cp][:].rearrange("p (n t) -> p n t", t=64), op=ALU.mult),
                     reads=rks[NT:] + tuple(e2k), writes=tuple(e2k), tag=f"Kp{s}_{c}")
                for half in range(2):
                    pt, ptk = (PT[half], f"PT{half}")
                    for j in range(8):
                        jt = half * 8 + j
                        em.t(lambda e, pt=pt, j=j, jt=jt, c=c, rb=rb: e.transpose(
                                pt[rb:rb + 64, j * 128:(j + 1) * 128],
                                k_sb[:, jt * TSA + c * 64: jt * TSA + c * 64 + 64],
                                c_i128[:]),
                             reads=(f"k:{jt}", "c_i128"), writes=(ptk,), tag=f"kt{s}_{c}_{jt}")
                    em.s(lambda e, pt=pt, half=half, rb=rb: e.copy(
                            ktp[rb:rb + 64, half * 1024:(half + 1) * 1024],
                            pt[rb:rb + 64, :]),
                         reads=(ptk,), writes=(f"ktp:{half}",), tag=f"kte{s}_{c}_{half}")
                em.v(lambda e, rb=rb: e.tensor_tensor(
                        ktp[rb:rb + 64, :], ktp[rb:rb + 64, :], E3[rb:rb + 64, :],
                        op=ALU.mult),
                     reads=("ktp:0", "ktp:1") + tuple(f"E3:{q}" for q in range(4)),
                     writes=("ktp:0", "ktp:1"), tag=f"Kpp{s}_{c}")
                em.v(lambda e, c=c: e.tensor_tensor(
                        P2[:].rearrange("p (n t) -> p n t", t=64),
                        r3[:, :, c * 64:(c + 1) * 64], k3[:, :, c * 64:(c + 1) * 64],
                        op=ALU.mult),
                     reads=rks, writes=("P2",), tag=f"P1{s}_{c}")
                em.v(lambda e: e.tensor_tensor(P2[:], P2[:], c_ubc[:], op=ALU.mult),
                     reads=("P2", "c_ubc"), writes=("P2",), tag=f"P2{s}_{c}")
                for jt in range(NT):
                    em.t(lambda e, jt=jt: e.matmul(
                            PA[0:32, 448:512], c_onesblk[:, jt * 32:(jt + 1) * 32],
                            P2[:, jt * 64:(jt + 1) * 64],
                            start=(jt == 0), stop=(jt == NT - 1)),
                         reads=("P2", "c_onesblk"), writes=("PA",), tag=f"du{s}_{c}_{jt}")
                em.s(lambda e, c=c: e.copy(du_sb[:, c * 64:(c + 1) * 64],
                                           PA[0:32, 448:512]),
                     reads=("PA",), writes=(f"du:{c}",), tag=f"due{s}_{c}")
                vcol = (c // 2) * C
                for g in range(NT // 4):
                    jts = [g * 4 + i for i in range(4)]
                    vks = tuple(f"Vtp:{c}:{j}" for j in jts)
                    for hh, pab, pabk in ((0, PA, "PA"), (1, PB, "PB")):
                        for i, jt in enumerate(jts):
                            em.t(lambda e, jt=jt, hh=hh, rb=rb, pab=pab, i=i, cp=cp: e.matmul(
                                    pab[rb:rb + 64, i * 64:(i + 1) * 64],
                                    E2[cp][hh * 64:(hh + 1) * 64, jt * 64:(jt + 1) * 64],
                                    E1[cp][hh * 64:(hh + 1) * 64, jt * 64:(jt + 1) * 64],
                                    start=True, stop=True),
                                 reads=tuple(e1k) + tuple(e2k), writes=(pabk,),
                                 tag=f"A{s}_{c}_{jt}_{hh}")
                        em.v(lambda e, hh=hh, rb=rb, pab=pab: e.tensor_tensor(
                                A_sl[rb:rb + 64, :].rearrange(
                                    "p (i x) -> p i x", x=128)[:, :, hh * 64:hh * 64 + 64],
                                pab[rb:rb + 64, 0:256].rearrange("p (i x) -> p i x", x=64),
                                c_tri4[rb:rb + 64, :].rearrange("p (i x) -> p i x", x=64),
                                op=ALU.mult),
                             reads=(pabk, "c_tri4"), writes=("Agrp",),
                             tag=f"Am{s}_{c}_{g}_{hh}")
                    for i, jt in enumerate(jts):
                        for hh in range(2):
                            em.t(lambda e, jt=jt, hh=hh, rb=rb, vcol=vcol, i=i: e.matmul(
                                    PC[0][rb:rb + 64, i * 128 + hh * 64: i * 128 + hh * 64 + 64],
                                    A_sl[rb:rb + 64, i * 128 + hh * 64: i * 128 + hh * 64 + 64],
                                    Vtp[rb:rb + 64,
                                        vcol + jt * 128 + hh * 64: vcol + jt * 128 + hh * 64 + 64],
                                    start=True, stop=True),
                                 reads=("Agrp", f"Vtp:{c}:{jt}"), writes=("PC0",),
                                 tag=f"Yi{s}_{c}_{jt}_{hh}")
                        em.t(lambda e, jt=jt, rb=rb, i=i, cp=cp: e.matmul(
                                PC[1][rb:rb + 64, 256 + i * 64: 256 + (i + 1) * 64],
                                E1[cp][0:64, jt * 64:(jt + 1) * 64],
                                Sbf[0:64, jt * 64:(jt + 1) * 64],
                                start=True, stop=True),
                             reads=tuple(e1k) + (f"Sbf:{jt}",), writes=("PC1",),
                             tag=f"Ys{s}_{c}_{jt}_0")
                        em.t(lambda e, jt=jt, rb=rb, i=i, cp=cp: e.matmul(
                                PM[0][rb:rb + 64, 256 + i * 64: 256 + (i + 1) * 64],
                                E1[cp][64:128, jt * 64:(jt + 1) * 64],
                                Sbf[64:128, jt * 64:(jt + 1) * 64],
                                start=True, stop=True),
                             reads=tuple(e1k) + (f"Sbf:{jt}",), writes=("PM0",),
                             tag=f"Ys{s}_{c}_{jt}_1")
                    em.v(lambda e, rb=rb: e.tensor_copy(
                            yt2[rb:rb + 64, :], PC[0][rb:rb + 64, 0:512]),
                         reads=("PC0",), writes=("yt2",), tag=f"Yc{s}_{c}_{g}")
                    yb = y_sb[:].rearrange("p (t x) -> p t x", x=C)
                    em.v(lambda e, rb=rb, tt=tt, g=g, yb=yb: e.tensor_tensor(
                            yb[rb:rb + 64, tt, g * 512:(g + 1) * 512].rearrange(
                                "p (i x) -> p i x", x=128)[:, :, 0:64],
                            yt2[rb:rb + 64, :].rearrange("p (i x) -> p i x", x=128)[:, :, 0:64],
                            PC[1][rb:rb + 64, 256:512].rearrange("p (i x) -> p i x", x=64),
                            op=ALU.add),
                         reads=("yt2", "PC1"), writes=(f"y:{tt}:{g}",), tag=f"Ye0{s}_{c}_{g}")
                    em.v(lambda e, rb=rb, tt=tt, g=g, yb=yb: e.tensor_tensor(
                            yb[rb:rb + 64, tt, g * 512:(g + 1) * 512].rearrange(
                                "p (i x) -> p i x", x=128)[:, :, 64:128],
                            yt2[rb:rb + 64, :].rearrange("p (i x) -> p i x", x=128)[:, :, 64:128],
                            PM[0][rb:rb + 64, 256:512].rearrange("p (i x) -> p i x", x=64),
                            op=ALU.add),
                         reads=("yt2", "PM0"), writes=(f"y:{tt}:{g}",), tag=f"Ye1{s}_{c}_{g}")
                    for i, jt in enumerate(jts):
                        for hh in range(2):
                            em.t(lambda e, jt=jt, hh=hh, rb=rb, vcol=vcol, i=i: e.matmul(
                                    PM[1][hh * 64:(hh + 1) * 64, 256 + i * 64: 256 + (i + 1) * 64],
                                    ktp[rb:rb + 64,
                                        jt * 128 + hh * 64: jt * 128 + hh * 64 + 64],
                                    Vtp[rb:rb + 64,
                                        vcol + jt * 128 + hh * 64: vcol + jt * 128 + hh * 64 + 64],
                                    start=True, stop=True),
                                 reads=("ktp:0", "ktp:1", f"Vtp:{c}:{jt}"),
                                 writes=("PM1",), tag=f"KV{s}_{c}_{jt}_{hh}")
                    for i, jt in enumerate(jts):
                        em.v(lambda e, jt=jt, i=i, cp=cp: e.scalar_tensor_tensor(
                                out=S_sb[:, jt * 64:(jt + 1) * 64],
                                in0=S_sb[:, jt * 64:(jt + 1) * 64],
                                scalar=Gam[cp][:, jt:jt + 1],
                                in1=PM[1][:, 256 + i * 64: 256 + (i + 1) * 64],
                                op0=ALU.mult, op1=ALU.add),
                             reads=("PM1", gmk, f"S:{jt}"), writes=(f"S:{jt}",),
                             tag=f"Su{s}_{c}_{jt}")
                        em.v(lambda e, jt=jt: e.tensor_copy(
                                Sbf[:, jt * 64:(jt + 1) * 64], S_sb[:, jt * 64:(jt + 1) * 64]),
                             reads=(f"S:{jt}",), writes=(f"Sbf:{jt}",), tag=f"Sc{s}_{c}_{jt}")

        def seg_gn_gate(s):
            for tt in range(TSA // 128):
                ytk = tuple(f"y:{tt}:{g}" for g in range(4))
                for q4 in range(4):
                    pb, pbk = (PM[q4 % 2], f"PM{q4 % 2}")
                    em.t(lambda e, pb=pb, tt=tt, q4=q4: e.matmul(
                            pb[:, 0:512], du_sb[:, tt * 128:(tt + 1) * 128],
                            c_blockones[:, q4 * 512:(q4 + 1) * 512],
                            start=True, stop=True),
                         reads=(f"du:{2 * tt}", f"du:{2 * tt + 1}", "c_blockones"),
                         writes=(pbk,), tag=f"db{s}_{tt}_{q4}")
                    vk = tuple(f"Vtp:{2 * tt + hh}:{q4 * 4 + i}"
                               for hh in range(2) for i in range(4))
                    em.v(lambda e, pb=pb, tt=tt, q4=q4: e.tensor_tensor(
                            yt2[:, 0:512], pb[:, 0:512],
                            Vtp[:, tt * C + q4 * 512: tt * C + (q4 + 1) * 512],
                            op=ALU.mult),
                         reads=(pbk, "yt2") + vk, writes=("yt2",), tag=f"dm{s}_{tt}_{q4}")
                    em.v(lambda e, tt=tt, q4=q4: e.tensor_tensor(
                            y_sb[:, tt * C + q4 * 512: tt * C + (q4 + 1) * 512],
                            y_sb[:, tt * C + q4 * 512: tt * C + (q4 + 1) * 512],
                            yt2[:, 0:512], op=ALU.add),
                         reads=("yt2", f"y:{tt}:{q4}"), writes=(f"y:{tt}:{q4}",),
                         tag=f"da{s}_{tt}_{q4}")
                em.s(lambda e, tt=tt: e.square(sqt[:], y_sb[:, tt * C:(tt + 1) * C]),
                     reads=ytk, writes=("sqt",), tag=f"gs{s}_{tt}")
                em.v(lambda e, tt=tt: e.tensor_reduce(
                        out=stat[:, 0:32],
                        in_=y_sb[:, tt * C:(tt + 1) * C].rearrange("p (h n) -> p h n", n=64),
                        axis=AX, op=ALU.add),
                     reads=ytk, writes=("stat",), tag=f"g1{s}_{tt}")
                em.v(lambda e: e.tensor_reduce(
                        out=stat[:, 32:64],
                        in_=sqt[:].rearrange("p (h n) -> p h n", n=64),
                        axis=AX, op=ALU.add),
                     reads=("sqt",), writes=("stat",), tag=f"g2r{s}_{tt}")
                em.v(lambda e: e.tensor_scalar_mul(stat[:, 0:32], stat[:, 0:32], 1.0 / 64),
                     reads=("stat",), writes=("stat",), tag=f"gmu{s}_{tt}")
                em.v(lambda e: e.tensor_scalar_mul(stat[:, 32:64], stat[:, 32:64], 1.0 / 64),
                     reads=("stat",), writes=("stat",), tag=f"gm2{s}_{tt}")
                em.v(lambda e: e.tensor_tensor(sqt[:, 0:32], stat[:, 0:32], stat[:, 0:32],
                                               op=ALU.mult),
                     reads=("stat",), writes=("sqt",), tag=f"gmm{s}_{tt}")
                em.v(lambda e: e.tensor_tensor(stat[:, 32:64], stat[:, 32:64], sqt[:, 0:32],
                                               op=ALU.subtract),
                     reads=("stat", "sqt"), writes=("stat",), tag=f"gv{s}_{tt}")
                em.s(lambda e: e.activation(stat[:, 32:64], stat[:, 32:64], AF.Ln,
                                            bias=c_bias[:, 1:2]),
                     reads=("stat",), writes=("stat",), tag=f"gl{s}_{tt}")
                em.s(lambda e: e.activation(stat[:, 32:64], stat[:, 32:64], AF.Exp,
                                            scale=-0.5),
                     reads=("stat",), writes=("stat",), tag=f"ge{s}_{tt}")
                em.v(lambda e: e.tensor_tensor(stat[:, 0:32], stat[:, 0:32], stat[:, 32:64],
                                               op=ALU.mult),
                     reads=("stat",), writes=("stat",), tag=f"gb{s}_{tt}")
                em.t(lambda e: e.transpose(PC[0][0:32, 0:128], stat[:, 32:64], c_i128f[:]),
                     reads=("stat", "c_i128f"), writes=("PC0",), tag=f"gta{s}_{tt}")
                em.v(lambda e: e.tensor_copy(abT[:, 0:128], PC[0][0:32, 0:128]),
                     reads=("PC0",), writes=("abT",), tag=f"gea{s}_{tt}")
                em.t(lambda e: e.transpose(PC[1][0:32, 0:128], stat[:, 0:32], c_i128f[:]),
                     reads=("stat", "c_i128f"), writes=("PC1",), tag=f"gtb{s}_{tt}")
                em.v(lambda e: e.tensor_copy(abT[:, 128:256], PC[1][0:32, 0:128]),
                     reads=("PC1",), writes=("abT",), tag=f"geb{s}_{tt}")
                for q4 in range(4):
                    em.t(lambda e, q4=q4: e.matmul(
                            PC[0][:, 0:512], abT[:, 0:128],
                            c_blockones[:, q4 * 512:(q4 + 1) * 512], start=True, stop=True),
                         reads=("abT", "c_blockones"), writes=("PC0",), tag=f"ba{s}_{tt}_{q4}")
                    em.t(lambda e, q4=q4: e.matmul(
                            PC[1][:, 0:512], abT[:, 128:256],
                            c_blockones[:, q4 * 512:(q4 + 1) * 512], start=True, stop=True),
                         reads=("abT", "c_blockones"), writes=("PC1",), tag=f"bb{s}_{tt}_{q4}")
                    em.v(lambda e, tt=tt, q4=q4: e.tensor_tensor(
                            sqt[:, q4 * 512:(q4 + 1) * 512],
                            y_sb[:, tt * C + q4 * 512: tt * C + (q4 + 1) * 512],
                            PC[0][:, 0:512], op=ALU.mult),
                         reads=(f"y:{tt}:{q4}", "PC0", "sqt"), writes=("sqt",),
                         tag=f"yn1{s}_{tt}_{q4}")
                    em.v(lambda e, q4=q4: e.tensor_tensor(
                            yn_tp[:, q4 * 512:(q4 + 1) * 512],
                            sqt[:, q4 * 512:(q4 + 1) * 512], PC[1][:, 0:512],
                            op=ALU.subtract),
                         reads=("sqt", "PC1"), writes=(f"yn:{q4}",), tag=f"yn2{s}_{tt}_{q4}")
                for half in range(2):
                    pt, ptk = (PT[half], f"PT{half}")
                    for j in range(8):
                        jt = half * 8 + j
                        em.t(lambda e, pt=pt, j=j, jt=jt: e.transpose(
                                pt[:, j * 128:(j + 1) * 128],
                                yn_tp[:, jt * 128:(jt + 1) * 128], c_i128[:]),
                             reads=(f"yn:{jt // 4}", "c_i128"), writes=(ptk,),
                             tag=f"zt{s}_{tt}_{jt}")
                    em.s(lambda e, pt=pt, half=half, tt=tt: e.copy(
                            xz_sb[:].rearrange("p (n t) -> p n t", t=TSA)[
                                :, half * 8:(half + 1) * 8, tt * 128:(tt + 1) * 128],
                            pt[:].rearrange("p (j n) -> p j n", n=128)),
                         reads=(ptk,),
                         writes=tuple(f"xz:{half * 8 + j}" for j in range(8)),
                         tag=f"ze{s}_{tt}_{half}")
            for jt in range(NT):
                em.v(lambda e, jt=jt: e.scalar_tensor_tensor(
                        out=xz_sb[:, jt * TSA:(jt + 1) * TSA],
                        in0=xz_sb[:, jt * TSA:(jt + 1) * TSA],
                        scalar=c_lnbt[:, jt:jt + 1],
                        in1=g2_sb[:, jt * TSA:(jt + 1) * TSA],
                        op0=ALU.add, op1=ALU.mult),
                     reads=(f"xz:{jt}", f"g2:{jt}", "c_lnbt"), writes=(f"xz:{jt}",),
                     tag=f"gz{s}_{jt}")

        def seg_wo_residual(s):
            t0 = s * TSA
            for tt in range(TSA // 128):
                b = tt % 2
                em.dma("qsync", xt[b][:], x_bf[t0 + tt * 128: t0 + (tt + 1) * 128, :],
                       reads=xbf_keys(t0 + tt * 128, t0 + (tt + 1) * 128),
                       writes=(f"xt:{b}",), semkey=f"xre{b}", tag=f"xr{s}_{tt}")
            for ot in range(NT):
                slot = ot % 4
                wk = stream_w(s, "o", ot, slot)
                pb, pbk = (PM[ot % 2], f"PM{ot % 2}")
                for tt in range(TSA // 128):
                    for kt in range(NT):
                        em.t(lambda e, pb=pb, slot=slot, kt=kt, tt=tt: e.matmul(
                                pb[:, tt * 128:(tt + 1) * 128],
                                xz_sb[:, kt * TSA + tt * 128: kt * TSA + tt * 128 + 128],
                                ws[slot][:, kt * 128:(kt + 1) * 128],
                                start=(kt == 0), stop=(kt == NT - 1)),
                             reads=(wk, f"xz:{kt}"), writes=(pbk,), tag=f"omm{s}_{tt}_{ot}_{kt}")
                for tt in range(TSA // 128):
                    b = tt % 2
                    em.v(lambda e, pb=pb, b=b, ot=ot, tt=tt: e.tensor_tensor(
                            xt[b][:, ot * 128:(ot + 1) * 128],
                            xt[b][:, ot * 128:(ot + 1) * 128],
                            pb[:, tt * 128:(tt + 1) * 128], op=ALU.add),
                         reads=(pbk, f"xt:{b}"), writes=(f"xt:{b}",), tag=f"x2{s}_{tt}_{ot}")
                drip_quant(1)
            for tt in range(TSA // 128):
                b = tt % 2
                em.dma("qpool", x2s[t0 + tt * 128: t0 + (tt + 1) * 128, :], xt[b][:],
                       reads=(f"xt:{b}",), writes=(f"x2s:{(t0 // 128) + tt}",),
                       semkey=f"x2o{b}", tag=f"x2st{s}_{tt}")

        for f in range(5):
            em.v(lambda e, f=f: e.memset(tanh_f[f][32:33, :], 1.0),
                 writes=(f"tanhf{f}",), tag=f"tones{f}")
        em.v(lambda e: e.memset(tanh2[64:65, :], 1.0), writes=("tanh2",), tag="t2ones")

        n_gj = len(gamma_jobs)
        n_qj = len(quant_jobs)
        g_per = (n_gj + max(1, NSA // 3) - 1) // max(1, NSA // 3)
        gamma_done = False
        quant_base = 0
        qseg0 = max(2, NSA // 3 + 1)
        q_segs = max(1, NSA - 1 - qseg0)
        q_per = (n_qj + q_segs - 1) // q_segs

        astage = 9
        for ch in parts:
            if ch.isdigit():
                astage = int(ch)
        for s in (range(NSA) if "A" in parts else []):
            if s < NSA // 3 or (not gamma_done and s == NSA // 3):
                lo = s * g_per
                emit_gamma_slice(gamma_jobs[lo:lo + g_per])
                if lo + g_per >= n_gj and not gamma_done:
                    emit_gamma_final()
                    gamma_done = True
            if gamma_done and s >= qseg0:
                _qdrip["on"] = True
                drip_quant(q_per)
                _qdrip["on"] = False
            stages = [seg_ln1, seg_maa, seg_attn_matmuls, seg_wkv, seg_gn_gate,
                      seg_wo_residual]
            for i, fn in enumerate(stages):
                if s == 0 and i == 2:
                    attn_prepass()
                if i < astage:
                    fn(s)
        if not gamma_done:
            emit_gamma_final()
            gamma_done = True
        _qdrip["on"] = True
        drip_quant(n_qj)

        em.fence({
            "tensor": (lambda e: e.matmul(PC[0][0:1, 0:1], c_ones1[0:1, 0:1],
                                          c_ones1[0:1, 0:1], start=True, stop=True),
                       ("PC0",)),
            "vector": (lambda e: e.memset(tiny[:, 0:1], 0.0), ("tiny",)),
            "scalar": (lambda e: e.copy(tiny[:, 1:2], c_ones1[:, 0:1]), ("tiny",)),
            "gpsimd": (lambda e: e.memset(tinyb[:, 0:1], 0.0), ("tinyb",)),
        }, "FENCE_AB")

        stackA.close()
        stackP.close()
        NBLK = max(1, T // 512)
        BW = min(512, T)
        BW2 = min(512, TQ)
        NJ = max(1, TQ // BW2)
        NT8 = TQ // 128

        def emit_phase_b():
            stackB0 = contextlib.ExitStack()
            stackB1 = contextlib.ExitStack()
            xn_sb = sb(stackB1, "xn_sb", [128, NT * T], BF16)
            qws = [sb(stackB1, f"qws{i}", [128, C], BF16) for i in range(3)]
            gf = [sb(stackB1, f"gf{i}", [128, T], BF16) for i in range(2)]
            x2t = [sb(stackB0, f"x2t{i}", [128, C], BF16) for i in range(2)]
            statb = sb(stackB0, "statb", [128, 16], F32)
            h2tp = sb(stackB0, "h2tp", [128, C], BF16)

            xn3 = xn_sb[:].rearrange("p (n t) -> p n t", t=T)
            for tt in range(T // 128):
                b = tt % 2
                xk = f"x2t:{b}"
                em.dma("qsync" if b == 0 else "qscalar", x2t[b][:],
                       x2s[tt * 128:(tt + 1) * 128, :],
                       reads=(f"x2s:{tt}",), writes=(xk,), semkey=f"x2b{b}",
                       tag=f"x2l{tt}")
                em.v(lambda e, b=b: e.tensor_reduce(
                        out=statb[:, 0:1], in_=x2t[b][:], axis=AX, op=ALU.add),
                     reads=(xk,), writes=("statb",), tag=f"br1{tt}")
                for cc in range(4):
                    pm, pmk = (PM[cc % 2], f"PM{cc % 2}")
                    em.s(lambda e, b=b, cc=cc, pm=pm: e.square(
                            pm[:, 0:512], x2t[b][:, cc * 512:(cc + 1) * 512]),
                         reads=(xk,), writes=(pmk,), tag=f"bsq{tt}_{cc}")
                    em.v(lambda e, cc=cc, pm=pm: e.tensor_reduce(
                            out=statb[:, 8 + cc:9 + cc], in_=pm[:, 0:512],
                            axis=AX, op=ALU.add),
                         reads=(pmk,), writes=("statb",), tag=f"br2{tt}_{cc}")
                em.v(lambda e: e.tensor_reduce(
                        out=statb[:, 1:2], in_=statb[:, 8:12], axis=AX, op=ALU.add),
                     reads=("statb",), writes=("statb",), tag=f"br2s{tt}")
                em.v(lambda e: e.tensor_scalar_mul(statb[:, 2:3], statb[:, 0:1], 1.0 / C),
                     reads=("statb",), writes=("statb",), tag=f"bmu{tt}")
                em.v(lambda e: e.tensor_tensor(statb[:, 3:4], statb[:, 2:3], statb[:, 2:3],
                                               op=ALU.mult),
                     reads=("statb",), writes=("statb",), tag=f"bm2{tt}")
                em.v(lambda e: e.tensor_scalar(
                        out=statb[:, 4:5], in0=statb[:, 1:2], scalar1=1.0 / C,
                        scalar2=statb[:, 3:4], op0=ALU.mult, op1=ALU.subtract),
                     reads=("statb",), writes=("statb",), tag=f"bv{tt}")
                em.s(lambda e: e.activation(statb[:, 5:6], statb[:, 4:5], AF.Ln,
                                            bias=c_bias[:, 0:1]),
                     reads=("statb",), writes=("statb",), tag=f"bl{tt}")
                em.s(lambda e: e.activation(statb[:, 6:7], statb[:, 5:6], AF.Exp, scale=-0.5),
                     reads=("statb",), writes=("statb",), tag=f"be{tt}")
                em.v(lambda e, b=b: e.tensor_scalar(
                        out=h2tp[:], in0=x2t[b][:], scalar1=statb[:, 2:3],
                        scalar2=statb[:, 6:7], op0=ALU.subtract, op1=ALU.mult),
                     reads=(xk, "statb"), writes=("h2tp",), tag=f"bh{tt}")
                for half in range(2):
                    pt, ptk = (PT[half], f"PT{half}")
                    for j in range(8):
                        jt = half * 8 + j
                        em.t(lambda e, pt=pt, j=j, jt=jt: e.transpose(
                                pt[:, j * 128:(j + 1) * 128],
                                h2tp[:, jt * 128:(jt + 1) * 128], c_i128[:]),
                             reads=("h2tp", "c_i128"), writes=(ptk,), tag=f"bt{tt}_{jt}")
                    em.s(lambda e, pt=pt, half=half, tt=tt: e.copy(
                            xn3[:, half * 8:(half + 1) * 8, tt * 128:(tt + 1) * 128],
                            pt[:].rearrange("p (j n) -> p j n", n=128)),
                         reads=(ptk,), writes=(f"xn:{half}",), tag=f"bte{tt}_{half}")

            def q12load(qi, f, sl):
                em.dma("qscalar" if sl == 1 else "qsync", qws[sl][:], scrQ12[qi, f],
                       reads=tuple(f"scrQ:{qi}:{f}:{c0}" for (c0, w) in col_chunks(C)),
                       writes=(f"qws:{sl}",), semkey=f"qw{sl}", tag=f"q{qi}l_{f}")

            q12load(0, 0, 0)
            q12load(1, 0, 2)

            em.fence({
                "tensor": (lambda e: e.matmul(PC[0][0:1, 0:1], c_ones1[0:1, 0:1],
                                              c_ones1[0:1, 0:1], start=True, stop=True),
                           ("PC0",)),
                "vector": (lambda e: e.memset(tiny[:, 0:1], 0.0), ("tiny",)),
                "scalar": (lambda e: e.copy(tiny[:, 1:2], c_ones1[:, 0:1]), ("tiny",)),
                "gpsimd": (lambda e: e.memset(tinyb[:, 0:1], 0.0), ("tinyb",)),
            }, "FENCE_B1")
            stackB0.close()
            stackPS.close()
            PB8 = [stackB1.enter_context(nc.psum_tensor(f"PB8_{i}", [128, BW], F32))
                   for i in range(NBLK)]

            NHALF = (NBLK + 1) // 2
            halves = [list(range(0, NHALF)), list(range(NHALF, NBLK))]
            for f in range(NF):
                s1 = f % 2
                gsl = f % 2
                if f > 0:
                    q12load(0, f, s1)
                    q12load(1, f, 2)
                for qi, sl in ((0, s1), (1, 2)):
                    for hblk in halves:
                        if not hblk:
                            continue
                        for kt in range(NT):
                            for blk in hblk:
                                em.t(lambda e, sl=sl, kt=kt, blk=blk: e.matmul(
                                        PB8[blk][:, 0:BW],
                                        qws[sl][:, kt * 128:(kt + 1) * 128],
                                        xn_sb[:, kt * T + blk * BW: kt * T + (blk + 1) * BW],
                                        start=(kt == 0), stop=(kt == NT - 1)),
                                     reads=(f"qws:{sl}", f"xn:{kt // 8}"),
                                     writes=(f"PB8_{blk}",), tag=f"b1m{qi}_{f}_{kt}_{blk}")
                        if qi == 0:
                            for blk in hblk:
                                em.s(lambda e, gsl=gsl, blk=blk: e.activation(
                                        gf[gsl][:, blk * BW:(blk + 1) * BW],
                                        PB8[blk][:, 0:BW], AF.Tanh, scale=0.5),
                                     reads=(f"PB8_{blk}",), writes=(f"gf:{gsl}",),
                                     tag=f"b1t{f}_{blk}")
                                em.s(lambda e, gsl=gsl, blk=blk: e.activation(
                                        gf[gsl][:, blk * BW:(blk + 1) * BW],
                                        gf[gsl][:, blk * BW:(blk + 1) * BW],
                                        AF.Copy, scale=0.5, bias=0.5),
                                     reads=(f"gf:{gsl}",), writes=(f"gf:{gsl}",),
                                     tag=f"b1c{f}_{blk}")
                        else:
                            for blk in hblk:
                                em.v(lambda e, gsl=gsl, blk=blk: e.tensor_tensor(
                                        gf[gsl][:, blk * BW:(blk + 1) * BW],
                                        gf[gsl][:, blk * BW:(blk + 1) * BW],
                                        PB8[blk][:, 0:BW], op=ALU.mult),
                                     reads=(f"PB8_{blk}", f"gf:{gsl}"),
                                     writes=(f"gf:{gsl}",), tag=f"b1g{f}_{blk}")
                for qh in range(4):
                    em.dma("qpool", scrG[qh, f], gf[gsl][:, qh * TQ:(qh + 1) * TQ],
                           reads=(f"gf:{gsl}",), writes=(f"scrG:{f}:{qh}",),
                           semkey=f"gfo{gsl}", tag=f"gfo{f}_{qh}")

            em.fence({
                "tensor": (lambda e: e.matmul(PB8[0][0:1, 0:1], c_ones1[0:1, 0:1],
                                              c_ones1[0:1, 0:1], start=True, stop=True),
                           ("PB8_0",)),
                "vector": (lambda e: e.memset(tiny[:, 0:1], 0.0), ("tiny",)),
                "scalar": (lambda e: e.copy(tiny[:, 1:2], c_ones1[:, 0:1]), ("tiny",)),
                "gpsimd": (lambda e: e.memset(tinyb[:, 0:1], 0.0), ("tinyb",)),
            }, "FENCE_B2")
            stackB1.close()

            gh = sb(stackB, "gh", [128, NF * TQ], BF16)
            outq = sb(stackB, "outq", [128, NT * TQ], BF16)
            q3ws = [sb(stackB, f"q3ws{i}", [128, FF], BF16) for i in range(2)]
            x2u = [sb(stackB, f"x2u{i}", [128, C], BF16) for i in range(2)]
            yout = [sb(stackB, f"yout{i}", [128, C], F32) for i in range(2)]
            PSB2 = [stackB.enter_context(nc.psum_tensor(f"PS2_{i}", [128, BW2], F32))
                    for i in range(2 * NJ)]
            PTB = [stackB.enter_context(nc.psum_tensor(f"PTB{i}", [128, 1024], BF16))
                   for i in range(2)]

            for qh in range(4):
                for f in range(NF):
                    em.dma("qsync", gh[:, f * TQ:(f + 1) * TQ], scrG[qh, f],
                           reads=(), writes=("ghq",), semkey="ghl",
                           tag=f"ghl{qh}_{f}")
                for ot in range(NT):
                    sl = ot % 2
                    em.dma("qscalar", q3ws[sl][:], scrQ3[ot],
                           reads=tuple(f"scrQ3:{ot}:{c0}" for (c0, w) in col_chunks(FF)),
                           writes=(f"q3ws:{sl}",), semkey=f"q3w{sl}",
                           tag=f"q3l{qh}_{ot}")
                    bs = 0 if ot % 2 == 0 else NJ
                    for kt in range(NF):
                        for j in range(NJ):
                            em.t(lambda e, sl=sl, kt=kt, j=j, bs=bs: e.matmul(
                                    PSB2[bs + j][:, 0:BW2],
                                    q3ws[sl][:, kt * 128:(kt + 1) * 128],
                                    gh[:, kt * TQ + j * BW2: kt * TQ + (j + 1) * BW2],
                                    start=(kt == 0), stop=(kt == NF - 1)),
                                 reads=(f"q3ws:{sl}", "ghq"),
                                 writes=(f"PS2_{bs + j}",), tag=f"b2m{qh}_{ot}_{kt}_{j}")
                    for j in range(NJ):
                        em.v(lambda e, ot=ot, j=j, bs=bs: e.tensor_copy(
                                outq[:, ot * TQ + j * BW2: ot * TQ + (j + 1) * BW2],
                                PSB2[bs + j][:, 0:BW2]),
                             reads=(f"PS2_{bs + j}",), writes=(f"outq:{ot}",),
                             tag=f"b2e{qh}_{ot}_{j}")
                for t8 in range(NT8):
                    gt = qh * NT8 + t8
                    b = t8 % 2
                    em.dma("qsync" if b == 0 else "qscalar", x2u[b][:],
                           x2s[gt * 128:(gt + 1) * 128, :],
                           reads=(f"x2s:{gt}",), writes=(f"x2u:{b}",),
                           semkey=f"x2u{b}", tag=f"x2u{qh}_{t8}")
                    for half in range(2):
                        ptb, ptk = (PTB[half], f"PTB{half}")
                        for j in range(8):
                            ot = half * 8 + j
                            em.t(lambda e, ptb=ptb, j=j, ot=ot, t8=t8: e.transpose(
                                    ptb[:, j * 128:(j + 1) * 128],
                                    outq[:, ot * TQ + t8 * 128: ot * TQ + t8 * 128 + 128],
                                    c_i128[:]),
                                 reads=(f"outq:{ot}", "c_i128"), writes=(ptk,),
                                 tag=f"b2t{qh}_{t8}_{ot}")
                        em.v(lambda e, ptb=ptb, b=b, half=half: e.tensor_tensor(
                                yout[b][:, half * 1024:(half + 1) * 1024],
                                x2u[b][:, half * 1024:(half + 1) * 1024],
                                ptb[:, 0:1024], op=ALU.add),
                             reads=(ptk, f"x2u:{b}"), writes=(f"yout:{b}",),
                             tag=f"b2y{qh}_{t8}_{half}")
                    em.dma("qpool", y_out[gt * 128:(gt + 1) * 128, :], yout[b][:],
                           reads=(f"yout:{b}",), writes=(), semkey=f"yo{b}",
                           tag=f"yst{qh}_{t8}")

        with stackB:
            if "B" in parts:
                emit_phase_b()

            em.resolve()
            sems = {}
            for st_ in em.sem_streams():
                sems[st_] = stackG.enter_context(
                    nc.semaphore("sem_" + st_.replace(":", "_")))
            run = em.make_runner(sems)
            with nc.Block() as block:
                @block.tensor
                def _(e):
                    run(e, "tensor")

                @block.vector
                def _(e):
                    run(e, "vector")

                @block.scalar
                def _(e):
                    run(e, "scalar")

                @block.gpsimd
                def _(e):
                    run(e, "gpsimd")
                    if "B" in parts:
                        em.final_dma_waits(e, sems, [k for k in ("yo0", "yo1")
                                                     if "dma:" + k in em.dma_semkeys])
                    else:
                        em.final_dma_waits(e, sems, None)

                @block.sync
                def _(e):
                    run(e, "sync")
    return nc, em


def _consts():
    ii, jj = np.meshgrid(np.arange(64), np.arange(64), indexing="ij")
    tri_ex = np.vstack([(ii < jj)] * 2).astype(np.float32)
    tri_in = np.vstack([(ii <= jj)] * 2).astype(np.float32)
    tri_suf = np.vstack([(ii > jj)] * 2).astype(np.float32)
    i128 = np.eye(128, dtype=bfdt)
    i128f = np.eye(128, dtype=np.float32)
    triu2 = np.vstack([np.hstack([(ii < jj)] * 2)] * 2).astype(bfdt)
    tri4 = np.vstack([np.hstack([(ii < jj)] * 4)] * 2).astype(bfdt)
    onesblk = np.zeros((128, NT * 32), dtype=bfdt)
    for jt in range(NT):
        onesblk[0:64, jt * 32 + 2 * jt] = 1
        onesblk[64:128, jt * 32 + 2 * jt + 1] = 1
    blockones = np.zeros((32, C), dtype=bfdt)
    for h in range(32):
        blockones[h, h * 64:(h + 1) * 64] = 1
    return dict(tri_ex=tri_ex, tri_in=tri_in, tri_suf=tri_suf, i128=i128,
                i128f=i128f, triu2=triu2, tri4=tri4, onesblk=onesblk, blockones=blockones)


def make_inmap(xb, P, T):
    cs = _consts()
    u_flat = np.ascontiguousarray(P["time_faaaa"], dtype=np.float32).reshape(C)
    u_bc = np.repeat(u_flat.reshape(NT, 128).T[:, :, None], 64, axis=2).reshape(128, NT * 64)
    s0 = np.zeros((128, NT * 64), dtype=np.float32)
    ts = np.asarray(P["time_state"], dtype=np.float32)
    for jt in range(NT):
        s0[0:64, jt * 64:(jt + 1) * 64] = ts[2 * jt]
        s0[64:128, jt * 64:(jt + 1) * 64] = ts[2 * jt + 1]
    d = {
        "x": np.ascontiguousarray(xb, dtype=np.float32),
        "maa_w1r": np.ascontiguousarray(
            np.asarray(P["time_maa_w1"], dtype=np.float32).reshape(NT, 128, 160)
            .transpose(1, 0, 2).reshape(128, NT * 160)),
        "dw1r": np.ascontiguousarray(
            np.asarray(P["time_decay_w1"], dtype=np.float32).reshape(NT, 128, 64)
            .transpose(1, 0, 2).reshape(128, NT * 64)),
        "dw2": np.asarray(P["time_decay_w2"], dtype=np.float32),
        "tdec": np.asarray(P["time_decay"], dtype=np.float32),
        "u_bc": np.ascontiguousarray(u_bc, dtype=np.float32),
        "s0": s0,
        "lnw_chan": np.ascontiguousarray(
            np.asarray(P["ln_w"], dtype=np.float32).reshape(NT, 128).T),
        "maax_chan": np.ascontiguousarray(
            np.asarray(P["time_maa_x"], dtype=np.float32).reshape(NT, 128).T),
        "lnb_chan": np.ascontiguousarray(
            np.asarray(P["ln_b"], dtype=np.float32).reshape(NT, 128).T),
        "wf1": np.asarray(P["ffn_w1"], dtype=np.float32),
        "wf2": np.asarray(P["ffn_w2"], dtype=np.float32),
        "wf3": np.asarray(P["ffn_w3"], dtype=np.float32),
    }
    for m in "rkvgo":
        d["W" + m] = np.asarray(P["W_" + m], dtype=np.float32)
    w2 = np.asarray(P["time_maa_w2"], dtype=np.float32)
    for f in range(5):
        d[f"w2f{f}"] = np.ascontiguousarray(w2[f])
    for f, nm in zip("xwkvrg", ["time_maa_x", "time_maa_w", "time_maa_k",
                                "time_maa_v", "time_maa_r", "time_maa_g"]):
        d["maa_" + f] = np.asarray(P[nm], dtype=np.float32)
    d.update(cs)
    return d


_NC_CACHE = {}


def kernel(L0_attn_x, L0_attn_time_maa_x, L0_attn_time_maa_w, L0_attn_time_maa_k,
           L0_attn_time_maa_v, L0_attn_time_maa_r, L0_attn_time_maa_g,
           L0_attn_time_maa_w1, L0_attn_time_maa_w2, L0_attn_time_decay,
           L0_attn_time_decay_w1, L0_attn_time_decay_w2, L0_attn_time_faaaa,
           L0_attn_time_state, L0_attn_W_r, L0_attn_W_k, L0_attn_W_v,
           L0_attn_W_g, L0_attn_W_o, L0_attn_ln_w, L0_attn_ln_b,
           L0_ffn_w1, L0_ffn_w2, L0_ffn_w3):
    P = dict(
        time_maa_x=L0_attn_time_maa_x, time_maa_w=L0_attn_time_maa_w,
        time_maa_k=L0_attn_time_maa_k, time_maa_v=L0_attn_time_maa_v,
        time_maa_r=L0_attn_time_maa_r, time_maa_g=L0_attn_time_maa_g,
        time_maa_w1=L0_attn_time_maa_w1, time_maa_w2=L0_attn_time_maa_w2,
        time_decay=L0_attn_time_decay, time_decay_w1=L0_attn_time_decay_w1,
        time_decay_w2=L0_attn_time_decay_w2, time_faaaa=L0_attn_time_faaaa,
        time_state=L0_attn_time_state, W_r=L0_attn_W_r, W_k=L0_attn_W_k,
        W_v=L0_attn_W_v, W_g=L0_attn_W_g, W_o=L0_attn_W_o,
        ln_w=L0_attn_ln_w, ln_b=L0_attn_ln_b,
        ffn_w1=L0_ffn_w1, ffn_w2=L0_ffn_w2, ffn_w3=L0_ffn_w3)
    x = np.asarray(L0_attn_x, dtype=np.float32)
    T = x.shape[1]
    if T not in _NC_CACHE:
        _NC_CACHE[T] = build_program(T)
    nc, em = _NC_CACHE[T]
    in_maps = [make_inmap(x[i], P, T) for i in range(x.shape[0])]
    res = run_bass_kernel_spmd(nc, in_maps, core_ids=list(range(x.shape[0])))
    out = np.stack([np.asarray(r["y"], dtype=np.float32) for r in res.results], axis=0)
    return out
